# version 07 (final)
# speedup vs baseline: 1.6245x; 1.0517x over previous
import contextlib
import numpy as np
import ml_dtypes
import concourse.bass as bass
import concourse.mybir as mybir
from concourse.bass_utils import run_bass_kernel_spmd

F32 = mybir.dt.float32
BF16 = mybir.dt.bfloat16
I32 = mybir.dt.int32
AF = mybir.ActivationFunctionType
ALU = mybir.AluOpType
AX = mybir.AxisListType

BF = ml_dtypes.bfloat16

B, S, D = 4, 4096, 1024
NCORES = 8
TC = 2048
NT = TC // 128
ND = D // 128


ENGS = ("pe", "act", "dve", "pool", "sp")

PARANOID = False

class Prog:
    def __init__(self, nc):
        self.nc = nc
        self.stream = {e: [] for e in ENGS}
        self.semval = {e: 0 for e in ENGS}
        self.pending = {e: [] for e in ENGS}
        self.known = {e: {} for e in ENGS}
        self.buf = {}
        self.dirty = {e: set() for e in ENGS}
        self.dstream = {}
        self._sems = None
        self._dsems = None
        self._names = {"pe": "tensor", "act": "scalar", "dve": "vector",
                       "pool": "gpsimd", "sp": "sync"}

    def _bufrec(self, name):
        if name not in self.buf:
            self.buf[name] = {"writer": None, "readers": []}
        return self.buf[name]

    def _force_inc(self, e, always=False):
        if not self.pending[e]:
            if not always:
                return
            self.semval[e] += 1
            val = self.semval[e]
            def emit0(engobj, _val=val):
                engobj.maybe_drain_then_inc((self._sems[e], 1))
            self.stream[e].append(("inc", emit0))
            return
        self.semval[e] += 1
        val = self.semval[e]
        for rec, kind in self.pending[e]:
            if kind == "w":
                w = rec["writer"]
                if w is not None and w[0] == e and w[1] is None:
                    rec["writer"] = (e, val, False)
            else:
                rec["readers"] = [((re, val, rd) if (re == e and rv is None) else (re, rv, rd))
                                  for (re, rv, rd) in rec["readers"]]
        self.pending[e] = []
        self.dirty[e] = set()
        def emit(engobj, _val=val):
            engobj.maybe_drain_then_inc((self._sems[e], 1))
        self.stream[e].append(("inc", emit))

    def op(self, eng, fn, reads=(), writes=(), dma=False, stream=None, tag=None):
        waits = {}
        need_drain = False

        def handle(prod):
            nonlocal need_drain
            pk, v, isdma = prod
            if isdma:
                if self.known[eng].get(("d", pk), 0) < v:
                    waits[("d", pk)] = max(waits.get(("d", pk), 0), v)
                return
            if pk == eng:
                if eng != "pe":
                    need_drain = True
                return
            if v is None:
                self._force_inc(pk)
                v = self.semval[pk]
            if self.known[eng].get(pk, 0) < v:
                waits[pk] = max(waits.get(pk, 0), v)

        for r in reads:
            rec = self._bufrec(r)
            if rec["writer"] is not None:
                handle(rec["writer"])
        for w in writes:
            rec = self._bufrec(w)
            if rec["writer"] is not None:
                handle(rec["writer"])
            for rd in list(rec["readers"]):
                handle(rd)
        if need_drain:
            deps = set(reads) | set(writes)
            if not (deps & self.dirty[eng]):
                need_drain = False
        for pk, v in waits.items():
            self.known[eng][pk] = v
            def emitw(engobj, _pk=pk, _v=v):
                if isinstance(_pk, tuple):
                    engobj.wait_ge(self._dsems[_pk[1]], _v)
                else:
                    engobj.wait_ge(self._sems[_pk], _v)
            self.stream[eng].append(("wait", emitw))
        if need_drain:
            self.dirty[eng] = set()
            def emitdr(engobj):
                engobj.drain()
            self.stream[eng].append(("drain", emitdr))
        if dma:
            key = stream or (writes[0] if writes else "o_" + (reads[0] if reads else "x"))
            ds = self.dstream.setdefault(key, {"val": 0, "eng": eng})
            ds["val"] += 16
            ds["eng"] = eng
            val = ds["val"]
            def emitd(engobj, _fn=fn, _key=key):
                inst = _fn(engobj)
                inst.then_inc(self._dsems[_key], 16)
            self.stream[eng].append(("dma", emitd))
            for w in writes:
                rec = self._bufrec(w)
                rec["writer"] = (key, val, True)
                rec["readers"] = []
            for r in reads:
                self._bufrec(r)["readers"].append((key, val, True))
        else:
            self.stream[eng].append(("op", fn))
            for w in writes:
                rec = self._bufrec(w)
                rec["writer"] = (eng, None, False)
                rec["readers"] = []
                self.pending[eng].append((rec, "w"))
                self.dirty[eng].add(w)
            for r in reads:
                rec = self._bufrec(r)
                rec["readers"].append((eng, None, False))
                self.pending[eng].append((rec, "r"))
                self.dirty[eng].add(r)
            if PARANOID:
                self._force_inc(eng)

    def finish(self):
        for key, ds in self.dstream.items():
            def emitw(engobj, _k=key, _v=ds["val"]):
                engobj.wait_ge(self._dsems[_k], _v)
            self.stream[ds["eng"]].append(("wait", emitw))

    def barrier_read(self, eng, names):
        def nop(engobj):
            pass
        self.op(eng, nop, reads=tuple(names))

    def barrier(self):
        for key, ds in self.dstream.items():
            eng, v = ds["eng"], ds["val"]
            if self.known[eng].get(("d", key), 0) < v:
                self.known[eng][("d", key)] = v
                def emitw(engobj, _k=key, _v=v):
                    engobj.wait_ge(self._dsems[_k], _v)
                self.stream[eng].append(("wait", emitw))
        for e in ENGS:
            self._force_inc(e, always=True)
        vals = dict(self.semval)
        for e in ENGS:
            for o in ENGS:
                if o == e:
                    continue
                if self.known[e].get(o, 0) < vals[o]:
                    self.known[e][o] = vals[o]
                    def emitw(engobj, _o=o, _v=vals[o]):
                        engobj.wait_ge(self._sems[_o], _v)
                    self.stream[e].append(("wait", emitw))
            self.dirty[e] = set()

    def finalize(self, stack):
        self._sems = {e: stack.enter_context(self.nc.semaphore(f"sem_{e}")) for e in ENGS}
        self._dsems = {k: stack.enter_context(self.nc.semaphore(f"dsem_{i}"))
                       for i, k in enumerate(self.dstream)}
        block = stack.enter_context(self.nc.Block())
        streams = self.stream
        names = self._names

        def runner(e):
            def run(engobj):
                for kind, f in streams[e]:
                    f(engobj)
            return run

        for e in ENGS:
            deco = getattr(block, names[e])
            deco(runner(e))



class Ctx:
    def __init__(self, name="k"):
        self.nc = bass.Bass("TRN2", target_bir_lowering=False)
        self.p = Prog(self.nc)
        self.stack = contextlib.ExitStack()
        self._uid = 0
        self.sb = {}
        self.ps = {}
        self.dram = {}

    def uid(self):
        self._uid += 1
        return self._uid

    def sbuf(self, name, shape, dtype, **kw):
        t = self.stack.enter_context(self.nc.sbuf_tensor(name, list(shape), dtype, **kw))
        self.sb[name] = t
        return t

    def sbuf_s(self, stack, name, shape, dtype, **kw):
        t = stack.enter_context(self.nc.sbuf_tensor(name, list(shape), dtype, **kw))
        self.sb[name] = t
        return t

    def psum_s(self, stack, name, shape, dtype=F32):
        shape = list(shape)
        full = 512 if dtype == F32 else 1024
        assert shape[0] == 128 and shape[1] <= full
        shape[1] = full
        t = stack.enter_context(self.nc.psum_tensor(name, shape, dtype))
        self.ps[name] = t
        return t

    def psum(self, name, shape, dtype=F32):
        shape = list(shape)
        full = 512 if dtype == F32 else 1024
        assert shape[0] == 128 and shape[1] <= full
        shape[1] = full
        t = self.stack.enter_context(self.nc.psum_tensor(name, shape, dtype))
        self.ps[name] = t
        return t

    def din(self, name, shape, dtype=F32):
        t = self.nc.dram_tensor(name, list(shape), dtype, kind="ExternalInput")
        self.dram[name] = t
        return t.ap()

    def dout(self, name, shape, dtype=F32):
        t = self.nc.dram_tensor(name, list(shape), dtype, kind="ExternalOutput")
        self.dram[name] = t
        return t.ap()

    def finalize(self):
        self.p.finalize(self.stack)
        return self.nc


def bf(x):
    return np.ascontiguousarray(x).astype(BF)

def f32(x):
    return np.ascontiguousarray(x, dtype=np.float32)


def const_inputs():
    return {
        "c_ident": np.eye(128, dtype=BF),
        "c_ones_r": np.ones((1, 128), np.float32),
    }

def load_consts(ctx):
    nc, p = ctx.nc, ctx.p
    ident_d = ctx.din("c_ident", [128, 128], BF16)
    ones_d = ctx.din("c_ones_r", [1, 128], F32)
    ident = ctx.sbuf("ident", [128, 128], BF16)
    onesr = ctx.sbuf("onesr", [1, 128], F32)
    p.op("sp", lambda e: e.dma_start(out=ident[:], in_=ident_d), writes=("ident",), dma=True)
    p.op("sp", lambda e: e.dma_start(out=onesr[:], in_=ones_d), writes=("onesr",), dma=True)
    return {"ident": ident, "onesr": onesr}


def emit_transpose_128(ctx, consts, src_ap, dst_ap, pbank, pbname, src_buf, dst_buf,
                       evac_eng="dve"):
    p = ctx.p
    ident = consts["ident"]
    def t_fn(e):
        e.transpose(pbank, src_ap, ident[:])
    p.op("pe", t_fn, reads=(src_buf, "ident"), writes=(pbname,))
    def ev_fn(e):
        e.tensor_copy(dst_ap, pbank)
    p.op(evac_eng, ev_fn, reads=(pbname,), writes=(dst_buf,))



def emit_ln_tile(ctx, consts, p, xnew_t, b_xn, stat, epsT, xnb_t, b_xnb, scrT, b_scrT,
                 pbf, xnT_d, ti, sumsq_scr, xlo_t=None, scrT2=None, xnTlo_d=None,
                 xnf_t=None, b_xnf=None, xnf_d=None):
    D_ = D
    p.op("act", lambda e: e.activation(sumsq_scr[:], xnew_t[:], AF.Square),
         reads=(b_xn,), writes=("lnscr",))
    p.op("dve", lambda e: e.reduce_sum(stat[:, 1:2], sumsq_scr[:], axis=AX.X),
         reads=("lnscr",), writes=("stat",))
    def stats_fn(e):
        e.tensor_scalar(stat[:, 2:3], stat[:, 0:1], 1.0 / D_, None, ALU.mult)
        e.tensor_scalar(stat[:, 3:4], stat[:, 1:2], 1.0 / D_, None, ALU.mult)
        e.drain()
        e.tensor_scalar(stat[:, 4:5], stat[:, 2:3], stat[:, 2:3], None, ALU.mult)
        e.drain()
        e.tensor_tensor(stat[:, 5:6], stat[:, 3:4], stat[:, 4:5], ALU.subtract)
    p.op("dve", stats_fn, reads=("stat",), writes=("stat",))
    p.op("act", lambda e: e.activation(stat[:, 6:7], stat[:, 5:6], AF.Sqrt, bias=epsT[:, 0:1]),
         reads=("stat", "epsT"), writes=("stat",))
    def newton_fn(e):
        e.reciprocal(stat[:, 7:8], stat[:, 6:7])
        e.tensor_scalar(stat[:, 8:9], stat[:, 5:6], 1e-5, None, ALU.add)
        e.drain()
        e.tensor_scalar(stat[:, 9:10], stat[:, 7:8], stat[:, 7:8], None, ALU.mult)
        e.drain()
        e.tensor_tensor(stat[:, 10:11], stat[:, 8:9], stat[:, 9:10], ALU.mult)
        e.drain()
        e.tensor_scalar(stat[:, 10:11], stat[:, 10:11], -0.5, 1.5, ALU.mult, ALU.add)
        e.drain()
        e.tensor_tensor(stat[:, 7:8], stat[:, 7:8], stat[:, 10:11], ALU.mult)
    p.op("dve", newton_fn, reads=("stat",), writes=("stat",))
    p.op("dve", lambda e: e.tensor_scalar(
            xnb_t[:], xnew_t[:], stat[:, 2:3], stat[:, 7:8], ALU.subtract, ALU.mult),
         reads=(b_xn, "stat"), writes=(b_xnb,))
    p.op("dve", lambda e: e.tensor_scalar(
            sumsq_scr[:], xnew_t[:], stat[:, 2:3], stat[:, 7:8], ALU.subtract, ALU.mult),
         reads=(b_xn, "stat"), writes=("lnscr",))
    p.op("dve", lambda e: e.tensor_tensor(xlo_t[:], sumsq_scr[:], xnb_t[:], ALU.subtract),
         reads=("lnscr", b_xnb), writes=("xlo",))
    if xnf_t is not None:
        p.op("dve", lambda e: e.tensor_scalar(
                xnf_t[:], xnew_t[:], stat[:, 2:3], stat[:, 7:8], ALU.subtract, ALU.mult),
             reads=(b_xn, "stat"), writes=(b_xnf,))
        p.op("sp", lambda e: e.dma_start(out=xnf_d[ti*128:(ti+1)*128, :], in_=xnf_t[:]),
             reads=(b_xnf,), dma=True)
    for fi in range(ND):
        bank = fi % 2
        p.op("pe", lambda e, fi=fi, bank=bank: e.transpose(
                pbf[bank][:, fi*128:(fi+1)*128], xnb_t[:, fi*128:(fi+1)*128], consts["ident"][:]),
             reads=(b_xnb, "ident"), writes=(f"pbf{bank}",))
        if bank == 0:
            p.op("dve", lambda e, fi=fi, bank=bank: e.tensor_copy(
                    scrT[:, fi*128:(fi+1)*128], pbf[bank][:, fi*128:(fi+1)*128]),
                 reads=(f"pbf{bank}",), writes=(b_scrT,))
        else:
            p.op("act", lambda e, fi=fi, bank=bank: e.copy(
                    scrT[:, fi*128:(fi+1)*128], pbf[bank][:, fi*128:(fi+1)*128]),
                 reads=(f"pbf{bank}",), writes=(b_scrT,))
    for fi in range(ND):
        bank = fi % 2
        p.op("pe", lambda e, fi=fi, bank=bank: e.transpose(
                pbf[bank][:, fi*128:(fi+1)*128], xlo_t[:, fi*128:(fi+1)*128], consts["ident"][:]),
             reads=("xlo", "ident"), writes=(f"pbf{bank}",))
        if bank == 0:
            p.op("dve", lambda e, fi=fi, bank=bank: e.tensor_copy(
                    scrT2[:, fi*128:(fi+1)*128], pbf[bank][:, fi*128:(fi+1)*128]),
                 reads=(f"pbf{bank}",), writes=(b_scrT + "L",))
        else:
            p.op("act", lambda e, fi=fi, bank=bank: e.copy(
                    scrT2[:, fi*128:(fi+1)*128], pbf[bank][:, fi*128:(fi+1)*128]),
                 reads=(f"pbf{bank}",), writes=(b_scrT + "L",))
    def out_fn(e):
        dst = bass.AP(xnT_d.tensor, ti * 128,
                      [[TC, 128], [128 * TC, ND], [1, 128]])
        srcv = scrT[:, :].rearrange("p (f c) -> p f c", f=ND)
        return e.dma_start(out=dst, in_=srcv)
    p.op("pool", out_fn, reads=(b_scrT,), dma=True)
    def out_fn2(e):
        dst = bass.AP(xnTlo_d.tensor, ti * 128,
                      [[TC, 128], [128 * TC, ND], [1, 128]])
        srcv = scrT2[:, :].rearrange("p (f c) -> p f c", f=ND)
        return e.dma_start(out=dst, in_=srcv)
    p.op("pool", out_fn2, reads=(b_scrT + "L",), dma=True)


def build_ln_update(with_deltas, out_xn_f32):
    ctx = Ctx()
    p = ctx.p
    x_d = ctx.din("x", [TC, D], F32)
    if with_deltas:
        d1_d = ctx.din("d1", [TC, D], F32)
        d2_d = ctx.din("d2", [TC, D], F32)
        if with_deltas == 3:
            d3_d = ctx.din("d3", [TC, D], F32)
    xnew_out_d = ctx.dout("x_new", [TC, D], F32)
    xnT_d = ctx.dout("xnT", [D, TC], BF16)
    xnTlo_d = ctx.dout("xnT_lo", [D, TC], BF16)
    xnf_d = ctx.dout("xn_f32", [TC, D], F32) if out_xn_f32 else None
    xnbf_d = ctx.dout("xn_bf", [TC, D], BF16) if out_xn_f32 else None
    consts = load_consts(ctx)

    xs = [ctx.sbuf(f"xs{i}", [128, D], F32) for i in range(2)]
    if with_deltas:
        d1s = [ctx.sbuf(f"d1s{i}", [128, D], F32) for i in range(2)]
        d2s = [ctx.sbuf(f"d2s{i}", [128, D], F32) for i in range(2)]
        if with_deltas == 3:
            d3s = [ctx.sbuf(f"d3s{i}", [128, D], F32) for i in range(2)]
    xnew = [ctx.sbuf(f"xnew{i}", [128, D], F32) for i in range(2)]
    xnb = [ctx.sbuf(f"xnb{i}", [128, D], BF16) for i in range(2)]
    xnf = [ctx.sbuf(f"xnf{i}", [128, D], F32) for i in range(2)] if out_xn_f32 else None
    scrT = [ctx.sbuf(f"scrTb{i}", [128, D], BF16) for i in range(2)]
    scrT2 = [ctx.sbuf(f"scrTc{i}", [128, D], BF16) for i in range(2)]
    xlo = ctx.sbuf("xlo", [128, D], BF16)
    lnscr = ctx.sbuf("lnscr", [128, D], F32)
    tmp = ctx.sbuf("tmp", [128, D], F32)
    stat = ctx.sbuf("stat", [128, 12], F32)
    epsT = ctx.sbuf("epsT", [128, 1], F32)
    pbf = [ctx.psum(f"pbf{i}", [128, 1024], BF16) for i in range(2)]
    p.op("dve", lambda e: e.memset(epsT[:], 1e-5), writes=("epsT",))

    for ti in range(NT):
        i = ti % 2
        b_x, b_xn, b_xnb = f"xs{i}", f"xnew{i}", f"xnb{i}"
        p.op("sp", lambda e, ti=ti, i=i: e.dma_start(out=xs[i][:], in_=x_d[ti*128:(ti+1)*128, :]),
             writes=(b_x,), dma=True)
        if with_deltas:
            p.op("sp", lambda e, ti=ti, i=i: e.dma_start(out=d1s[i][:], in_=d1_d[ti*128:(ti+1)*128, :]),
                 writes=(f"d1s{i}",), dma=True)
            p.op("sp", lambda e, ti=ti, i=i: e.dma_start(out=d2s[i][:], in_=d2_d[ti*128:(ti+1)*128, :]),
                 writes=(f"d2s{i}",), dma=True)
            p.op("dve", lambda e, i=i: e.tensor_tensor(tmp[:], xs[i][:], d1s[i][:], ALU.add),
                 reads=(b_x, f"d1s{i}"), writes=("tmp",))
            if with_deltas == 3:
                p.op("sp", lambda e, ti=ti, i=i: e.dma_start(out=d3s[i][:], in_=d3_d[ti*128:(ti+1)*128, :]),
                     writes=(f"d3s{i}",), dma=True)
                p.op("dve", lambda e, i=i: e.tensor_tensor(tmp[:], tmp[:], d3s[i][:], ALU.add),
                     reads=("tmp", f"d3s{i}"), writes=("tmp",))
            p.op("dve", lambda e, i=i: e.tensor_tensor(xnew[i][:], tmp[:], d2s[i][:], ALU.add),
                 reads=("tmp", f"d2s{i}"), writes=(b_xn,))
            p.op("dve", lambda e, i=i: e.reduce_sum(stat[:, 0:1], xnew[i][:], axis=AX.X),
                 reads=(b_xn,), writes=("stat",))
        else:
            p.op("dve", lambda e, i=i: e.tensor_copy(xnew[i][:], xs[i][:]),
                 reads=(b_x,), writes=(b_xn,))
            p.op("dve", lambda e, i=i: e.reduce_sum(stat[:, 0:1], xnew[i][:], axis=AX.X),
                 reads=(b_xn,), writes=("stat",))
        p.op("sp", lambda e, ti=ti, i=i: e.dma_start(
                out=xnew_out_d[ti*128:(ti+1)*128, :], in_=xnew[i][:]),
             reads=(b_xn,), dma=True)
        emit_ln_tile(ctx, consts, p, xnew[i], b_xn, stat, epsT, xnb[i], b_xnb,
                     scrT[i], f"scrTb{i}", pbf, xnT_d, ti, lnscr,
                     xlo_t=xlo, scrT2=scrT2[i], xnTlo_d=xnTlo_d,
                     xnf_t=(xnf[i] if out_xn_f32 else None),
                     b_xnf=(f"xnf{i}" if out_xn_f32 else None),
                     xnf_d=xnf_d)
        if out_xn_f32:
            p.op("pool", lambda e, ti=ti, i=i: e.dma_start(
                    out=xnbf_d[ti*128:(ti+1)*128, :], in_=xnb[i][:]),
                 reads=(b_xnb,), dma=True)
    p.finish()
    return ctx



def run_launch(ctx, per_core_inputs):
    nc = ctx.finalize()
    res = run_bass_kernel_spmd(nc, per_core_inputs, core_ids=list(range(NCORES)))
    return res.results


def load_xnT_tile(ctx, p, xnT_d, dst, dstbuf, t0, tw, eng="sp"):
    def fn(e):
        src = bass.AP(xnT_d.tensor, t0, [[TC, 128], [128 * TC, ND], [1, tw]])
        dstv = dst[:, 0:ND * tw].rearrange("p (f c) -> p f c", f=ND)
        return e.dma_start(out=dstv, in_=src)
    p.op(eng, fn, writes=(dstbuf,), dma=True)


def build_l1_ffn():
    H3 = 3072
    NH = H3 // 128
    s = 1.0 / 127.0
    ctx = Ctx()
    p = ctx.p
    xnT_d = ctx.din("xnT", [D, TC], BF16)
    wg_d = ctx.din("wgT", [D, H3], I32)
    wu_d = ctx.din("wuT", [D, H3], I32)
    wd_d = ctx.din("wdT", [H3, D], I32)
    out_d = ctx.dout("out", [TC, D], F32)

    wg = [ctx.sbuf(f"wg{k}", [128, H3], BF16) for k in range(ND)]
    wu = [ctx.sbuf(f"wu{k}", [128, H3], BF16) for k in range(ND)]
    wd = [ctx.sbuf(f"wd{m}", [128, D], BF16) for m in range(NH)]
    TW = 256
    xt = [ctx.sbuf(f"xt{i}", [128, ND * TW], BF16) for i in range(2)]
    sg = [ctx.sbuf(f"sg{ip}", [128, TW], BF16) for ip in range(2)]
    hm = [ctx.sbuf(f"hm{ip}", [128, TW], BF16) for ip in range(2)]
    outt = ctx.sbuf("outt", [128, D], F32)
    pg = ctx.psum("pg", [128, TW], F32)
    pu = ctx.psum("pu", [128, TW], F32)
    po = [ctx.psum(f"po{j}", [128, 512], F32) for j in range(4)]

    for k in range(ND):
        p.op("pool", lambda e, k=k: e.dma_start(out=wg[k][:], in_=wg_d[k*128:(k+1)*128, :]),
             writes=(f"wg{k}",), dma=True)
        p.op("pool", lambda e, k=k: e.dma_start(out=wu[k][:], in_=wu_d[k*128:(k+1)*128, :]),
             writes=(f"wu{k}",), dma=True)
    for m in range(NH):
        p.op("pool", lambda e, m=m: e.dma_start(out=wd[m][:], in_=wd_d[m*128:(m+1)*128, :]),
             writes=(f"wd{m}",), dma=True)

    ntile = TC // TW
    for t in range(ntile):
        i = t % 2
        load_xnT_tile(ctx, p, xnT_d, xt[i], f"xt{i}", t * TW, TW)
        def emitGU(m, i=i):
            ip = m % 2
            def mm_g(e, m=m, i=i):
                for k in range(ND):
                    e.matmul(pg[:, 0:TW], wg[k][:, m*128:(m+1)*128],
                             xt[i][:, k*TW:(k+1)*TW], start=(k == 0), stop=(k == ND-1))
            p.op("pe", mm_g, reads=(f"xt{i}",) + tuple(f"wg{k}" for k in range(ND)),
                 writes=("pg",))
            def mm_u(e, m=m, i=i):
                for k in range(ND):
                    e.matmul(pu[:, 0:TW], wu[k][:, m*128:(m+1)*128],
                             xt[i][:, k*TW:(k+1)*TW], start=(k == 0), stop=(k == ND-1))
            p.op("pe", mm_u, reads=(f"xt{i}",) + tuple(f"wu{k}" for k in range(ND)),
                 writes=("pu",))
            p.op("act", lambda e, ip=ip: e.activation(sg[ip][:], pg[:, 0:TW], AF.Silu, scale=s),
                 reads=("pg",), writes=(f"sg{ip}",))
            p.op("dve", lambda e, ip=ip: e.tensor_tensor(hm[ip][:], sg[ip][:], pu[:, 0:TW], ALU.mult),
                 reads=(f"sg{ip}", "pu"), writes=(f"hm{ip}",))
        def emitD(m):
            ip = m % 2
            def mm_d(e, m=m, ip=ip):
                for tc in range(TW // 128):
                    for j in range(2):
                        e.matmul(po[tc*2+j][:], hm[ip][:, tc*128:(tc+1)*128],
                                 wd[m][:, j*512:(j+1)*512],
                                 start=(m == 0), stop=(m == NH-1))
            p.op("pe", mm_d, reads=(f"hm{ip}", f"wd{m}"),
                 writes=tuple(f"po{j}" for j in range(4)))
        for m in range(NH):
            emitGU(m)
            if m > 0:
                emitD(m - 1)
        emitD(NH - 1)
        for tc in range(TW // 128):
            def ev(e, tc=tc):
                e.tensor_scalar(outt[:, 0:512], po[tc*2][:], s*s, None, ALU.mult)
                e.tensor_scalar(outt[:, 512:1024], po[tc*2+1][:], s*s, None, ALU.mult)
            p.op("dve", ev, reads=(f"po{tc*2}", f"po{tc*2+1}"), writes=("outt",))
            p.op("sp", lambda e, t=t, tc=tc: e.dma_start(
                    out=out_d[t*TW + tc*128: t*TW + (tc+1)*128, :], in_=outt[:]),
                 reads=("outt",), dma=True)
    p.finish()
    return ctx


def build_l2_ffn():
    E = 8
    ctx = Ctx()
    p = ctx.p
    xnT_d = ctx.din("xnT", [D, TC], BF16)
    xnTlo_d = ctx.din("xnT_lo", [D, TC], BF16)
    rw_d = ctx.din("rwT", [D, E], F32)
    rb_d = ctx.din("rb", [1, E], F32)
    ew_d = ctx.din("ewT", [E, D, D], F32)
    eb_d = ctx.din("eb", [E, D], F32)
    out_d = ctx.dout("out", [TC, D], F32)
    consts = load_consts(ctx)

    rw = ctx.sbuf("rw", [128, ND * E], BF16)
    rwlo = ctx.sbuf("rwlo", [128, ND * E], BF16)
    rwf = ctx.sbuf("rwf", [128, ND * E], F32)
    xtlo = ctx.sbuf("xtlo", [128, ND * 128], BF16)
    rbr = ctx.sbuf("rbr", [1, E], F32)
    rbb = ctx.sbuf("rbb", [128, E], F32)
    ew = [[ctx.sbuf(f"ew{e}_{k}", [128, D], BF16) for k in range(ND)] for e in range(E)]
    eb = ctx.sbuf("eb_s", [E, D], BF16)
    sel = ctx.sbuf("sel_s", [E, E * 128], BF16)
    sel_d = ctx.din("c_sel8", [E, E * 128], BF16)
    xt = [ctx.sbuf(f"xt{i}", [128, ND * 256], BF16) for i in range(2)]
    xp = ctx.sbuf("xp", [128, ND * 256], BF16)
    lg = ctx.sbuf("lg", [128, E], F32)
    st8 = ctx.sbuf("st8", [128, 8], F32)
    mwb = ctx.sbuf("mwb", [128, E], BF16)
    mwT = ctx.sbuf("mwT", [E, TC], BF16)
    bsb = ctx.sbuf("bsb", [128, 256], BF16)
    outt = ctx.sbuf("outt", [128, D], F32)
    plog = ctx.psum("plog", [128, 512], F32)
    pB = ctx.psum("pB", [128, 512], F32)
    pmw = ctx.psum("pmw", [128, 1024], BF16)
    po = [ctx.psum(f"po{j}", [128, 512], F32) for j in range(4)]

    def ld(e):
        return e.dma_start(out=rwf[:].rearrange("p (f c) -> p f c", f=ND),
                           in_=rw_d.rearrange("(f p) c -> p f c", p=128))
    p.op("pool", ld, writes=("rwf",), dma=True)
    p.op("dve", lambda e: e.tensor_copy(rw[:], rwf[:]), reads=("rwf",), writes=("rw",))
    p.op("dve", lambda e: e.tensor_tensor(rwlo[:], rwf[:], rw[:], ALU.subtract),
         reads=("rwf", "rw"), writes=("rwlo",))
    p.op("pool", lambda e: e.dma_start(out=rbr[:], in_=rb_d), writes=("rbr",), dma=True)
    p.op("pool", lambda e: e.dma_start(out=eb[:], in_=eb_d), writes=("eb_s",), dma=True)
    p.op("pool", lambda e: e.dma_start(out=sel[:], in_=sel_d), writes=("sel_s",), dma=True)
    for e_ in range(E):
        for k in range(ND):
            p.op("pool", lambda e, e_=e_, k=k: e.dma_start(
                    out=ew[e_][k][:], in_=ew_d[e_, k*128:(k+1)*128, :]),
                 writes=(f"ew{e_}_{k}",), dma=True)
    p.op("pe", lambda e: e.matmul(plog[:, 0:E], consts["onesr"][:], rbr[:], start=True, stop=True),
         reads=("rbr", "onesr"), writes=("plog",))
    p.op("dve", lambda e: e.tensor_copy(rbb[:], plog[:, 0:E]), reads=("plog",), writes=("rbb",))

    for ti in range(NT):
        i = ti % 2
        load_xnT_tile(ctx, p, xnT_d, xt[i], f"xt{i}", ti * 128, 128)
        load_xnT_tile(ctx, p, xnTlo_d, xtlo, "xtlo", ti * 128, 128, eng="pool")
        def mm_lg(e, i=i):
            for k in range(ND):
                e.matmul(plog[:, 0:E], xt[i][:, k*128:(k+1)*128], rw[:, k*E:(k+1)*E],
                         start=(k == 0), stop=False)
            for k in range(ND):
                e.matmul(plog[:, 0:E], xtlo[:, k*128:(k+1)*128], rw[:, k*E:(k+1)*E],
                         start=False, stop=False)
            for k in range(ND):
                e.matmul(plog[:, 0:E], xt[i][:, k*128:(k+1)*128], rwlo[:, k*E:(k+1)*E],
                         start=False, stop=(k == ND-1))
        p.op("pe", mm_lg, reads=(f"xt{i}", "xtlo", "rw", "rwlo"), writes=("plog",))
        p.op("dve", lambda e: e.tensor_tensor(lg[:], plog[:, 0:E], rbb[:], ALU.add),
             reads=("plog", "rbb"), writes=("lg",))
        p.op("dve", lambda e: e.reduce_max(st8[:, 0:1], lg[:], axis=AX.X),
             reads=("lg",), writes=("st8",))
        p.op("dve", lambda e: e.tensor_scalar(st8[:, 1:2], st8[:, 0:1], -1.0, None, ALU.mult),
             reads=("st8",), writes=("st8",))
        p.op("act", lambda e: e.activation(lg[:], lg[:], AF.Exp, bias=st8[:, 1:2],
                                           accum_out=st8[:, 2:3]),
             reads=("lg", "st8"), writes=("lg", "st8"))
        p.op("dve", lambda e: e.reciprocal(st8[:, 3:4], st8[:, 2:3]),
             reads=("st8",), writes=("st8",))
        def mw_fn(e):
            e.tensor_scalar(lg[:], lg[:], 0.9999, None, ALU.is_ge)
            e.drain()
            e.tensor_scalar(mwb[:], lg[:], st8[:, 3:4], None, ALU.mult)
        p.op("dve", mw_fn, reads=("lg", "st8"), writes=("mwb", "lg"))
        p.op("pe", lambda e: e.transpose(pmw[0:8, 0:128], mwb[:], consts["ident"][:]),
             reads=("mwb", "ident"), writes=("pmw",))
        p.op("dve", lambda e, ti=ti: e.tensor_copy(mwT[0:E, ti*128:(ti+1)*128], pmw[0:E, 0:128]),
             reads=("pmw",), writes=("mwT",))

    TW = 256
    for t in range(TC // TW):
        i = t % 2
        load_xnT_tile(ctx, p, xnT_d, xt[i], f"xt{i}", t * TW, TW)
        first = True
        for e_ in range(E):
            p.op("pe", lambda e, e_=e_, t=t: e.matmul(
                    pB[:, 0:TW], sel[:, e_*128:(e_+1)*128], mwT[:, t*TW:(t+1)*TW],
                    start=True, stop=True),
                 reads=("sel_s", "mwT"), writes=("pB",))
            p.op("dve", lambda e: e.tensor_copy(bsb[:], pB[:, 0:TW]), reads=("pB",), writes=("bsb",))
            def scale_fn(e, i=i):
                for k in range(ND):
                    e.tensor_tensor(xp[:, k*TW:(k+1)*TW], xt[i][:, k*TW:(k+1)*TW],
                                    bsb[:], ALU.mult)
            p.op("dve", scale_fn, reads=(f"xt{i}", "bsb"), writes=("xp",))
            def mm_e(e, e_=e_, first=first):
                for tc in range(TW // 128):
                    for j in range(2):
                        for k in range(ND):
                            e.matmul(po[tc*2+j][:], xp[:, k*TW + tc*128: k*TW + (tc+1)*128],
                                     ew[e_][k][:, j*512:(j+1)*512],
                                     start=(first and k == 0), stop=False)
            p.op("pe", mm_e, reads=("xp",) + tuple(f"ew{e_}_{k}" for k in range(ND)),
                 writes=tuple(f"po{j}" for j in range(4)))
            first = False
        def mm_b(e, t=t):
            for tc in range(TW // 128):
                for j in range(2):
                    e.matmul(po[tc*2+j][:], mwT[:, t*TW + tc*128: t*TW + (tc+1)*128],
                             eb[:, j*512:(j+1)*512], start=False, stop=True)
        p.op("pe", mm_b, reads=("mwT", "eb_s"), writes=tuple(f"po{j}" for j in range(4)))
        for tc in range(TW // 128):
            def ev(e, tc=tc):
                e.tensor_copy(outt[:, 0:512], po[tc*2][:])
                e.tensor_copy(outt[:, 512:1024], po[tc*2+1][:])
            p.op("dve", ev, reads=(f"po{tc*2}", f"po{tc*2+1}"), writes=("outt",))
            p.op("sp", lambda e, t=t, tc=tc: e.dma_start(
                    out=out_d[t*TW + tc*128: t*TW + (tc+1)*128, :], in_=outt[:]),
                 reads=("outt",), dma=True)
    p.finish()
    return ctx


def build_l3_ffn():
    E, I = 32, 512
    NI = I // 128
    ctx = Ctx()
    p = ctx.p
    xnT_d = ctx.din("xnT", [D, TC], BF16)
    xnTlo_d = ctx.din("xnT_lo", [D, TC], BF16)
    rw_d = ctx.din("rwT", [D, E], F32)
    rb_d = ctx.din("rb", [1, E], F32)
    gw_d = ctx.din("gwT", [E, D, I], F32)
    uw_d = ctx.din("uwT", [E, D, I], F32)
    dw_d = ctx.din("dwT", [E, I, D], F32)
    sgw_d = ctx.din("sgwT", [D, D], F32)
    suw_d = ctx.din("suwT", [D, D], F32)
    sdw_d = ctx.din("sdwT", [D, D], F32)
    g48_d = ctx.din("c_g48", [4, E], BF16)
    sel_d = ctx.din("c_sel32", [E, E * 128], BF16)
    out_d = ctx.dout("out", [TC, D], F32)
    consts = load_consts(ctx)

    rw = ctx.sbuf("rw", [128, ND * E], BF16)
    rwlo = ctx.sbuf("rwlo", [128, ND * E], BF16)
    rwf = ctx.sbuf("rwf", [128, ND * E], F32)
    xtlo = ctx.sbuf("xtlo", [128, ND * 128], BF16)
    rbb = ctx.sbuf("rbb", [128, E], F32)
    rbr = ctx.sbuf("rbr", [1, E], F32)
    g48 = ctx.sbuf("g48", [4, E], BF16)
    sel = ctx.sbuf("sel32", [E, E * 128], BF16)
    gw = [[ctx.sbuf(f"gw{i}_{k}", [128, I], BF16) for k in range(ND)] for i in range(2)]
    uw = [[ctx.sbuf(f"uw{i}_{k}", [128, I], BF16) for k in range(ND)] for i in range(2)]
    dw = [[ctx.sbuf(f"dw{i}_{k}", [128, D], BF16) for k in range(NI)] for i in range(2)]
    TW = 256
    NTW = TC // TW
    xtall = ctx.sbuf("xtall", [128, ND * TC], BF16)
    acc = [ctx.sbuf(f"acc{tc}", [128, D], F32) for tc in range(NT)]
    sc32 = ctx.sbuf("sc32", [128, E], F32)
    sfc = ctx.sbuf("sfc", [128, E], F32)
    tmp32 = ctx.sbuf("tmp32", [128, E], F32)
    orig = ctx.sbuf("orig", [128, E], F32)
    msk = ctx.sbuf("msk", [128, E], F32)
    gmT = ctx.sbuf("gmT", [4, 128], BF16)
    gs = ctx.sbuf("gs", [128, 8], F32)
    st8 = ctx.sbuf("st8", [128, 8], F32)
    gmb = ctx.sbuf("gmb", [128, 4], BF16)
    rr = ctx.sbuf("rr", [128, E], F32)
    rrb = ctx.sbuf("rrb", [128, E], BF16)
    rT = ctx.sbuf("rT", [E, TC], BF16)
    sg = [ctx.sbuf(f"sg3{ip}", [128, TW], BF16) for ip in range(2)]
    hm = [ctx.sbuf(f"hm3{ip}", [128, TW], BF16) for ip in range(2)]
    bsb = ctx.sbuf("bsb3", [128, TW], BF16)
    outt = ctx.sbuf("outt3", [128, D], F32)
    pB = ctx.psum("pB", [128, 512], F32)
    pmw = ctx.psum("pmw", [128, 1024], BF16)
    pg = ctx.psum("pg3", [128, 512], F32)
    pu = ctx.psum("pu3", [128, 512], F32)
    po = [ctx.psum(f"po{j}", [128, 512], F32) for j in range(4)]

    def ldrw(e):
        return e.dma_start(out=rwf[:].rearrange("p (f c) -> p f c", f=ND),
                           in_=rw_d.rearrange("(f p) c -> p f c", p=128))
    p.op("pool", ldrw, writes=("rwf",), dma=True)
    p.op("dve", lambda e: e.tensor_copy(rw[:], rwf[:]), reads=("rwf",), writes=("rw",))
    p.op("dve", lambda e: e.tensor_tensor(rwlo[:], rwf[:], rw[:], ALU.subtract),
         reads=("rwf", "rw"), writes=("rwlo",))
    p.op("pool", lambda e: e.dma_start(out=rbr[:], in_=rb_d), writes=("rbr",), dma=True)
    p.op("pool", lambda e: e.dma_start(out=g48[:], in_=g48_d), writes=("g48",), dma=True)
    p.op("pool", lambda e: e.dma_start(out=sel[:], in_=sel_d), writes=("sel32",), dma=True)
    p.op("pe", lambda e: e.matmul(pg[:, 0:E], consts["onesr"][:], rbr[:], start=True, stop=True),
         reads=("rbr", "onesr"), writes=("pg3",))
    p.op("dve", lambda e: e.tensor_copy(rbb[:], pg[:, 0:E]), reads=("pg3",), writes=("rbb",))
    def ldx(e):
        src = bass.AP(xnT_d.tensor, 0, [[TC, 128], [128 * TC, ND], [1, TC]])
        return e.dma_start(out=xtall[:].rearrange("p (f c) -> p f c", f=ND), in_=src)
    p.op("sp", ldx, writes=("xtall",), dma=True)

    for ti in range(NT):
        load_xnT_tile(ctx, p, xnTlo_d, xtlo, "xtlo", ti * 128, 128, eng="pool")
        def mm_lg(e, ti=ti):
            for k in range(ND):
                e.matmul(pg[:, 0:E], xtall[:, k*TC + ti*128: k*TC + (ti+1)*128],
                         rw[:, k*E:(k+1)*E], start=(k == 0), stop=False)
            for k in range(ND):
                e.matmul(pg[:, 0:E], xtlo[:, k*128:(k+1)*128], rw[:, k*E:(k+1)*E],
                         start=False, stop=False)
            for k in range(ND):
                e.matmul(pg[:, 0:E], xtall[:, k*TC + ti*128: k*TC + (ti+1)*128],
                         rwlo[:, k*E:(k+1)*E], start=False, stop=(k == ND-1))
        p.op("pe", mm_lg, reads=("xtall", "xtlo", "rw", "rwlo"), writes=("pg3",))
        p.op("act", lambda e: e.activation(sc32[:], pg[:, 0:E], AF.Sigmoid),
             reads=("pg3",), writes=("sc32",))
        p.op("dve", lambda e: e.tensor_tensor(sfc[:], sc32[:], rbb[:], ALU.add),
             reads=("sc32", "rbb"), writes=("sfc",))
        for g in range(4):
            def grp(e, g=g):
                e.reduce_max(st8[:, 0:1], sfc[:, g*8:(g+1)*8], axis=AX.X)
                e.drain()
                e.tensor_scalar(msk[:, 0:8], sfc[:, g*8:(g+1)*8], st8[:, 0:1], None, ALU.is_ge)
                e.drain()
                e.scalar_tensor_tensor(tmp32[:, 0:8], msk[:, 0:8], -1e30,
                                       sfc[:, g*8:(g+1)*8], ALU.mult, ALU.add)
                e.drain()
                e.reduce_max(st8[:, 1:2], tmp32[:, 0:8], axis=AX.X)
                e.drain()
                e.tensor_tensor(gs[:, g:g+1], st8[:, 0:1], st8[:, 1:2], ALU.add)
            p.op("dve", grp, reads=("sfc",), writes=("gs", "msk", "tmp32", "st8"))
        def topg(e):
            e.reduce_max(st8[:, 2:3], gs[:, 0:4], axis=AX.X)
            e.drain()
            e.tensor_scalar(msk[:, 0:4], gs[:, 0:4], st8[:, 2:3], None, ALU.is_ge)
            e.drain()
            e.scalar_tensor_tensor(tmp32[:, 0:4], msk[:, 0:4], -1e30, gs[:, 0:4],
                                   ALU.mult, ALU.add)
            e.drain()
            e.reduce_max(st8[:, 3:4], tmp32[:, 0:4], axis=AX.X)
            e.drain()
            e.tensor_scalar(gmb[:, 0:4], gs[:, 0:4], st8[:, 3:4], None, ALU.is_ge)
        p.op("dve", topg, reads=("gs",), writes=("gmb", "msk", "tmp32", "st8"))
        p.op("pe", lambda e: e.transpose(pmw[0:4, 0:128], gmb[:, 0:4], consts["ident"][:]),
             reads=("gmb", "ident"), writes=("pmw",))
        p.op("dve", lambda e: e.tensor_copy(gmT[:], pmw[0:4, 0:128]),
             reads=("pmw",), writes=("gmT",))
        p.op("pe", lambda e: e.matmul(pB[:, 0:E], gmT[:], g48[:], start=True, stop=True),
             reads=("gmT", "g48"), writes=("pB",))
        p.op("dve", lambda e: e.tensor_tensor(orig[:], pB[:, 0:E], sfc[:], ALU.mult),
             reads=("pB", "sfc"), writes=("orig",))
        p.op("dve", lambda e: e.tensor_copy(tmp32[:], orig[:]), reads=("orig",), writes=("tmp32",))
        for it in range(3):
            def pen(e):
                e.reduce_max(st8[:, 4:5], tmp32[:], axis=AX.X)
                e.drain()
                e.tensor_scalar(msk[:], tmp32[:], st8[:, 4:5], None, ALU.is_ge)
                e.drain()
                e.scalar_tensor_tensor(tmp32[:], msk[:], -1e30, tmp32[:], ALU.mult, ALU.add)
            p.op("dve", pen, reads=("tmp32",), writes=("tmp32", "msk", "st8"))
        def fin(e):
            e.reduce_max(st8[:, 5:6], tmp32[:], axis=AX.X)
            e.drain()
            e.tensor_scalar(msk[:], orig[:], st8[:, 5:6], None, ALU.is_ge)
            e.drain()
            e.tensor_tensor(rr[:], msk[:], sc32[:], ALU.mult)
            e.drain()
            e.reduce_sum(st8[:, 6:7], rr[:], axis=AX.X)
            e.drain()
            e.tensor_scalar(st8[:, 6:7], st8[:, 6:7], 1e-20, None, ALU.add)
        p.op("dve", fin, reads=("tmp32", "orig", "sc32"), writes=("rr", "msk", "st8"))
        p.op("dve", lambda e: e.reciprocal(st8[:, 7:8], st8[:, 6:7]),
             reads=("st8",), writes=("st8",))
        p.op("dve", lambda e: e.tensor_scalar(rrb[:], rr[:], st8[:, 7:8], 2.5, ALU.mult, ALU.mult),
             reads=("rr", "st8"), writes=("rrb",))
        p.op("pe", lambda e: e.transpose(pmw[0:E, 0:128], rrb[:], consts["ident"][:]),
             reads=("rrb", "ident"), writes=("pmw",))
        p.op("dve", lambda e, ti=ti: e.tensor_copy(rT[:, ti*128:(ti+1)*128], pmw[0:E, 0:128]),
             reads=("pmw",), writes=("rT",))

    for ic in range(ND):
        for k in range(ND):
            p.op("pool", lambda e, ic=ic, k=k: e.dma_start(
                    out=gw[0][k][:, 0:128], in_=sgw_d[k*128:(k+1)*128, ic*128:(ic+1)*128]),
                 writes=(f"gw0_{k}",), dma=True)
            p.op("pool", lambda e, ic=ic, k=k: e.dma_start(
                    out=uw[0][k][:, 0:128], in_=suw_d[k*128:(k+1)*128, ic*128:(ic+1)*128]),
                 writes=(f"uw0_{k}",), dma=True)
        p.op("pool", lambda e, ic=ic: e.dma_start(
                out=dw[0][0][:], in_=sdw_d[ic*128:(ic+1)*128, :]),
             writes=("dw0_0",), dma=True)
        def emitGUs(t):
            ip = t % 2
            def mm_g(e, t=t):
                for k in range(ND):
                    e.matmul(pg[:, 0:TW], gw[0][k][:, 0:128],
                             xtall[:, k*TC + t*TW: k*TC + (t+1)*TW],
                             start=(k == 0), stop=(k == ND-1))
            p.op("pe", mm_g, reads=("xtall",) + tuple(f"gw0_{k}" for k in range(ND)), writes=("pg3",))
            def mm_u(e, t=t):
                for k in range(ND):
                    e.matmul(pu[:, 0:TW], uw[0][k][:, 0:128],
                             xtall[:, k*TC + t*TW: k*TC + (t+1)*TW],
                             start=(k == 0), stop=(k == ND-1))
            p.op("pe", mm_u, reads=("xtall",) + tuple(f"uw0_{k}" for k in range(ND)), writes=("pu3",))
            p.op("act", lambda e, ip=ip: e.activation(sg[ip][:], pg[:, 0:TW], AF.Silu),
                 reads=("pg3",), writes=(f"sg3{ip}",))
            p.op("dve", lambda e, ip=ip: e.tensor_tensor(hm[ip][:], sg[ip][:], pu[:, 0:TW], ALU.mult),
                 reads=(f"sg3{ip}", "pu3"), writes=(f"hm3{ip}",))
        def emitDs(t, ic):
            ip = t % 2
            def mm_d(e, ip=ip):
                for tc in range(TW // 128):
                    for j in range(2):
                        e.matmul(po[tc*2+j][:], hm[ip][:, tc*128:(tc+1)*128],
                                 dw[0][0][:, j*512:(j+1)*512],
                                 start=True, stop=True)
            p.op("pe", mm_d, reads=(f"hm3{ip}", "dw0_0"), writes=tuple(f"po{j}" for j in range(4)))
            for tc in range(TW // 128):
                gtc = t * (TW // 128) + tc
                if ic == 0:
                    def ev(e, tc=tc, gtc=gtc):
                        e.tensor_copy(acc[gtc][:, 0:512], po[tc*2][:])
                        e.tensor_copy(acc[gtc][:, 512:1024], po[tc*2+1][:])
                else:
                    def ev(e, tc=tc, gtc=gtc):
                        e.tensor_tensor(acc[gtc][:, 0:512], acc[gtc][:, 0:512], po[tc*2][:], ALU.add)
                        e.tensor_tensor(acc[gtc][:, 512:1024], acc[gtc][:, 512:1024], po[tc*2+1][:], ALU.add)
                p.op("dve", ev, reads=(f"po{tc*2}", f"po{tc*2+1}") + ((f"acc{gtc}",) if ic else ()),
                     writes=(f"acc{gtc}",))
        for t in range(NTW):
            emitGUs(t)
            if t > 0:
                emitDs(t - 1, ic)
        emitDs(NTW - 1, ic)

    def load_expert(e_, slot):
        for k in range(ND):
            p.op("pool", lambda e, e_=e_, k=k, slot=slot: e.dma_start(
                    out=gw[slot][k][:], in_=gw_d[e_, k*128:(k+1)*128, :]),
                 writes=(f"gw{slot}_{k}",), dma=True)
            p.op("pool", lambda e, e_=e_, k=k, slot=slot: e.dma_start(
                    out=uw[slot][k][:], in_=uw_d[e_, k*128:(k+1)*128, :]),
                 writes=(f"uw{slot}_{k}",), dma=True)
        for k in range(NI):
            p.op("pool", lambda e, e_=e_, k=k, slot=slot: e.dma_start(
                    out=dw[slot][k][:], in_=dw_d[e_, k*128:(k+1)*128, :]),
                 writes=(f"dw{slot}_{k}",), dma=True)

    load_expert(0, 0)
    load_expert(1, 1)
    for pair in range(16):
        for t in range(NTW):
            for e2 in range(2):
                e_ = pair * 2 + e2
                slot = e_ % 2
                p.op("pe", lambda e, e_=e_, t=t: e.matmul(
                        pB[:, 0:TW], sel[:, e_*128:(e_+1)*128], rT[:, t*TW:(t+1)*TW],
                        start=True, stop=True),
                     reads=("sel32", "rT"), writes=("pB",))
                p.op("dve", lambda e: e.tensor_copy(bsb[:], pB[:, 0:TW]),
                     reads=("pB",), writes=("bsb3",))
                def emitGUe(ic, t=t, slot=slot, e_=e_):
                    ip = ic % 2
                    def mm_g(e, t=t, ic=ic, slot=slot):
                        for k in range(ND):
                            e.matmul(pg[:, 0:TW], gw[slot][k][:, ic*128:(ic+1)*128],
                                     xtall[:, k*TC + t*TW: k*TC + (t+1)*TW],
                                     start=(k == 0), stop=(k == ND-1))
                    p.op("pe", mm_g, reads=("xtall",) + tuple(f"gw{slot}_{k}" for k in range(ND)),
                         writes=("pg3",))
                    def mm_u(e, t=t, ic=ic, slot=slot):
                        for k in range(ND):
                            e.matmul(pu[:, 0:TW], uw[slot][k][:, ic*128:(ic+1)*128],
                                     xtall[:, k*TC + t*TW: k*TC + (t+1)*TW],
                                     start=(k == 0), stop=(k == ND-1))
                    p.op("pe", mm_u, reads=("xtall",) + tuple(f"uw{slot}_{k}" for k in range(ND)),
                         writes=("pu3",))
                    p.op("act", lambda e, ip=ip: e.activation(sg[ip][:], pg[:, 0:TW], AF.Silu),
                         reads=("pg3",), writes=(f"sg3{ip}",))
                    def hfn(e, ip=ip):
                        e.tensor_tensor(hm[ip][:], sg[ip][:], pu[:, 0:TW], ALU.mult)
                        e.drain()
                        e.tensor_tensor(hm[ip][:], hm[ip][:], bsb[:], ALU.mult)
                    p.op("dve", hfn, reads=(f"sg3{ip}", "pu3", "bsb3"), writes=(f"hm3{ip}",))
                def emitDe(ic, e2=e2, slot=slot):
                    ip = ic % 2
                    def mm_d(e, ic=ic, e2=e2, slot=slot, ip=ip):
                        for tc in range(TW // 128):
                            for j in range(2):
                                e.matmul(po[tc*2+j][:], hm[ip][:, tc*128:(tc+1)*128],
                                         dw[slot][ic][:, j*512:(j+1)*512],
                                         start=(e2 == 0 and ic == 0),
                                         stop=(e2 == 1 and ic == NI-1))
                    p.op("pe", mm_d, reads=(f"hm3{ip}", f"dw{slot}_{ic}"),
                         writes=tuple(f"po{j}" for j in range(4)))
                for ic in range(NI):
                    emitGUe(ic)
                    if ic > 0:
                        emitDe(ic - 1)
                emitDe(NI - 1)
                if t == NTW - 1 and e_ + 2 < 2 * E // 2:
                    load_expert(e_ + 2, slot)
            for tc in range(TW // 128):
                gtc = t * (TW // 128) + tc
                def ev(e, tc=tc, gtc=gtc):
                    e.tensor_tensor(acc[gtc][:, 0:512], acc[gtc][:, 0:512], po[tc*2][:], ALU.add)
                    e.tensor_tensor(acc[gtc][:, 512:1024], acc[gtc][:, 512:1024], po[tc*2+1][:], ALU.add)
                p.op("dve", ev, reads=(f"po{tc*2}", f"po{tc*2+1}", f"acc{gtc}"),
                     writes=(f"acc{gtc}",))
    for gtc in range(NT):
        p.op("sp", lambda e, gtc=gtc: e.dma_start(
                out=out_d[gtc*128:(gtc+1)*128, :], in_=acc[gtc][:]),
             reads=(f"acc{gtc}",), dma=True)
    p.finish()
    return ctx


def build_l0a_sig():
    ctx = Ctx()
    p = ctx.p
    xnT_d = ctx.din("xnT", [D, TC], BF16)
    wret_d = ctx.din("wret", [D, D], F32)
    bret_d = ctx.din("bret", [128, ND], F32)
    sig_d = ctx.dout("sig", [D, TC], F32)
    wr = [ctx.sbuf(f"wr{k}", [128, D], BF16) for k in range(ND)]
    br = ctx.sbuf("br", [128, ND], F32)
    xt = [ctx.sbuf(f"xt{i}", [128, ND * 512], BF16) for i in range(2)]
    sig = ctx.sbuf("sig_s", [128, 512], F32)
    ps = ctx.psum("ps", [128, 512], F32)
    for k in range(ND):
        p.op("pool", lambda e, k=k: e.dma_start(out=wr[k][:], in_=wret_d[k*128:(k+1)*128, :]),
             writes=(f"wr{k}",), dma=True)
    p.op("pool", lambda e: e.dma_start(out=br[:], in_=bret_d), writes=("br",), dma=True)
    for t in range(TC // 512):
        i = t % 2
        load_xnT_tile(ctx, p, xnT_d, xt[i], f"xt{i}", t * 512, 512)
        for m in range(ND):
            def mm(e, m=m, i=i):
                for k in range(ND):
                    e.matmul(ps[:], wr[k][:, m*128:(m+1)*128], xt[i][:, k*512:(k+1)*512],
                             start=(k == 0), stop=(k == ND-1))
            p.op("pe", mm, reads=(f"xt{i}",) + tuple(f"wr{k}" for k in range(ND)), writes=("ps",))
            p.op("act", lambda e, m=m: e.activation(sig[:], ps[:], AF.Sigmoid, bias=br[:, m:m+1]),
                 reads=("ps", "br"), writes=("sig_s",))
            p.op("sp", lambda e, m=m, t=t: e.dma_start(
                    out=sig_d[m*128:(m+1)*128, t*512:(t+1)*512], in_=sig[:]),
                 reads=("sig_s",), dma=True)
    p.finish()
    return ctx


def build_l0a_ema():
    ctx = Ctx()
    p = ctx.p
    sig_d = ctx.din("sig_bd", [512, S], F32)
    ret_d = ctx.dout("ret_bd", [512, S], F32)
    halfs = ctx.sbuf("halfs", [128, S], F32)
    row = [ctx.sbuf(f"row{i}", [128, S], F32) for i in range(2)]
    out = [ctx.sbuf(f"out{i}", [128, S], F32) for i in range(2)]
    p.op("dve", lambda e: e.memset(halfs[:], 0.5), writes=("halfs",))
    for c in range(4):
        i = c % 2
        p.op("sp", lambda e, c=c, i=i: e.dma_start(out=row[i][:], in_=sig_d[c*128:(c+1)*128, :]),
             writes=(f"row{i}",), dma=True)
        p.op("dve", lambda e, i=i: e.tensor_tensor_scan(
                out[i][:], halfs[:], row[i][:], 0.0, ALU.mult, ALU.add),
             reads=(f"row{i}", "halfs"), writes=(f"out{i}",))
        p.op("sp", lambda e, c=c, i=i: e.dma_start(out=ret_d[c*128:(c+1)*128, :], in_=out[i][:]),
             reads=(f"out{i}",), dma=True)
    p.finish()
    return ctx


def emit_feat_ln(ctx, p, consts, v_bufs, v_tiles, TW, onescol, pstat1, pstat2,
                 g_s, gcol, b_s, bcol, rowscr, vsq, h_tiles, h_bufs, hb_tiles, hb_bufs,
                 epsT1, musb, rssb):
    for k in range(ND):
        p.op("act", lambda e, k=k: e.activation(vsq[:, 0:TW], v_tiles[k][:], AF.Square),
             reads=(v_bufs[k],), writes=("vsq",))
        def mm_s(e, k=k):
            e.matmul(pstat1[0:1, 0:TW], onescol[:, 0:1], v_tiles[k][:],
                     start=(k == 0), stop=(k == ND-1))
        p.op("pe", mm_s, reads=(v_bufs[k], "onescol"), writes=("pstat1",))
        def mm_q(e, k=k):
            e.matmul(pstat2[0:1, 0:TW], onescol[:, 0:1], vsq[:, 0:TW],
                     start=(k == 0), stop=(k == ND-1))
        p.op("pe", mm_q, reads=("vsq", "onescol"), writes=("pstat2",))
    def rows_fn(e):
        e.tensor_scalar(rowscr[0:1, 0:TW], pstat1[0:1, 0:TW], 1.0 / D, None, ALU.mult)
        e.tensor_scalar(rowscr[0:1, TW:2*TW], pstat2[0:1, 0:TW], 1.0 / D, None, ALU.mult)
        e.drain()
        e.tensor_tensor(rowscr[0:1, 2*TW:3*TW], rowscr[0:1, 0:TW], rowscr[0:1, 0:TW], ALU.mult)
        e.drain()
        e.tensor_tensor(rowscr[0:1, 3*TW:4*TW], rowscr[0:1, TW:2*TW], rowscr[0:1, 2*TW:3*TW], ALU.subtract)
    p.op("dve", rows_fn, reads=("pstat1", "pstat2"), writes=("rowscr",))
    p.op("act", lambda e: e.activation(rowscr[0:1, 4*TW:5*TW], rowscr[0:1, 3*TW:4*TW],
                                       AF.Sqrt, bias=epsT1[0:1, 0:1]),
         reads=("rowscr", "epsT1"), writes=("rowscr",))
    def newton_fn(e):
        e.reciprocal(rowscr[0:1, 5*TW:6*TW], rowscr[0:1, 4*TW:5*TW])
        e.tensor_scalar(rowscr[0:1, 3*TW:4*TW], rowscr[0:1, 3*TW:4*TW], 1e-5, None, ALU.add)
        e.drain()
        e.tensor_tensor(rowscr[0:1, 6*TW:7*TW], rowscr[0:1, 5*TW:6*TW], rowscr[0:1, 5*TW:6*TW], ALU.mult)
        e.drain()
        e.tensor_tensor(rowscr[0:1, 6*TW:7*TW], rowscr[0:1, 6*TW:7*TW], rowscr[0:1, 3*TW:4*TW], ALU.mult)
        e.drain()
        e.tensor_scalar(rowscr[0:1, 6*TW:7*TW], rowscr[0:1, 6*TW:7*TW], -0.5, 1.5, ALU.mult, ALU.add)
        e.drain()
        e.tensor_tensor(rowscr[0:1, 5*TW:6*TW], rowscr[0:1, 5*TW:6*TW], rowscr[0:1, 6*TW:7*TW], ALU.mult)
    p.op("dve", newton_fn, reads=("rowscr",), writes=("rowscr",))
    p.op("pe", lambda e: e.matmul(pstat1[:, 0:TW], consts["onesr"][:], rowscr[0:1, 0:TW],
                                  start=True, stop=True),
         reads=("rowscr", "onesr"), writes=("pstat1",))
    p.op("pe", lambda e: e.matmul(pstat2[:, 0:TW], consts["onesr"][:], rowscr[0:1, 5*TW:6*TW],
                                  start=True, stop=True),
         reads=("rowscr", "onesr"), writes=("pstat2",))
    p.op("dve", lambda e: e.tensor_copy(musb[:], pstat1[:, 0:TW]), reads=("pstat1",),
         writes=("musb",))
    p.op("dve", lambda e: e.tensor_copy(rssb[:], pstat2[:, 0:TW]), reads=("pstat2",),
         writes=("rssb",))
    for k in range(ND):
        def hfn(e, k=k):
            e.tensor_tensor(h_tiles[k][:], v_tiles[k][:], musb[:], ALU.subtract)
            e.drain()
            e.tensor_tensor(h_tiles[k][:], h_tiles[k][:], rssb[:], ALU.mult)
            e.drain()
            e.tensor_scalar(h_tiles[k][:], h_tiles[k][:], g_s[:, gcol+k:gcol+k+1],
                            b_s[:, bcol+k:bcol+k+1], ALU.mult, ALU.add)
            e.drain()
            e.tensor_copy(hb_tiles[k][:], h_tiles[k][:])
        p.op("dve", hfn, reads=(v_bufs[k], "musb", "rssb", "lnp"),
             writes=(h_bufs[k], hb_bufs[k]))


def build_l0a_rest():
    F4 = 4096
    NF = F4 // 128
    TW = 256
    ctx = Ctx()
    p = ctx.p
    xnT_d = ctx.din("xnT", [D, TC], BF16)
    retT_d = ctx.din("retT", [D, TC], F32)
    w1_d = ctx.din("w1", [D, F4], F32)
    b1_d = ctx.din("b1", [128, 32], F32)
    w2_d = ctx.din("w2", [F4, D], F32)
    b2_d = ctx.din("b2", [128, ND], F32)
    lnp_d = ctx.din("lnp4", [128, 4 * ND], F32)
    identf_d = ctx.din("c_ident_f32", [128, 128], F32)
    onescol_d = ctx.din("c_ones_col", [128, 1], F32)
    out_d = ctx.dout("out", [TC, D], F32)
    consts = load_consts(ctx)

    w1b = [ctx.sbuf(f"w1b{k}", [128, F4], BF16) for k in range(ND)]
    w2b = [ctx.sbuf(f"w2b{m}", [128, D], BF16) for m in range(NF)]
    lnp4 = ctx.sbuf("lnp4_s", [128, 4 * ND], F32)
    b1s = ctx.sbuf("b1s", [128, NF], F32)
    b2s = ctx.sbuf("b2s", [128, ND], F32)
    identf = ctx.sbuf("identf", [128, 128], F32)
    onescol = ctx.sbuf("onescol", [128, 1], F32)
    xt = ctx.sbuf("xt", [128, ND * TW], BF16)
    rt = ctx.sbuf("rt", [128, ND * TW], F32)
    v = [ctx.sbuf(f"v{k}", [128, TW], F32) for k in range(ND)]
    h = [ctx.sbuf(f"h{k}", [128, TW], F32) for k in range(ND)]
    hb = [ctx.sbuf(f"hb{k}", [128, TW], BF16) for k in range(ND)]
    gelsb = ctx.sbuf("gel", [128, 32 * TW], BF16)
    vsq = ctx.sbuf("vsq", [128, TW], F32)
    rowscr = ctx.sbuf("rowscr", [1, 7 * TW], F32)
    musb = ctx.sbuf("musb", [128, TW], F32)
    rssb = ctx.sbuf("rssb", [128, TW], F32)
    epsT1 = ctx.sbuf("epsT1", [1, 1], F32)
    outsb = ctx.sbuf("outsb", [128, D], F32)
    pga = [ctx.psum(f"pga{ip}", [128, 512], F32) for ip in range(2)]
    pf = [ctx.psum(f"pf{j}", [128, 512], F32) for j in range(4)]
    pstat1 = ctx.psum("pstat1", [128, 512], F32)
    pstat2 = ctx.psum("pstat2", [128, 512], F32)

    p.op("dve", lambda e: e.memset(epsT1[:], 1e-5), writes=("epsT1",))
    for k in range(ND):
        p.op("pool", lambda e, k=k: e.dma_start(out=w1b[k][:], in_=w1_d[k*128:(k+1)*128, :]),
             writes=(f"w1b{k}",), dma=True)
    for m in range(NF):
        p.op("pool", lambda e, m=m: e.dma_start(out=w2b[m][:], in_=w2_d[m*128:(m+1)*128, :]),
             writes=(f"w2b{m}",), dma=True)
    p.op("pool", lambda e: e.dma_start(out=lnp4[:], in_=lnp_d), writes=("lnp",), dma=True)
    p.op("pool", lambda e: e.dma_start(out=b1s[:], in_=b1_d), writes=("lnp",), dma=True)
    p.op("pool", lambda e: e.dma_start(out=b2s[:], in_=b2_d), writes=("lnp",), dma=True)
    p.op("sp", lambda e: e.dma_start(out=identf[:], in_=identf_d), writes=("identf",), dma=True)
    p.op("sp", lambda e: e.dma_start(out=onescol[:], in_=onescol_d), writes=("onescol",), dma=True)

    vb = [f"v{k}" for k in range(ND)]
    hbn = [f"h{k}" for k in range(ND)]
    hbb = [f"hb{k}" for k in range(ND)]
    for t in range(TC // TW):
        def ldx(e, t=t):
            srcv = bass.AP(xnT_d.tensor, t * TW, [[TC, 128], [128 * TC, ND], [1, TW]])
            return e.dma_start(out=xt[:].rearrange("p (f c) -> p f c", f=ND), in_=srcv)
        p.op("sp", ldx, writes=("xt",), dma=True)
        def ldr(e, t=t):
            srcv = bass.AP(retT_d.tensor, t * TW, [[TC, 128], [128 * TC, ND], [1, TW]])
            return e.dma_start(out=rt[:].rearrange("p (f c) -> p f c", f=ND), in_=srcv)
        p.op("sp", ldr, writes=("rt",), dma=True)
        for k in range(ND):
            p.op("dve", lambda e, k=k: e.scalar_tensor_tensor(
                    v[k][:], rt[:, k*TW:(k+1)*TW], 0.5, xt[:, k*TW:(k+1)*TW], ALU.mult, ALU.add),
                 reads=("rt", "xt"), writes=(f"v{k}",))
        emit_feat_ln(ctx, p, consts, vb, v, TW, onescol, pstat1, pstat2,
                     lnp4, 0 * ND, lnp4, 1 * ND, rowscr, vsq, h, hbn, hb, hbb,
                     epsT1, musb, rssb)
        def emit1(m):
            ip = m % 2
            def mm_1(e, m=m, ip=ip):
                for k in range(ND):
                    e.matmul(pga[ip][:, 0:TW], w1b[k][:, m*128:(m+1)*128], hb[k][:],
                             start=(k == 0), stop=(k == ND-1))
            p.op("pe", mm_1, reads=tuple(hbb) + tuple(f"w1b{k}" for k in range(ND)),
                 writes=(f"pga{ip}",))
            p.op("act", lambda e, m=m, ip=ip: e.activation(gelsb[:, m*TW:(m+1)*TW], pga[ip][:, 0:TW],
                                                           AF.Gelu, bias=b1s[:, m:m+1]),
                 reads=(f"pga{ip}", "lnp"), writes=("gel",))
        def emit2(m):
            def mm_2(e, m=m):
                for dc in range(4):
                    e.matmul(pf[dc][:, 0:TW], w2b[m][:, dc*128:(dc+1)*128],
                             gelsb[:, m*TW:(m+1)*TW], start=(m == 0), stop=(m == NF-1))
            p.op("pe", mm_2, reads=("gel", f"w2b{m}"), writes=tuple(f"pf{j}" for j in range(4)))
        for m in range(NF):
            emit1(m)
            if m > 0:
                emit2(m - 1)
        emit2(NF - 1)
        for k in range(4):
            p.op("dve", lambda e, k=k: e.scalar_tensor_tensor(
                    v[k][:], pf[k][:, 0:TW], b2s[:, k:k+1], h[k][:], ALU.add, ALU.add),
                 reads=(f"pf{k}", f"h{k}", "lnp"), writes=(f"v{k}",))
        for m in range(NF):
            def mm_2b(e, m=m):
                for dc in range(4):
                    e.matmul(pf[dc][:, 0:TW], w2b[m][:, (dc+4)*128:(dc+5)*128],
                             gelsb[:, m*TW:(m+1)*TW], start=(m == 0), stop=(m == NF-1))
            p.op("pe", mm_2b, reads=("gel", f"w2b{m}"), writes=tuple(f"pf{j}" for j in range(4)))
        for k in range(4, ND):
            p.op("dve", lambda e, k=k: e.scalar_tensor_tensor(
                    v[k][:], pf[k-4][:, 0:TW], b2s[:, k:k+1], h[k][:], ALU.add, ALU.add),
                 reads=(f"pf{k-4}", f"h{k}", "lnp"), writes=(f"v{k}",))
        emit_feat_ln(ctx, p, consts, vb, v, TW, onescol, pstat1, pstat2,
                     lnp4, 2 * ND, lnp4, 3 * ND, rowscr, vsq, h, hbn, hb, hbb,
                     epsT1, musb, rssb)
        for tc in range(TW // 128):
            for k in range(ND):
                bank, bname = (pstat1, "pstat1") if k % 2 == 0 else (pstat2, "pstat2")
                sl = (k // 2) % 2
                p.op("pe", lambda e, k=k, tc=tc, bank=bank, sl=sl: e.transpose(
                        bank[:, sl*128:(sl+1)*128], h[k][:, tc*128:(tc+1)*128], identf[:]),
                     reads=(f"h{k}", "identf"), writes=(bname,))
                p.op("dve", lambda e, k=k, bank=bank, sl=sl: e.tensor_copy(
                        outsb[:, k*128:(k+1)*128], bank[:, sl*128:(sl+1)*128]),
                     reads=(bname,), writes=("outsb",))
            p.op("sp", lambda e, t=t, tc=tc: e.dma_start(
                    out=out_d[t*TW + tc*128: t*TW + (tc+1)*128, :], in_=outsb[:]),
                 reads=("outsb",), dma=True)
    p.finish()
    return ctx


HALO = 384
YTW = HALO + TC

def l0f_host_consts(s):
    DH, W = 64, 256
    half = np.arange(0, DH, 2, dtype=np.float64)
    inv_freq = 1.0 / 10000.0 ** (half / DH)
    base_scale = (half + 0.4 * DH) / (1.4 * DH)
    def tables(pos):
        t = pos.astype(np.float64)
        fr = t[:, None] * inv_freq[None, :]
        fr = np.concatenate([fr, fr], -1)
        power = (t - S // 2) / (W // 2)
        sc = base_scale[None, :] ** power[:, None]
        sc = np.concatenate([sc, sc], -1)
        return fr, sc
    pos_q = s * TC + np.arange(TC)
    pos_k = s * TC + np.arange(-HALO, TC)
    frq, scq = tables(pos_q)
    frk, sck = tables(pos_k)
    def pack(m):
        r = np.concatenate([m.T, m.T], 0)
        return np.ascontiguousarray(r).astype(BF)
    cq_cos = pack(np.cos(frq) * scq)
    cq_sin = pack(np.sin(frq) * scq)
    ck_cos = pack(np.cos(frk) / sck)
    ck_sin = pack(np.sin(frk) / sck)
    P64 = np.zeros((128, 128), np.float32)
    for hh in range(2):
        o = hh * 64
        for m in range(64):
            if m < 32:
                P64[o + m + 32, o + m] = -1.0
            else:
                P64[o + m - 32, o + m] = 1.0
    masks = np.zeros((2, 2, 128, 512), np.float32)
    for wt in range(2):
        for c2 in range(2):
            i = np.arange(128)[:, None] + 128 * c2
            j = np.arange(512)[None, :]
            allowed = (j >= i) & (j <= i + 256)
            if wt == 0 and s == 0:
                allowed &= (j >= 256)
            masks[wt, c2] = np.where(allowed, 0.0, -1e30)
    G2 = np.zeros((128, 2), np.float32)
    G2[0:64, 0] = 1.0
    G2[64:128, 1] = 1.0
    G2T = np.ascontiguousarray(G2.T)
    return dict(l0f_cqc=cq_cos, l0f_cqs=cq_sin, l0f_ckc=ck_cos, l0f_cks=ck_sin,
                l0f_p64=P64.astype(BF), l0f_masks=masks.reshape(4 * 128, 512),
                l0f_g2=G2, l0f_g2t=G2T)


def build_l0f_attn(dbg=False):
    NW = TC // 256
    ctx = Ctx()
    p = ctx.p
    y_d = ctx.din("y", [TC, D], F32)
    yhalo_d = ctx.din("y_halo", [HALO, D], F32)
    wqkv_d = ctx.din("wqkv", [D, 3072], F32)
    wo_d = ctx.din("wo", [1024, D], F32)
    qs_d = ctx.din("qsv", [128, 1], F32)
    ks_d = ctx.din("ksv", [128, 1], F32)
    cqc_d = ctx.din("l0f_cqc", [128, TC], BF16)
    cqs_d = ctx.din("l0f_cqs", [128, TC], BF16)
    ckc_d = ctx.din("l0f_ckc", [128, YTW], BF16)
    cks_d = ctx.din("l0f_cks", [128, YTW], BF16)
    p64_d = ctx.din("l0f_p64", [128, 128], BF16)
    masks_d = ctx.din("l0f_masks", [4 * 128, 512], F32)
    g2_d = ctx.din("l0f_g2", [128, 2], F32)
    g2t_d = ctx.din("l0f_g2t", [2, 128], F32)
    yout_d = ctx.dout("y_new", [TC, D], F32)
    if dbg:
        dbg_q = ctx.dout("dbg_q", [ND * 128, 256], F32)
        dbg_k = ctx.dout("dbg_k", [ND * 128, 256], F32)
        dbg_p = ctx.dout("dbg_p", [128, 512], F32)
        dbg_at = ctx.dout("dbg_at", [ND * 128, 256], F32)
    consts = load_consts(ctx)

    wqkv = [ctx.sbuf(f"wq{k}", [128, 3072], BF16) for k in range(ND)]
    wo = [ctx.sbuf(f"wo{k}", [128, D], BF16) for k in range(ND)]
    yT = [ctx.sbuf(f"yT{k}", [128, YTW], BF16) for k in range(ND)]
    cqc = ctx.sbuf("cqc", [128, TC], BF16)
    cqs = ctx.sbuf("cqs", [128, TC], BF16)
    ckc = ctx.sbuf("ckc", [128, YTW], BF16)
    cks = ctx.sbuf("cks", [128, YTW], BF16)
    p64 = ctx.sbuf("p64", [128, 128], BF16)
    msks = ctx.sbuf("msks", [128, 4 * 512], F32)
    g2 = ctx.sbuf("g2", [128, 2], F32)
    g2t = ctx.sbuf("g2t", [2, 128], F32)
    qsv = ctx.sbuf("qsv_s", [128, 1], F32)
    ksv = ctx.sbuf("ksv_s", [128, 1], F32)
    yin = ctx.sbuf("yin", [128, D], F32)
    ybf = ctx.sbuf("ybf", [128, D], BF16)
    qw = [ctx.sbuf(f"qw{c}", [128, 256], BF16) for c in range(ND)]
    kw = [[ctx.sbuf(f"kw{r}_{c}", [128, 256], BF16) for c in range(ND)] for r in range(2)]
    vw = [[ctx.sbuf(f"vw{r}_{tc}", [128, D], BF16) for tc in range(2)] for r in range(2)]
    qsq = ctx.sbuf("qsq", [128, 256], F32)
    ninv = ctx.sbuf("ninv", [128, 256], F32)
    nrow = ctx.sbuf("nrow", [2, 3 * 256], F32)
    qnb = ctx.sbuf("qnb", [128, 256], BF16)
    rotb = ctx.sbuf("rotb", [128, 256], BF16)
    smsk = ctx.sbuf("smsk", [128, 512], F32)
    pexp = ctx.sbuf("pexp", [128, 512], BF16)
    ptsb = ctx.sbuf("ptsb", [128, 512], BF16)
    rowstat = ctx.sbuf("rowstat", [128, 4], F32)
    attnw = [ctx.sbuf(f"attnw{c}", [128, 256], BF16) for c in range(ND)]
    outt = ctx.sbuf("outt", [128, D], F32)
    pqk = ctx.psum("pqk", [128, 512], F32)
    pS = [ctx.psum(f"pS{r}", [128, 512], F32) for r in range(2)]
    pT = ctx.psum("pT", [128, 1024], BF16)
    pav = [ctx.psum(f"pav{r}", [128, 512], F32) for r in range(2)]
    pn = ctx.psum("pn", [128, 512], F32)

    for k in range(ND):
        p.op("pool", lambda e, k=k: e.dma_start(out=wqkv[k][:], in_=wqkv_d[k*128:(k+1)*128, :]),
             writes=(f"wq{k}",), dma=True)
        p.op("pool", lambda e, k=k: e.dma_start(out=wo[k][:], in_=wo_d[k*128:(k+1)*128, :]),
             writes=(f"wo{k}",), dma=True)
    for sb, dd, nm in [(cqc, cqc_d, "cqc"), (cqs, cqs_d, "cqs"), (ckc, ckc_d, "ckc"),
                       (cks, cks_d, "cks"), (p64, p64_d, "p64"), (g2, g2_d, "g2"),
                       (g2t, g2t_d, "g2t"), (qsv, qs_d, "qsv_s"), (ksv, ks_d, "ksv_s")]:
        p.op("sp", lambda e, sb=sb, dd=dd: e.dma_start(out=sb[:], in_=dd), writes=(nm,), dma=True)
    def ldm(e):
        return e.dma_start(out=msks[:].rearrange("p (m c) -> p m c", m=4),
                           in_=masks_d.rearrange("(m p) c -> p m c", p=128))
    p.op("sp", ldm, writes=("msks",), dma=True)

    for blk in range(YTW // 128):
        if blk < 3:
            p.op("sp", lambda e, blk=blk: e.dma_start(
                    out=yin[:], in_=yhalo_d[blk*128:(blk+1)*128, :]), writes=("yin",), dma=True)
        else:
            t0 = (blk - 3) * 128
            p.op("sp", lambda e, t0=t0: e.dma_start(
                    out=yin[:], in_=y_d[t0:t0+128, :]), writes=("yin",), dma=True)
        p.op("dve", lambda e: e.tensor_copy(ybf[:], yin[:]), reads=("yin",), writes=("ybf",))
        for k in range(ND):
            bank = k % 2
            p.op("pe", lambda e, k=k, bank=bank: e.transpose(
                    pT[:, (k % 2)*512 + (k // 2 % 4)*128: (k % 2)*512 + (k // 2 % 4)*128 + 128],
                    ybf[:, k*128:(k+1)*128], consts["ident"][:]),
                 reads=("ybf", "ident"), writes=("pT",))
            p.op("dve" if k % 2 == 0 else "act",
                 (lambda e, k=k, blk=blk: e.tensor_copy(
                    yT[k][:, blk*128:(blk+1)*128],
                    pT[:, (k % 2)*512 + (k // 2 % 4)*128: (k % 2)*512 + (k // 2 % 4)*128 + 128]))
                 if k % 2 == 0 else
                 (lambda e, k=k, blk=blk: e.copy(
                    yT[k][:, blk*128:(blk+1)*128],
                    pT[:, (k % 2)*512 + (k // 2 % 4)*128: (k % 2)*512 + (k // 2 % 4)*128 + 128])),
                 reads=("pT",), writes=(f"yT{k}",))

    def shifted(k, c0, width):
        off = 1 if k >= 4 else 0
        return yT[k][:, c0 - off: c0 + width - off]

    def qkprep(dst, dstbuf, pin, pinbuf, colrange, cosT, sinT, cosbuf, scl, sclbuf, kind):
        c0, c1 = colrange
        p.op("act", lambda e: e.activation(qsq[:], pin, AF.Square), reads=(pinbuf,), writes=("qsq",))
        p.op("pe", lambda e: e.matmul(pn[0:2, 0:256], g2[:], qsq[:], start=True, stop=True),
             reads=("qsq", "g2"), writes=("pn",))
        def rowfn(e):
            e.tensor_copy(nrow[0:2, 0:256], pn[0:2, 0:256])
        p.op("dve", rowfn, reads=("pn",), writes=("nrow",))
        p.op("act", lambda e: e.activation(nrow[0:2, 256:512], nrow[0:2, 0:256], AF.Sqrt),
             reads=("nrow",), writes=("nrow",))
        def invfn(e):
            e.tensor_scalar(nrow[0:2, 256:512], nrow[0:2, 256:512], 1e-12, None, ALU.max)
            e.drain()
            e.reciprocal(nrow[0:2, 512:768], nrow[0:2, 256:512])
        p.op("dve", invfn, reads=("nrow",), writes=("nrow",))
        p.op("pe", lambda e: e.matmul(pn[:, 256:512], g2t[:], nrow[0:2, 512:768], start=True, stop=True),
             reads=("nrow", "g2t"), writes=("pn",))
        p.op("dve", lambda e: e.tensor_copy(ninv[:], pn[:, 256:512]), reads=("pn",), writes=("ninv",))
        def qnfn(e):
            e.tensor_tensor(ninv[:], ninv[:], pin, ALU.mult)
            e.drain()
            e.tensor_scalar(qnb[:], ninv[:], scl[:, 0:1], None, ALU.mult)
        p.op("dve", qnfn, reads=("ninv", pinbuf, sclbuf), writes=("qnb", "ninv"))
        p.op("pe", lambda e: e.matmul(pn[:, 0:256], p64[:], qnb[:], start=True, stop=True),
             reads=("qnb", "p64"), writes=("pn",))
        def rotfn(e):
            e.tensor_copy(rotb[:], pn[:, 0:256])
        p.op("dve", rotfn, reads=("pn",), writes=("rotb",))
        def mixfn(e):
            e.tensor_tensor(rotb[:], rotb[:], sinT[:, c0:c1], ALU.mult)
            e.drain()
            e.tensor_tensor(qsq[:], qnb[:], cosT[:, c0:c1], ALU.mult)
            e.drain()
            e.tensor_tensor(dst[:], qsq[:], rotb[:], ALU.add)
        p.op("dve", mixfn, reads=("rotb", "qnb", cosbuf), writes=(dstbuf, "rotb", "qsq"))

    for w in range(NW):
        newwins = [w] if w > 0 else [-1, 0]
        for wv in newwins:
            r = (wv + 1) % 2
            c0 = HALO + wv * 256
            for c in range(ND):
                def mmk(e, c=c, c0=c0):
                    for k in range(ND):
                        e.matmul(pqk[:, 0:256], wqkv[k][:, 1024 + c*128:1024 + (c+1)*128],
                                 shifted(k, c0, 256), start=(k == 0), stop=(k == ND-1))
                p.op("pe", mmk, reads=tuple(f"yT{k}" for k in range(ND)) + tuple(f"wq{k}" for k in range(ND)),
                     writes=("pqk",))
                qkprep(kw[r][c], f"kw{r}_{c}", pqk[:, 0:256], "pqk", (c0, c0 + 256),
                       ckc, cks, "ckc", ksv, "ksv_s", "k")
            for tc in range(2):
                for j in range(2):
                    def mmv(e, tc=tc, j=j, c0=c0):
                        for k in range(ND):
                            e.matmul(pav[tc][:], shifted(k, c0 + tc*128, 128),
                                     wqkv[k][:, 2048 + j*512: 2048 + (j+1)*512],
                                     start=(k == 0), stop=(k == ND-1))
                    p.op("pe", mmv, reads=tuple(f"yT{k}" for k in range(ND)) + tuple(f"wq{k}" for k in range(ND)),
                         writes=(f"pav{tc}",))
                    p.op("dve", lambda e, tc=tc, j=j, r=r: e.tensor_copy(
                            vw[r][tc][:, j*512:(j+1)*512], pav[tc][:]),
                         reads=(f"pav{tc}",), writes=(f"vw{r}_{tc}",))
        cq0 = w * 256
        for c in range(ND):
            def mmq(e, c=c, cq0=cq0):
                for k in range(ND):
                    e.matmul(pqk[:, 0:256], wqkv[k][:, c*128:(c+1)*128],
                             shifted(k, HALO + cq0, 256), start=(k == 0), stop=(k == ND-1))
            p.op("pe", mmq, reads=tuple(f"yT{k}" for k in range(ND)) + tuple(f"wq{k}" for k in range(ND)),
                 writes=("pqk",))
            qkprep(qw[c], f"qw{c}", pqk[:, 0:256], "pqk", (cq0, cq0 + 256),
                   cqc, cqs, "cqc", qsv, "qsv_s", "q")
        rprev = w % 2
        rcur = (w + 1) % 2
        wt = 0 if w == 0 else 1
        if dbg and w == 1:
            for c in range(ND):
                p.op("dve", lambda e, c=c: e.tensor_copy(yin[:, 0:256], qw[c][:]),
                     reads=(f"qw{c}",), writes=("yin",))
                p.op("sp", lambda e, c=c: e.dma_start(out=dbg_q[c*128:(c+1)*128, :], in_=yin[:, 0:256]),
                     reads=("yin",), dma=True)
                p.op("dve", lambda e, c=c: e.tensor_copy(yin[:, 0:256], kw[rcur][c][:]),
                     reads=(f"kw{rcur}_{c}",), writes=("yin",))
                p.op("sp", lambda e, c=c: e.dma_start(out=dbg_k[c*128:(c+1)*128, :], in_=yin[:, 0:256]),
                     reads=("yin",), dma=True)
        for h in range(16):
            c = h // 2
            o = (h % 2) * 64
            for c2 in range(2):
                sb = c2 % 2
                def mmS(e, c=c, o=o, c2=c2, rprev=rprev, rcur=rcur, sb=sb):
                    e.matmul(pS[sb][:, 0:256], qw[c][o:o+64, c2*128:(c2+1)*128],
                             kw[rprev][c][o:o+64, :], start=True, stop=False)
                    e.matmul(pS[sb][:, 256:512], qw[c][o:o+64, c2*128:(c2+1)*128],
                             kw[rcur][c][o:o+64, :], start=False, stop=True)
                p.op("pe", mmS, reads=(f"qw{c}", f"kw{rprev}_{c}", f"kw{rcur}_{c}"),
                     writes=(f"pS{sb}",))
                p.op("dve", lambda e, sb=sb, wt=wt, c2=c2: e.tensor_tensor(
                        smsk[:], pS[sb][:], msks[:, (wt*2 + c2)*512:(wt*2 + c2 + 1)*512], ALU.add),
                     reads=(f"pS{sb}", "msks"), writes=("smsk",))
                def softm(e):
                    e.reduce_max(rowstat[:, 0:1], smsk[:], axis=AX.X)
                    e.drain()
                    e.tensor_scalar(rowstat[:, 1:2], rowstat[:, 0:1], -8.0, None, ALU.mult)
                p.op("dve", softm, reads=("smsk",), writes=("rowstat",))
                p.op("act", lambda e: e.activation(pexp[:], smsk[:], AF.Exp, bias=rowstat[:, 1:2],
                                                   scale=8.0, accum_out=rowstat[:, 2:3]),
                     reads=("smsk", "rowstat"), writes=("pexp", "rowstat"))
                p.op("dve", lambda e: e.reciprocal(rowstat[:, 3:4], rowstat[:, 2:3]),
                     reads=("rowstat",), writes=("rowstat",))
                p.op("dve", lambda e: e.tensor_scalar(pexp[:], pexp[:], rowstat[:, 3:4], None, ALU.mult),
                     reads=("pexp", "rowstat"), writes=("pexp",))
                if dbg and w == 1 and h == 0 and c2 == 0:
                    p.op("dve", lambda e: e.tensor_copy(smsk[:], pexp[:]), reads=("pexp",), writes=("smsk",))
                    p.op("sp", lambda e: e.dma_start(out=dbg_p, in_=smsk[:]), reads=("smsk",), dma=True)
                for kc in range(4):
                    p.op("pe", lambda e, kc=kc: e.transpose(
                            pT[:, kc*128:(kc+1)*128], pexp[:, kc*128:(kc+1)*128], consts["ident"][:]),
                         reads=("pexp", "ident"), writes=("pT",))
                    if kc % 2 == 0:
                        p.op("dve", lambda e, kc=kc: e.tensor_copy(
                                ptsb[:, kc*128:(kc+1)*128], pT[:, kc*128:(kc+1)*128]),
                             reads=("pT",), writes=("ptsb",))
                    else:
                        p.op("act", lambda e, kc=kc: e.copy(
                                ptsb[:, kc*128:(kc+1)*128], pT[:, kc*128:(kc+1)*128]),
                             reads=("pT",), writes=("ptsb",))
                def mmav(e, h=h, rprev=rprev, rcur=rcur, sb=sb):
                    for kc in range(4):
                        vsrc = vw[rprev if kc < 2 else rcur][kc % 2]
                        e.matmul(pav[sb][0:64, 0:128], vsrc[:, h*64:(h+1)*64],
                                 ptsb[:, kc*128:(kc+1)*128], start=(kc == 0), stop=(kc == 3))
                p.op("pe", mmav, reads=("ptsb", f"vw{rprev}_0", f"vw{rprev}_1",
                                        f"vw{rcur}_0", f"vw{rcur}_1"),
                     writes=(f"pav{sb}",))
                p.op("dve", lambda e, h=h, c2=c2, sb=sb: e.tensor_copy(
                        attnw[h // 2][(h % 2)*64:(h % 2)*64+64, c2*128:(c2+1)*128], pav[sb][0:64, 0:128]),
                     reads=(f"pav{sb}",), writes=(f"attnw{h//2}",))
        if dbg and w == 1:
            for c in range(ND):
                p.op("dve", lambda e, c=c: e.tensor_copy(yin[:, 0:256], attnw[c][:]),
                     reads=(f"attnw{c}",), writes=("yin",))
                p.op("sp", lambda e, c=c: e.dma_start(out=dbg_at[c*128:(c+1)*128, :], in_=yin[:, 0:256]),
                     reads=("yin",), dma=True)
        for tc in range(2):
            t0 = w * 256 + tc * 128
            p.op("sp", lambda e, t0=t0: e.dma_start(out=yin[:], in_=y_d[t0:t0+128, :]),
                 writes=("yin",), dma=True)
            for j in range(2):
                def mmo(e, tc=tc, j=j):
                    for c in range(ND):
                        e.matmul(pS[j][:], attnw[c][:, tc*128:(tc+1)*128],
                                 wo[c][:, j*512:(j+1)*512], start=(c == 0), stop=(c == ND-1))
                p.op("pe", mmo, reads=tuple(f"attnw{c}" for c in range(ND)) + tuple(f"wo{c}" for c in range(ND)),
                     writes=(f"pS{j}",))
            def evout(e):
                e.tensor_tensor(outt[:, 0:512], pS[0][:], yin[:, 0:512], ALU.add)
                e.tensor_tensor(outt[:, 512:1024], pS[1][:], yin[:, 512:1024], ALU.add)
            p.op("dve", evout, reads=("pS0", "pS1", "yin"), writes=("outt",))
            p.op("sp", lambda e, t0=t0: e.dma_start(out=yout_d[t0:t0+128, :], in_=outt[:]),
                 reads=("outt",), dma=True)
    p.finish()
    return ctx

KW = HALO + TC
HB = 528
NVB = KW // 128


def l0f_host_consts2(s):
    DH, W = 64, 256
    half = np.arange(0, DH, 2, dtype=np.float64)
    inv_freq = 1.0 / 10000.0 ** (half / DH)
    base_scale = (half + 0.4 * DH) / (1.4 * DH)

    def tables(pos):
        t = pos.astype(np.float64)
        fr = t[:, None] * inv_freq[None, :]
        fr = np.concatenate([fr, fr], -1)
        power = (t - S // 2) / (W // 2)
        sc = base_scale[None, :] ** power[:, None]
        sc = np.concatenate([sc, sc], -1)
        return fr, sc

    def pack(m):
        r = np.concatenate([m.T, m.T], 0)
        return np.ascontiguousarray(r).astype(BF)
    pos_q = s * TC + np.arange(TC)
    pos_k = s * TC + np.arange(-HALO, TC)
    frq, scq = tables(pos_q)
    frk, sck = tables(pos_k)
    cq_cos = pack(np.cos(frq) * scq)
    cq_sin = pack(np.sin(frq) * scq)
    ck_cos = pack(np.cos(frk) / sck)
    ck_sin = pack(np.sin(frk) / sck)
    P64 = np.zeros((128, 128), np.float32)
    for hh in range(2):
        o = hh * 64
        for m in range(64):
            if m < 32:
                P64[o + m + 32, o + m] = -1.0
            else:
                P64[o + m - 32, o + m] = 1.0
    mt = np.zeros((128, 8 * 256), np.float32)
    for wt in range(2):
        for kb in range(4):
            kk = np.arange(128)[:, None]
            qq = np.arange(256)[None, :]
            jr = kb * 128 + kk - 256
            allowed = (jr <= qq) & (jr >= qq - 256)
            if wt == 0 and s == 0:
                allowed = allowed & (jr >= 0)
            mt[:, (wt * 4 + kb) * 256:(wt * 4 + kb + 1) * 256] = np.where(allowed, 0.0, -1e29)
    mq = np.zeros((128, 4 * 512), np.float32)
    for wt in range(2):
        for sbt in range(2):
            qq = np.arange(128)[:, None] + sbt * 128
            j = np.arange(512)[None, :]
            jr = j - 256
            allowed = (jr <= qq) & (jr >= qq - 256)
            if wt == 0 and s == 0:
                allowed = allowed & (jr >= 0)
            mq[:, (wt * 2 + sbt) * 512:(wt * 2 + sbt + 1) * 512] = np.where(allowed, 0.0, -1e29)
    h2AB = np.zeros((128, 4), np.float32)
    h2AB[:, 0] = 1.0
    h2AB[:, 3] = 1.0
    G2 = np.zeros((128, 2), np.float32)
    G2[0:64, 0] = 1.0
    G2[64:128, 1] = 1.0
    return dict(l2f_cqc=cq_cos, l2f_cqs=cq_sin, l2f_ckc=ck_cos, l2f_cks=ck_sin,
                l2f_p64=P64.astype(BF), l2f_maskT=mt.astype(BF), l2f_maskQ=mq,
                l2f_h2=h2AB.astype(BF),
                l2f_g2=G2, l2f_g2t=np.ascontiguousarray(G2.T))


def build_l0f_attn2(dbg=False):
    NW = TC // 256
    ctx = Ctx()
    p = ctx.p
    nc = ctx.nc
    y_d = ctx.din("y", [TC, D], F32)
    ybfh_d = ctx.din("ybfh", [HB + TC, D], BF16)
    wqkv_d = ctx.din("wqkv", [D, 3072], F32)
    wo_d = ctx.din("wo", [1024, D], F32)
    qs_d = ctx.din("qsv", [128, 1], F32)
    ks_d = ctx.din("ksv", [128, 1], F32)
    cqc_d = ctx.din("l2f_cqc", [128, TC], BF16)
    cqs_d = ctx.din("l2f_cqs", [128, TC], BF16)
    ckc_d = ctx.din("l2f_ckc", [128, KW], BF16)
    cks_d = ctx.din("l2f_cks", [128, KW], BF16)
    p64_d = ctx.din("l2f_p64", [128, 128], BF16)
    mq_d = ctx.din("l2f_maskQ", [128, 4 * 512], F32)
    g2_d = ctx.din("l2f_g2", [128, 2], F32)
    g2t_d = ctx.din("l2f_g2t", [2, 128], F32)
    yout_d = ctx.dout("y_new", [TC, D], F32)
    youtb_d = ctx.dout("y_new_bf", [TC, D], BF16)

    kraw = [ctx.sbuf(f"kr{c}", [128, KW], BF16) for c in range(ND)]
    qraw = [ctx.sbuf(f"qr{c}", [128, TC], BF16) for c in range(ND)]
    vtok = ctx.sbuf("vtok", [128, NVB * 1024], BF16)
    mqb = ctx.sbuf("mqb", [128, 4 * 512], F32)
    p64 = ctx.sbuf("p64", [128, 128], BF16)
    g2 = ctx.sbuf("g2", [128, 2], F32)
    g2t = ctx.sbuf("g2t", [2, 128], F32)
    ident_d = ctx.din("c_ident", [128, 128], BF16)
    ident = ctx.sbuf("ident", [128, 128], BF16)
    qsv = ctx.sbuf("qsv_s", [128, 1], F32)
    ksv = ctx.sbuf("ksv_s", [128, 1], F32)
    for sb, dd, nm in [(mqb, mq_d, "mqb"), (p64, p64_d, "p64"),
                       (g2, g2_d, "g2"), (g2t, g2t_d, "g2t"), (ident, ident_d, "ident"),
                       (qsv, qs_d, "qsv_s"), (ksv, ks_d, "ksv_s")]:
        p.op("sp", lambda e, sb=sb, dd=dd: e.dma_start(out=sb[:], in_=dd), writes=(nm,), dma=True)

    sa1 = contextlib.ExitStack()
    yts = [[ctx.sbuf_s(sa1, f"yts{i}_{k}", [128, HB], BF16, is_transpose_dest=True)
            for k in range(ND)] for i in range(2)]
    ws = [ctx.sbuf_s(sa1, f"ws{k}", [128, 2048], BF16) for k in range(ND)]
    pmm = [ctx.psum_s(sa1, f"pmm{i}", [128, 512], F32) for i in range(2)]

    for k in range(ND):
        p.op("pool", lambda e, k=k: e.dma_start(out=ws[k][:], in_=wqkv_d[k*128:(k+1)*128, 1024:3072]),
             writes=(f"ws{k}",), dma=True)

    def load_stripe(i, st):
        for k in range(ND):
            p.op("sp", lambda e, k=k, st=st, i=i: e.dma_start(
                    out=yts[i][k][:], in_=ybfh_d[st*512: st*512 + HB, k*128:(k+1)*128],
                    transpose=True),
                 writes=(f"yts{i}_{k}",), dma=True)

    def ysh(i, k, lc0, n):
        off = 1 if k >= 4 else 0
        return yts[i][k][:, lc0 - off: lc0 + n - off]

    for st in range(5):
        i = st % 2
        load_stripe(i, st)
        t0 = -HALO if st == 0 else st * 512 - 512
        t1 = st * 512
        n = t1 - t0
        lc0 = t0 - st * 512 + HB
        for c in range(ND):
            def mmk(e, c=c, i=i, lc0=lc0, n=n):
                for k in range(ND):
                    e.matmul(pmm[c % 2][:, 0:n], ws[k][:, c*128:(c+1)*128],
                             ysh(i, k, lc0, n), start=(k == 0), stop=(k == ND-1))
            p.op("pe", mmk, reads=tuple(f"yts{i}_{k}" for k in range(ND)) + tuple(f"ws{k}" for k in range(ND)),
                 writes=(f"pmm{c % 2}",))
            eng = "dve" if c % 2 == 0 else "act"
            fn = ((lambda e, c=c, t0=t0, n=n: e.tensor_copy(
                    kraw[c][:, t0+HALO:t0+HALO+n], pmm[c % 2][:, 0:n])) if c % 2 == 0 else
                  (lambda e, c=c, t0=t0, n=n: e.copy(
                    kraw[c][:, t0+HALO:t0+HALO+n], pmm[c % 2][:, 0:n])))
            p.op(eng, fn, reads=(f"pmm{c % 2}",), writes=(f"kr{c}",))
        nvb_ = n // 128
        for vb in range(nvb_):
            tb = t0 + vb * 128
            blk = (tb + HALO) // 128
            for j in range(2):
                def mmv(e, i=i, tb=tb, j=j, st=st):
                    lc = tb - st * 512 + HB
                    for k in range(ND):
                        e.matmul(pmm[j][:], ysh(i, k, lc, 128),
                                 ws[k][:, 1024 + j*512: 1024 + (j+1)*512],
                                 start=(k == 0), stop=(k == ND-1))
                p.op("pe", mmv, reads=tuple(f"yts{i}_{k}" for k in range(ND)) + tuple(f"ws{k}" for k in range(ND)),
                     writes=(f"pmm{j}",))
                eng = "dve" if j == 0 else "act"
                fn = ((lambda e, blk=blk, j=j: e.tensor_copy(
                        vtok[:, blk*1024 + j*512: blk*1024 + (j+1)*512], pmm[j][:])) if j == 0 else
                      (lambda e, blk=blk, j=j: e.copy(
                        vtok[:, blk*1024 + j*512: blk*1024 + (j+1)*512], pmm[j][:])))
                p.op(eng, fn, reads=(f"pmm{j}",), writes=("vtok",))
    for k in range(ND):
        p.op("pool", lambda e, k=k: e.dma_start(out=ws[k][:, 0:1024], in_=wqkv_d[k*128:(k+1)*128, 0:1024]),
             writes=(f"ws{k}",), dma=True)
    for st in range(1, 5):
        i = st % 2
        load_stripe(i, st)
        t0 = st * 512 - 512
        n = 512
        lc0 = HB - 512
        for c in range(ND):
            def mmq(e, c=c, i=i, lc0=lc0, n=n):
                for k in range(ND):
                    e.matmul(pmm[c % 2][:, 0:n], ws[k][:, c*128:(c+1)*128],
                             ysh(i, k, lc0, n), start=(k == 0), stop=(k == ND-1))
            p.op("pe", mmq, reads=tuple(f"yts{i}_{k}" for k in range(ND)) + tuple(f"ws{k}" for k in range(ND)),
                 writes=(f"pmm{c % 2}",))
            eng = "dve" if c % 2 == 0 else "act"
            fn = ((lambda e, c=c, t0=t0, n=n: e.tensor_copy(
                    qraw[c][:, t0:t0+n], pmm[c % 2][:, 0:n])) if c % 2 == 0 else
                  (lambda e, c=c, t0=t0, n=n: e.copy(
                    qraw[c][:, t0:t0+n], pmm[c % 2][:, 0:n])))
            p.op(eng, fn, reads=(f"pmm{c % 2}",), writes=(f"qr{c}",))
    sa1.close()
    p.barrier()

    sa2 = contextlib.ExitStack()
    sqb = [ctx.sbuf_s(sa2, f"sqb{j}", [128, 512], F32) for j in range(2)]
    nsb = [ctx.sbuf_s(sa2, f"nsb{j}", [2, 512], F32) for j in range(2)]
    nrc = [ctx.sbuf_s(sa2, f"nrc{j}", [2, 512], F32) for j in range(2)]
    knb = [ctx.sbuf_s(sa2, f"knb{j}", [128, 512], BF16) for j in range(2)]
    rotsin = [ctx.sbuf_s(sa2, f"rotsin{j}", [128, 512], BF16) for j in range(2)]
    kcosb = [ctx.sbuf_s(sa2, f"kcosb{j}", [128, 512], BF16) for j in range(2)]
    tabc = [ctx.sbuf_s(sa2, f"tabc{j}", [128, 512], BF16) for j in range(2)]
    tabs = [ctx.sbuf_s(sa2, f"tabs{j}", [128, 512], BF16) for j in range(2)]
    pnrm = [ctx.psum_s(sa2, f"pnrm{j}", [128, 512], F32) for j in range(2)]
    pbc = [ctx.psum_s(sa2, f"pbc{j}", [128, 512], F32) for j in range(2)]
    prot = [ctx.psum_s(sa2, f"prot{j}", [128, 512], F32) for j in range(2)]

    def prep(kind):
        buf, bname = (kraw, "kr") if kind == "k" else (qraw, "qr")
        cosd, sind = (ckc_d, cks_d) if kind == "k" else (cqc_d, cqs_d)
        scl, sclb = (ksv, "ksv_s") if kind == "k" else (qsv, "qsv_s")
        width = KW if kind == "k" else TC
        nst = 5 if kind == "k" else 4
        for st in range(nst):
            ti = st % 2
            s0 = st * 512
            n = min(512, width - s0)
            p.op("sp", lambda e, s0=s0, n=n, cosd=cosd, ti=ti: e.dma_start(
                    out=tabc[ti][:, 0:n], in_=cosd[:, s0:s0+n]), writes=(f"tabc{ti}",), dma=True)
            p.op("sp", lambda e, s0=s0, n=n, sind=sind, ti=ti: e.dma_start(
                    out=tabs[ti][:, 0:n], in_=sind[:, s0:s0+n]), writes=(f"tabs{ti}",), dma=True)
            for c in range(ND):
                j = c % 2
                p.op("act", lambda e, c=c, s0=s0, n=n, j=j, buf=buf: e.activation(
                        sqb[j][:, 0:n], buf[c][:, s0:s0+n], AF.Square),
                     reads=(f"{bname}{c}",), writes=(f"sqb{j}",))
                p.op("pe", lambda e, n=n, j=j: e.matmul(
                        pnrm[j][0:2, 0:n], g2[:], sqb[j][:, 0:n], start=True, stop=True),
                     reads=(f"sqb{j}", "g2"), writes=(f"pnrm{j}",))
                p.op("act", lambda e, n=n, j=j: e.activation(nsb[j][:, 0:n], pnrm[j][0:2, 0:n], AF.Sqrt),
                     reads=(f"pnrm{j}",), writes=(f"nsb{j}",))
                def invfn(e, n=n, j=j):
                    e.tensor_scalar(nsb[j][:, 0:n], nsb[j][:, 0:n], 1e-12, None, ALU.max)
                    e.drain()
                    e.reciprocal(nrc[j][:, 0:n], nsb[j][:, 0:n])
                p.op("dve", invfn, reads=(f"nsb{j}",), writes=(f"nrc{j}", f"nsb{j}"))
                p.op("pe", lambda e, n=n, j=j: e.matmul(pbc[j][:, 0:n], g2t[:], nrc[j][:, 0:n],
                                                        start=True, stop=True),
                     reads=(f"nrc{j}", "g2t"), writes=(f"pbc{j}",))
                p.op("dve", lambda e, c=c, s0=s0, n=n, j=j, buf=buf, scl=scl: e.scalar_tensor_tensor(
                        knb[j][:, 0:n], pbc[j][:, 0:n], scl[:, 0:1], buf[c][:, s0:s0+n],
                        ALU.mult, ALU.mult),
                     reads=(f"pbc{j}", sclb, f"{bname}{c}"), writes=(f"knb{j}",))
                p.op("pe", lambda e, n=n, j=j: e.matmul(prot[j][:, 0:n], p64[:], knb[j][:, 0:n],
                                                        start=True, stop=True),
                     reads=(f"knb{j}", "p64"), writes=(f"prot{j}",))
                p.op("dve", lambda e, n=n, j=j, ti=ti: e.tensor_tensor(
                        rotsin[j][:, 0:n], prot[j][:, 0:n], tabs[ti][:, 0:n], ALU.mult),
                     reads=(f"prot{j}", f"tabs{ti}"), writes=(f"rotsin{j}",))
                p.op("dve", lambda e, n=n, j=j, ti=ti: e.tensor_tensor(
                        kcosb[j][:, 0:n], knb[j][:, 0:n], tabc[ti][:, 0:n], ALU.mult),
                     reads=(f"knb{j}", f"tabc{ti}"), writes=(f"kcosb{j}",))
                p.op("dve", lambda e, c=c, s0=s0, n=n, j=j, buf=buf: e.tensor_tensor(
                        buf[c][:, s0:s0+n], kcosb[j][:, 0:n], rotsin[j][:, 0:n], ALU.add),
                     reads=(f"kcosb{j}", f"rotsin{j}"), writes=(f"{bname}{c}",))

    prep("k")
    prep("q")
    sa2.close()
    p.barrier()

    wo = [ctx.sbuf(f"wo{c}", [128, D], BF16) for c in range(ND)]
    NDEP = 2
    PT = [ctx.sbuf(f"PT{i}", [128, 1024], BF16) for i in range(NDEP)]
    pq2 = [ctx.sbuf(f"pq2{i}", [128, 1024], BF16) for i in range(NDEP)]
    sm = [ctx.sbuf(f"sm{i}", [128, 1024], F32) for i in range(NDEP)]
    attnw = [[ctx.sbuf(f"aw{wp}_{c}", [128, 256], BF16) for c in range(ND)] for wp in range(2)]
    mxc = [ctx.sbuf(f"mxc{i}", [128, 2], F32) for i in range(NDEP)]
    mxn = [ctx.sbuf(f"mxn{i}", [128, 2], F32) for i in range(NDEP)]
    rsc = [ctx.sbuf(f"rsc{i}", [128, 2], F32) for i in range(NDEP)]
    rsr = [ctx.sbuf(f"rsr{i}", [128, 2], F32) for i in range(NDEP)]
    yin = ctx.sbuf("yin", [128, D], F32)
    outt = ctx.sbuf("outt", [128, D], F32)
    outtb = ctx.sbuf("outtb", [128, D], BF16)
    pst = [ctx.psum(f"pst{b}", [128, 512], F32) for b in range(4)]
    pT2 = [ctx.psum(f"pT2{i}", [128, 1024], BF16) for i in range(2)]
    pav = ctx.psum("pav", [128, 512], F32)
    pWo = ctx.psum("pWo", [128, 512], F32)
    for c in range(ND):
        p.op("pool", lambda e, c=c: e.dma_start(out=wo[c][:], in_=wo_d[c*128:(c+1)*128, :]),
             writes=(f"wo{c}",), dma=True)
    if dbg:
        dbg_k = ctx.dout("dbg_k", [128, KW], F32)
        dbg_q = ctx.dout("dbg_q", [128, TC], F32)
        dbg_v = ctx.dout("dbg_v", [128, NVB * 1024], F32)
        dcp = ctx.sbuf("dcp", [128, 2560], F32)
        p.op("dve", lambda e: e.tensor_copy(dcp[:, 0:KW], kraw[0][:]), reads=("kr0",), writes=("dcp",))
        p.op("sp", lambda e: e.dma_start(out=dbg_k, in_=dcp[:, 0:KW]), reads=("dcp",), dma=True)
        p.op("dve", lambda e: e.tensor_copy(dcp[:, 0:TC], qraw[0][:]), reads=("qr0",), writes=("dcp",))
        p.op("sp", lambda e: e.dma_start(out=dbg_q, in_=dcp[:, 0:TC]), reads=("dcp",), dma=True)
        for vb in range(NVB):
            p.op("dve", lambda e, vb=vb: e.tensor_copy(dcp[:, 0:1024], vtok[:, vb*1024:(vb+1)*1024]),
                 reads=("vtok",), writes=("dcp",))
            p.op("sp", lambda e, vb=vb: e.dma_start(out=dbg_v[:, vb*1024:(vb+1)*1024], in_=dcp[:, 0:1024]),
                 reads=("dcp",), dma=True)

    its = [(w, c, o2) for w in range(NW) for c in range(ND) for o2 in range(2)]

    def emitA(idx):
        w, c, o2 = its[idx]
        i = idx % NDEP
        ip = idx % 2
        kc0 = 128 + w * 256
        qc0 = w * 256
        def mmsq(e, c=c, o2=o2, ip=ip, kc0=kc0, qc0=qc0):
            for sbt in range(2):
                e.matmul(pst[ip*2 + sbt][:],
                         qraw[c][o2*64:(o2+1)*64, qc0 + sbt*128: qc0 + (sbt+1)*128],
                         kraw[c][o2*64:(o2+1)*64, kc0: kc0 + 512],
                         start=True, stop=True)
        p.op("pe", mmsq, reads=(f"kr{c}", f"qr{c}"),
             writes=(f"pst{ip*2}", f"pst{ip*2+1}"))

    def emitB(idx):
        w, c, o2 = its[idx]
        i = idx % NDEP
        ip = idx % 2
        wt = 0 if w == 0 else 1
        for sbt in range(2):
            p.op("dve", lambda e, sbt=sbt, i=i, ip=ip, wt=wt: e.tensor_tensor(
                    sm[i][:, sbt*512:(sbt+1)*512], pst[ip*2+sbt][:],
                    mqb[:, (wt*2 + sbt)*512:(wt*2 + sbt + 1)*512], ALU.add),
                 reads=(f"pst{ip*2+sbt}", "mqb"), writes=(f"sm{i}",))
            p.op("dve", lambda e, sbt=sbt, i=i: e.reduce_max(
                    mxc[i][:, sbt:sbt+1], sm[i][:, sbt*512:(sbt+1)*512], axis=AX.X),
                 reads=(f"sm{i}",), writes=(f"mxc{i}",))
        p.op("dve", lambda e, i=i: e.tensor_scalar(mxn[i][:], mxc[i][:], -8.0, None, ALU.mult),
             reads=(f"mxc{i}",), writes=(f"mxn{i}",))
        for sbt in range(2):
            p.op("act", lambda e, sbt=sbt, i=i: e.activation(
                    pq2[i][:, sbt*512:(sbt+1)*512], sm[i][:, sbt*512:(sbt+1)*512],
                    AF.Exp, bias=mxn[i][:, sbt:sbt+1], scale=8.0,
                    accum_out=rsc[i][:, sbt:sbt+1]),
                 reads=(f"sm{i}", f"mxn{i}"), writes=(f"pq2{i}", f"rsc{i}"))
        p.op("dve", lambda e, i=i: e.reciprocal(rsr[i][:], rsc[i][:]),
             reads=(f"rsc{i}",), writes=(f"rsr{i}",))
        for sbt in range(2):
            p.op("act", lambda e, sbt=sbt, i=i: e.activation(
                    pq2[i][:, sbt*512:(sbt+1)*512], pq2[i][:, sbt*512:(sbt+1)*512],
                    AF.Identity, scale=rsr[i][:, sbt:sbt+1]),
                 reads=(f"pq2{i}", f"rsr{i}"), writes=(f"pq2{i}",))

    def emitC(idx):
        w, c, o2 = its[idx]
        i = idx % NDEP
        ip = idx % 2
        for sbt in range(2):
            for kb in range(4):
                p.op("pe", lambda e, sbt=sbt, i=i, ip=ip, kb=kb: e.transpose(
                        pT2[ip][:, sbt*512 + kb*128: sbt*512 + (kb+1)*128],
                        pq2[i][:, sbt*512 + kb*128: sbt*512 + (kb+1)*128],
                        ident[:]),
                     reads=(f"pq2{i}", "ident"), writes=(f"pT2{ip}",))
            dst_v = PT[i][:].rearrange("p (k s c) -> p s k c", k=4, s=2)[:, sbt]
            src_v = pT2[ip][:, sbt*512:(sbt+1)*512].rearrange("p (k c) -> p k c", k=4)
            eng = "dve" if sbt == 0 else "act"
            fn = ((lambda e, d_=dst_v, s_=src_v: e.tensor_copy(d_, s_)) if sbt == 0 else
                  (lambda e, d_=dst_v, s_=src_v: e.copy(d_, s_)))
            p.op(eng, fn, reads=(f"pT2{ip}",), writes=(f"PT{i}",))
        h = 2 * c + o2
        def mmav(e, i=i, w=w, o2=o2, h=h):
            for kb in range(4):
                blk = 1 + 2 * w + kb
                e.matmul(pav[0:64, o2*256:(o2+1)*256],
                         vtok[:, blk*1024 + h*64: blk*1024 + (h+1)*64],
                         PT[i][:, kb*256:(kb+1)*256],
                         start=(kb == 0), stop=(kb == 3))
        p.op("pe", mmav, reads=(f"PT{i}", "vtok"), writes=("pav",))
        if o2 == 1:
            wp = w % 2
            p.op("dve", lambda e, wp=wp, c=c: e.tensor_copy(
                    attnw[wp][c][0:64, :], pav[0:64, 0:256]),
                 reads=("pav",), writes=(f"aw{wp}_{c}",))
            p.op("act", lambda e, wp=wp, c=c: e.copy(
                    attnw[wp][c][64:128, :], pav[0:64, 256:512]),
                 reads=("pav",), writes=(f"aw{wp}_{c}",))

    def emitWo(w):
        wp = w % 2
        for tc in range(2):
            t0 = w * 256 + tc * 128
            p.op("sp", lambda e, t0=t0: e.dma_start(out=yin[:], in_=y_d[t0:t0+128, :]),
                 writes=("yin",), dma=True)
            for j in range(2):
                def mmo(e, wp=wp, tc=tc, j=j):
                    for c in range(ND):
                        e.matmul(pWo[:], attnw[wp][c][:, tc*128:(tc+1)*128],
                                 wo[c][:, j*512:(j+1)*512], start=(c == 0), stop=(c == ND-1))
                p.op("pe", mmo, reads=tuple(f"aw{wp}_{c}" for c in range(ND)) + tuple(f"wo{c}" for c in range(ND)),
                     writes=("pWo",))
                p.op("dve", lambda e, j=j: e.tensor_tensor(
                        outt[:, j*512:(j+1)*512], pWo[:], yin[:, j*512:(j+1)*512], ALU.add),
                     reads=("pWo", "yin"), writes=("outt",))
            p.op("act", lambda e: e.copy(outtb[:], outt[:]), reads=("outt",), writes=("outtb",))
            p.op("sp", lambda e, t0=t0: e.dma_start(out=yout_d[t0:t0+128, :], in_=outt[:]),
                 reads=("outt",), dma=True)
            p.op("pool", lambda e, t0=t0: e.dma_start(out=youtb_d[t0:t0+128, :], in_=outtb[:]),
                 reads=("outtb",), dma=True)

    NIT = len(its)
    for idx in range(NIT + 2):
        if idx < NIT:
            emitA(idx)
        if 1 <= idx <= NIT:
            emitB(idx - 1)
        if 2 <= idx <= NIT + 1:
            emitC(idx - 2)
            w, c, o2 = its[idx - 2]
            if c == ND - 1 and o2 == 1:
                emitWo(w)
    p.finish()
    return ctx


def build_l0f_ffn():
    HL = 128
    YW = HL + TC
    F1, F2 = 5460, 2730
    NJ = 22
    TW = 256
    ctx = Ctx()
    p = ctx.p
    y_d = ctx.din("y", [TC, D], F32)
    ybfh1_d = ctx.din("ybfh1", [HL + TC, D], BF16)
    w1_d = ctx.din("w1p", [44, D, 128], F32)
    b1_d = ctx.din("b1p", [128, 44], F32)
    w2_d = ctx.din("w2", [F2, D], F32)
    b2_d = ctx.din("b2row", [1, D], F32)
    g_d = ctx.din("lng", [128, ND], F32)
    b_d = ctx.din("lnb", [128, ND], F32)
    onescol_d = ctx.din("c_ones_col", [128, 1], F32)
    yout_d = ctx.dout("y_new", [TC, D], F32)
    youtb_d = ctx.dout("y_new_bf", [TC, D], BF16)
    consts = load_consts(ctx)

    w1b = [[ctx.sbuf(f"w1b{a}_{j}", [128, ND * 128], BF16) for j in range(NJ)] for a in range(2)]
    w2b = [ctx.sbuf(f"w2b{j}", [128, D], BF16) for j in range(NJ)]
    b1s = ctx.sbuf("b1s", [128, 44], F32)
    gs = ctx.sbuf("gs_s", [128, ND], F32)
    bs = ctx.sbuf("bs_s", [128, ND], F32)
    b2r = ctx.sbuf("b2r", [1, D], F32)
    b2rb = ctx.sbuf("b2rb", [1, D], BF16)
    onescol = ctx.sbuf("onescol", [128, 1], F32)
    onescolb = ctx.sbuf("onescolb", [128, 1], BF16)
    onesrb = ctx.sbuf("onesrb", [1, 128], BF16)
    yT = [ctx.sbuf(f"yT{k}", [128, YW], BF16, is_transpose_dest=True) for k in range(ND)]
    yin = ctx.sbuf("yin", [128, D], F32)
    xn = [[ctx.sbuf(f"xn{ip}_{k}", [128, TW], BF16) for k in range(ND)] for ip in range(2)]
    vsq = ctx.sbuf("vsq", [128, TW], BF16)
    rowscr = ctx.sbuf("rowscr", [1, 7 * TW], F32)
    musb = ctx.sbuf("musb", [128, TW], F32)
    rssb = ctx.sbuf("rssb", [128, TW], F32)
    epsT1 = ctx.sbuf("epsT1", [1, 1], F32)
    gelg = [ctx.sbuf(f"gelg{jp}", [128, TW], BF16) for jp in range(2)]
    hj = [ctx.sbuf(f"hj{jp}", [128, TW], BF16) for jp in range(2)]
    outt = ctx.sbuf("outt", [128, D], F32)
    outtb = ctx.sbuf("outtb", [128, D], BF16)
    pa = [ctx.psum(f"pa{jp}", [128, 512], F32) for jp in range(2)]
    pg = [ctx.psum(f"pg{jp}", [128, 512], F32) for jp in range(2)]
    po = [ctx.psum(f"po{j}", [128, 512], F32) for j in range(4)]

    for a in range(2):
        for j in range(NJ):
            def ldw1(e, a=a, j=j):
                src = bass.AP(w1_d.tensor, (a * NJ + j) * D * 128,
                              [[128, 128], [128 * 128, ND], [1, 128]])
                return e.dma_start(out=w1b[a][j][:].rearrange("p (f c) -> p f c", f=ND), in_=src)
            p.op("pool", ldw1, writes=(f"w1b{a}_{j}",), dma=True)
    for j in range(NJ):
        rows = min(128, F2 - j * 128)
        p.op("pool", lambda e, j=j, rows=rows: e.dma_start(
                out=w2b[j][0:rows, :], in_=w2_d[j*128:j*128+rows, :]),
             writes=(f"w2b{j}",), dma=True)
    p.op("pool", lambda e: e.dma_start(out=b1s[:], in_=b1_d), writes=("b1s",), dma=True)
    p.op("pool", lambda e: e.dma_start(out=gs[:], in_=g_d), writes=("lnp",), dma=True)
    p.op("pool", lambda e: e.dma_start(out=bs[:], in_=b_d), writes=("lnp",), dma=True)
    p.op("pool", lambda e: e.dma_start(out=b2r[:], in_=b2_d), writes=("b2r",), dma=True)
    p.op("sp", lambda e: e.dma_start(out=onescol[:], in_=onescol_d), writes=("onescol",), dma=True)
    p.op("dve", lambda e: e.tensor_copy(b2rb[:], b2r[:]), reads=("b2r",), writes=("b2rb",))
    p.op("dve", lambda e: e.tensor_copy(onescolb[:], onescol[:]), reads=("onescol",), writes=("onescolb",))
    p.op("dve", lambda e: e.tensor_copy(onesrb[:], consts["onesr"][:]), reads=("onesr",), writes=("onesrb",))
    p.op("dve", lambda e: e.memset(epsT1[:], 1e-5), writes=("epsT1",))

    for k in range(ND):
        p.op("sp", lambda e, k=k: e.dma_start(
                out=yT[k][:], in_=ybfh1_d[:, k*128:(k+1)*128], transpose=True),
             writes=(f"yT{k}",), dma=True)

    def shifted(k, c0, width):
        off = 1 if k >= 4 else 0
        return yT[k][:, c0 - off: c0 + width - off]

    for t in range(TC // TW):
        c0 = HL + t * TW
        for k in range(ND):
            p.op("act", lambda e, k=k, c0=c0: e.activation(vsq[:], shifted(k, c0, TW), AF.Square),
                 reads=(f"yT{k}",), writes=("vsq",))
            def mm_s(e, k=k, c0=c0):
                e.matmul(pa[0][0:1, 0:TW], onescolb[:, 0:1], shifted(k, c0, TW),
                         start=(k == 0), stop=(k == ND-1))
            p.op("pe", mm_s, reads=(f"yT{k}", "onescolb"), writes=("pa0",))
            def mm_q(e, k=k):
                e.matmul(pg[0][0:1, 0:TW], onescolb[:, 0:1], vsq[:],
                         start=(k == 0), stop=(k == ND-1))
            p.op("pe", mm_q, reads=("vsq", "onescolb"), writes=("pg0",))
        def rows_fn(e):
            e.tensor_scalar(rowscr[0:1, 0:TW], pa[0][0:1, 0:TW], 1.0 / D, None, ALU.mult)
            e.tensor_scalar(rowscr[0:1, TW:2*TW], pg[0][0:1, 0:TW], 1.0 / D, None, ALU.mult)
            e.drain()
            e.tensor_tensor(rowscr[0:1, 2*TW:3*TW], rowscr[0:1, 0:TW], rowscr[0:1, 0:TW], ALU.mult)
            e.drain()
            e.tensor_tensor(rowscr[0:1, 3*TW:4*TW], rowscr[0:1, TW:2*TW], rowscr[0:1, 2*TW:3*TW], ALU.subtract)
        p.op("dve", rows_fn, reads=("pa0", "pg0"), writes=("rowscr",))
        p.op("act", lambda e: e.activation(rowscr[0:1, 4*TW:5*TW], rowscr[0:1, 3*TW:4*TW],
                                           AF.Sqrt, bias=epsT1[0:1, 0:1]),
             reads=("rowscr", "epsT1"), writes=("rowscr",))
        def newton_fn(e):
            e.reciprocal(rowscr[0:1, 5*TW:6*TW], rowscr[0:1, 4*TW:5*TW])
            e.tensor_scalar(rowscr[0:1, 3*TW:4*TW], rowscr[0:1, 3*TW:4*TW], 1e-5, None, ALU.add)
            e.drain()
            e.tensor_tensor(rowscr[0:1, 6*TW:7*TW], rowscr[0:1, 5*TW:6*TW], rowscr[0:1, 5*TW:6*TW], ALU.mult)
            e.drain()
            e.tensor_tensor(rowscr[0:1, 6*TW:7*TW], rowscr[0:1, 6*TW:7*TW], rowscr[0:1, 3*TW:4*TW], ALU.mult)
            e.drain()
            e.tensor_scalar(rowscr[0:1, 6*TW:7*TW], rowscr[0:1, 6*TW:7*TW], -0.5, 1.5, ALU.mult, ALU.add)
            e.drain()
            e.tensor_tensor(rowscr[0:1, 5*TW:6*TW], rowscr[0:1, 5*TW:6*TW], rowscr[0:1, 6*TW:7*TW], ALU.mult)
        p.op("dve", newton_fn, reads=("rowscr",), writes=("rowscr",))
        p.op("pe", lambda e: e.matmul(pa[0][:, 0:TW], consts["onesr"][:], rowscr[0:1, 0:TW],
                                      start=True, stop=True),
             reads=("rowscr", "onesr"), writes=("pa0",))
        p.op("pe", lambda e: e.matmul(pg[0][:, 0:TW], consts["onesr"][:], rowscr[0:1, 5*TW:6*TW],
                                      start=True, stop=True),
             reads=("rowscr", "onesr"), writes=("pg0",))
        p.op("dve", lambda e: e.tensor_copy(musb[:], pa[0][:, 0:TW]), reads=("pa0",), writes=("musb",))
        p.op("dve", lambda e: e.tensor_copy(rssb[:], pg[0][:, 0:TW]), reads=("pg0",), writes=("rssb",))
        ip = t % 2
        for k in range(ND):
            def xnfn(e, k=k, c0=c0, ip=ip):
                e.tensor_tensor(vsq[:], shifted(k, c0, TW), musb[:], ALU.subtract)
                e.drain()
                e.tensor_tensor(vsq[:], vsq[:], rssb[:], ALU.mult)
                e.drain()
                e.tensor_scalar(xn[ip][k][:], vsq[:], gs[:, k:k+1], bs[:, k:k+1], ALU.mult, ALU.add)
            p.op("dve", xnfn, reads=(f"yT{k}", "musb", "rssb", "lnp"), writes=(f"xn{ip}_{k}", "vsq"))
        def emitGU(j, ip):
            jp = j % 2
            def mm_a(e, j=j, jp=jp, ip=ip):
                for k in range(ND):
                    e.matmul(pa[jp][:, 0:TW], w1b[0][j][:, k*128:(k+1)*128], xn[ip][k][:],
                             start=(k == 0), stop=(k == ND-1))
            p.op("pe", mm_a, reads=tuple(f"xn{ip}_{k}" for k in range(ND)) + (f"w1b0_{j}",),
                 writes=(f"pa{jp}",))
            def mm_gt(e, j=j, jp=jp, ip=ip):
                for k in range(ND):
                    e.matmul(pg[jp][:, 0:TW], w1b[1][j][:, k*128:(k+1)*128], xn[ip][k][:],
                             start=(k == 0), stop=(k == ND-1))
            p.op("pe", mm_gt, reads=tuple(f"xn{ip}_{k}" for k in range(ND)) + (f"w1b1_{j}",),
                 writes=(f"pg{jp}",))
            p.op("act", lambda e, j=j, jp=jp: e.activation(gelg[jp][:], pg[jp][:, 0:TW], AF.Gelu,
                                                           bias=b1s[:, 22+j:23+j]),
                 reads=(f"pg{jp}", "b1s"), writes=(f"gelg{jp}",))
            p.op("dve", lambda e, j=j, jp=jp: e.scalar_tensor_tensor(
                    hj[jp][:], pa[jp][:, 0:TW], b1s[:, j:j+1], gelg[jp][:], ALU.add, ALU.mult),
                 reads=(f"pa{jp}", "b1s", f"gelg{jp}"), writes=(f"hj{jp}",))
        def emitD(j):
            jp = j % 2
            rows = min(128, F2 - j * 128)
            def mm_d(e, j=j, jp=jp, rows=rows):
                for tc in range(2):
                    for jj in range(2):
                        e.matmul(po[tc*2+jj][:], hj[jp][0:rows, tc*128:(tc+1)*128],
                                 w2b[j][0:rows, jj*512:(jj+1)*512],
                                 start=(j == 0), stop=False)
            p.op("pe", mm_d, reads=(f"hj{jp}", f"w2b{j}"), writes=tuple(f"po{j2}" for j2 in range(4)))
        for j in range(NJ):
            emitGU(j, ip)
            if j > 0:
                emitD(j - 1)
        emitD(NJ - 1)
        def mm_b2(e):
            for tc in range(2):
                for jj in range(2):
                    e.matmul(po[tc*2+jj][:], onesrb[:, 0:128], b2rb[0:1, jj*512:(jj+1)*512],
                             start=False, stop=True)
        p.op("pe", mm_b2, reads=("onesrb", "b2rb"), writes=tuple(f"po{j2}" for j2 in range(4)))
        for tc in range(2):
            t0 = t * TW + tc * 128
            p.op("sp", lambda e, t0=t0: e.dma_start(out=yin[:], in_=y_d[t0:t0+128, :]),
                 writes=("yin",), dma=True)
            def evout(e, tc=tc):
                e.tensor_tensor(outt[:, 0:512], po[tc*2][:], yin[:, 0:512], ALU.add)
                e.drain()
                e.tensor_tensor(outt[:, 512:1024], po[tc*2+1][:], yin[:, 512:1024], ALU.add)
            p.op("dve", evout, reads=(f"po{tc*2}", f"po{tc*2+1}", "yin"), writes=("outt",))
            p.op("act", lambda e: e.copy(outtb[:], outt[:]), reads=("outt",), writes=("outtb",))
            p.op("sp", lambda e, t0=t0: e.dma_start(out=yout_d[t0:t0+128, :], in_=outt[:]),
                 reads=("outt",), dma=True)
            p.op("pool", lambda e, t0=t0: e.dma_start(out=youtb_d[t0:t0+128, :], in_=outtb[:]),
                 reads=("outtb",), dma=True)
    p.finish()
    return ctx


def build_l1_qkv():
    M = 256
    ctx = Ctx()
    p = ctx.p
    xnT_d = ctx.din("xnT", [D, TC], BF16)
    wq_d = ctx.din("wqT", [D, M], F32)
    wk_d = ctx.din("wkT", [D, M], F32)
    wv_d = ctx.din("wvT", [D, M], F32)
    qf_d = ctx.dout("q_feat", [M, TC], BF16)
    kt_d = ctx.dout("k_tok", [TC, M], BF16)
    vt_d = ctx.dout("v_tok", [TC, M], BF16)
    wq = [ctx.sbuf(f"wq{k}", [128, M], BF16) for k in range(ND)]
    wk = [ctx.sbuf(f"wk{k}", [128, M], BF16) for k in range(ND)]
    wv = [ctx.sbuf(f"wv{k}", [128, M], BF16) for k in range(ND)]
    xt = [ctx.sbuf(f"xt{i}", [128, ND * 512], BF16) for i in range(2)]
    ev = ctx.sbuf("ev", [128, 512], BF16)
    ps = ctx.psum("ps", [128, 512], F32)
    pt = ctx.psum("pt", [128, 512], F32)
    for k in range(ND):
        for w_, d_, nm in [(wq, wq_d, "wq"), (wk, wk_d, "wk"), (wv, wv_d, "wv")]:
            p.op("pool", lambda e, w_=w_, d_=d_, k=k: e.dma_start(
                    out=w_[k][:], in_=d_[k*128:(k+1)*128, :]), writes=(f"{nm}{k}",), dma=True)
    for t in range(TC // 512):
        i = t % 2
        load_xnT_tile(ctx, p, xnT_d, xt[i], f"xt{i}", t * 512, 512)
        for m in range(2):
            def mmq(e, m=m, i=i):
                for k in range(ND):
                    e.matmul(ps[:], wq[k][:, m*128:(m+1)*128], xt[i][:, k*512:(k+1)*512],
                             start=(k == 0), stop=(k == ND-1))
            p.op("pe", mmq, reads=(f"xt{i}",) + tuple(f"wq{k}" for k in range(ND)), writes=("ps",))
            p.op("dve", lambda e: e.tensor_copy(ev[:], ps[:]), reads=("ps",), writes=("ev",))
            p.op("sp", lambda e, m=m, t=t: e.dma_start(
                    out=qf_d[m*128:(m+1)*128, t*512:(t+1)*512], in_=ev[:]),
                 reads=("ev",), dma=True)
        for tc in range(4):
            for w_, dd, nm in [(wk, kt_d, "wk"), (wv, vt_d, "wv")]:
                def mmt(e, tc=tc, i=i, w_=w_):
                    for k in range(ND):
                        e.matmul(pt[:, 0:M], xt[i][:, k*512 + tc*128: k*512 + (tc+1)*128],
                                 w_[k][:], start=(k == 0), stop=(k == ND-1))
                p.op("pe", mmt, reads=(f"xt{i}",) + tuple(f"{nm}{k}" for k in range(ND)),
                     writes=("pt",))
                p.op("dve", lambda e: e.tensor_copy(ev[:, 0:M], pt[:, 0:M]), reads=("pt",), writes=("ev",))
                p.op("sp", lambda e, dd=dd, t=t, tc=tc: e.dma_start(
                        out=dd[t*512 + tc*128: t*512 + (tc+1)*128, :], in_=ev[:, 0:M]),
                     reads=("ev",), dma=True)
    p.finish()
    return ctx


def l1_host_consts():
    DEC = 0.99
    u = np.arange(128)
    wdec = (DEC ** (127 - u) / B).astype(np.float32).reshape(128, 1)
    t = np.arange(128)
    wmask = np.where(t[None, :] > u[:, None], DEC ** (t[None, :] - 1 - u[:, None]) / B, 0.0).astype(np.float32)
    qdec = np.tile((DEC ** t).astype(np.float32).reshape(1, 128), (128, 1))
    return dict(l1_wdec=wdec, l1_wmask=wmask, l1_qdec=qdec)


def build_l1_attn():
    M = 256
    NCH = 32
    ctx = Ctx()
    p = ctx.p
    qf_d = ctx.din("q_feat", [M, S], BF16)
    kt_d = ctx.din("k_all", [B * S, M], BF16)
    vt_d = ctx.din("v_all", [B * S, M], BF16)
    m0_d = ctx.din("m0", [M, M], F32)
    wo_d = ctx.din("woT", [M, D], F32)
    gate_d = ctx.din("gate", [1, 1], F32)
    wdec_d = ctx.din("l1_wdec", [128, 1], F32)
    wmask_d = ctx.din("l1_wmask", [128, 128], F32)
    qdec_d = ctx.din("l1_qdec", [128, 128], F32)
    out_d = ctx.dout("out", [S, D], F32)
    consts = load_consts(ctx)

    qf = [ctx.sbuf(f"qf{n}", [128, S], BF16) for n in range(2)]
    wo = [ctx.sbuf(f"wo{m}", [128, D], BF16) for m in range(2)]
    MT = [ctx.sbuf(f"MT{n}", [128, M], F32) for n in range(2)]
    MTb = [ctx.sbuf(f"MTb{n}", [128, M], BF16) for n in range(2)]
    kt = [ctx.sbuf(f"kt{b}", [128, M], BF16) for b in range(B)]
    vt = [ctx.sbuf(f"vt{b}", [128, M], BF16) for b in range(B)]
    kw_ = [ctx.sbuf(f"kwd{b}", [128, M], BF16) for b in range(B)]
    wdec = ctx.sbuf("wdec", [128, 1], F32)
    wmask = ctx.sbuf("wmask", [128, 128], F32)
    qdec = ctx.sbuf("qdec", [128, 128], F32)
    kfT = [ctx.sbuf(f"kfT{n}", [128, 128], BF16) for n in range(2)]
    awt = ctx.sbuf("awt", [128, 128], BF16)
    qd = [ctx.sbuf(f"qd{n}", [128, 128], BF16) for n in range(2)]
    rfb = [ctx.sbuf(f"rfb{m}", [128, 128], BF16) for m in range(2)]
    gv = ctx.sbuf("gv", [128, 1], F32)
    g1 = ctx.sbuf("g1", [1, 1], F32)
    outt = ctx.sbuf("outt", [128, D], F32)
    pS = ctx.psum("pS", [128, 512], F32)
    pA = ctx.psum("pA", [128, 512], F32)
    pR = [ctx.psum(f"pR{m}", [128, 512], F32) for m in range(2)]
    pO = [ctx.psum(f"pO{j}", [128, 512], F32) for j in range(2)]
    pT = ctx.psum("pT", [128, 1024], BF16)

    p.op("pool", lambda e: e.dma_start(out=wdec[:], in_=wdec_d), writes=("wdec",), dma=True)
    p.op("pool", lambda e: e.dma_start(out=wmask[:], in_=wmask_d), writes=("wmask",), dma=True)
    p.op("pool", lambda e: e.dma_start(out=qdec[:], in_=qdec_d), writes=("qdec",), dma=True)
    p.op("pool", lambda e: e.dma_start(out=g1[:], in_=gate_d), writes=("g1",), dma=True)
    for n in range(2):
        p.op("pool", lambda e, n=n: e.dma_start(out=qf[n][:], in_=qf_d[n*128:(n+1)*128, :]),
             writes=(f"qf{n}",), dma=True)
        p.op("pool", lambda e, n=n: e.dma_start(out=MT[n][:], in_=m0_d[n*128:(n+1)*128, :]),
             writes=(f"MT{n}",), dma=True)
        p.op("pool", lambda e, n=n: e.dma_start(out=wo[n][:], in_=wo_d[n*128:(n+1)*128, :]),
             writes=(f"wo{n}",), dma=True)
    p.op("act", lambda e: e.activation(g1[:], g1[:], AF.Sigmoid), reads=("g1",), writes=("g1",))
    p.op("pe", lambda e: e.matmul(pS[:, 0:1], consts["onesr"][:], g1[:], start=True, stop=True),
         reads=("g1", "onesr"), writes=("pS",))
    p.op("dve", lambda e: e.tensor_copy(gv[:], pS[:, 0:1]), reads=("pS",), writes=("gv",))

    for c in range(NCH):
        seqc = c
        for b_ in range(B):
            r0 = b_ * S + seqc * 128
            p.op("sp", lambda e, b_=b_, r0=r0: e.dma_start(out=kt[b_][:], in_=kt_d[r0:r0+128, :]),
                 writes=(f"kt{b_}",), dma=True)
            p.op("sp", lambda e, b_=b_, r0=r0: e.dma_start(out=vt[b_][:], in_=vt_d[r0:r0+128, :]),
                 writes=(f"vt{b_}",), dma=True)
        own = True
        if own:
            tloc = seqc * 128
            for n in range(2):
                p.op("dve", lambda e, n=n: e.tensor_copy(MTb[n][:], MT[n][:]),
                     reads=(f"MT{n}",), writes=(f"MTb{n}",))
                p.op("dve", lambda e, n=n, tloc=tloc: e.tensor_tensor(
                        qd[n][:], qf[n][:, tloc:tloc+128], qdec[:], ALU.mult),
                     reads=(f"qf{n}", "qdec"), writes=(f"qd{n}",))
            for m in range(2):
                def mmri(e, m=m):
                    for n in range(2):
                        e.matmul(pR[m][:, 0:128], MTb[n][:, m*128:(m+1)*128], qd[n][:],
                                 start=(n == 0), stop=False)
                p.op("pe", mmri, reads=("MTb0", "MTb1", "qd0", "qd1"),
                     writes=(f"pR{m}",))
            for b_ in range(B):
                for n in range(2):
                    p.op("pe", lambda e, b_=b_, n=n: e.transpose(
                            pT[:, n*128:(n+1)*128], kt[b_][:, n*128:(n+1)*128], consts["ident"][:]),
                         reads=(f"kt{b_}", "ident"), writes=("pT",))
                    p.op("dve", lambda e, n=n: e.tensor_copy(kfT[n][:], pT[:, n*128:(n+1)*128]),
                         reads=("pT",), writes=(f"kfT{n}",))
                def mmA(e, tloc=tloc):
                    for n in range(2):
                        e.matmul(pA[:, 0:128], kfT[n][:], qf[n][:, tloc:tloc+128],
                                 start=(n == 0), stop=(n == 1))
                p.op("pe", mmA, reads=("kfT0", "kfT1", "qf0", "qf1"), writes=("pA",))
                p.op("dve", lambda e: e.tensor_tensor(awt[:], pA[:, 0:128], wmask[:], ALU.mult),
                     reads=("pA", "wmask"), writes=("awt",))
                for m in range(2):
                    def mmrintra(e, b_=b_, m=m):
                        e.matmul(pR[m][:, 0:128], vt[b_][:, m*128:(m+1)*128], awt[:],
                                 start=False, stop=(b_ == B-1))
                    p.op("pe", mmrintra, reads=(f"vt{b_}", "awt"), writes=(f"pR{m}",))
            for m in range(2):
                p.op("dve", lambda e, m=m: e.tensor_copy(rfb[m][:], pR[m][:, 0:128]),
                     reads=(f"pR{m}",), writes=(f"rfb{m}",))
            for j in range(2):
                def mmo(e, j=j):
                    for m in range(2):
                        e.matmul(pO[j][:], rfb[m][:], wo[m][:, j*512:(j+1)*512],
                                 start=(m == 0), stop=(m == 1))
                p.op("pe", mmo, reads=("rfb0", "rfb1", "wo0", "wo1"), writes=(f"pO{j}",))
            def evo(e):
                e.tensor_scalar(outt[:, 0:512], pO[0][:], gv[:, 0:1], None, ALU.mult)
                e.tensor_scalar(outt[:, 512:1024], pO[1][:], gv[:, 0:1], None, ALU.mult)
            p.op("dve", evo, reads=("pO0", "pO1", "gv"), writes=("outt",))
            p.op("sp", lambda e, tloc=tloc: e.dma_start(out=out_d[tloc:tloc+128, :], in_=outt[:]),
                 reads=("outt",), dma=True)
        for b_ in range(B):
            p.op("dve", lambda e, b_=b_: e.tensor_scalar(kw_[b_][:], kt[b_][:], wdec[:, 0:1],
                                                         None, ALU.mult),
                 reads=(f"kt{b_}", "wdec"), writes=(f"kwd{b_}",))
        for n in range(2):
            def mmst(e, n=n):
                for b_ in range(B):
                    e.matmul(pS[:, 0:M], kw_[b_][:, n*128:(n+1)*128], vt[b_][:],
                             start=(b_ == 0), stop=(b_ == B-1))
            p.op("pe", mmst, reads=tuple(f"kwd{b_}" for b_ in range(B)) + tuple(f"vt{b_}" for b_ in range(B)),
                 writes=("pS",))
            p.op("dve", lambda e, n=n: e.scalar_tensor_tensor(
                    MT[n][:], MT[n][:], 0.99 ** 128, pS[:, 0:M], ALU.mult, ALU.add),
                 reads=(f"MT{n}", "pS"), writes=(f"MT{n}",))
    p.finish()
    return ctx


def build_l2_proj():
    ctx = Ctx()
    p = ctx.p
    xnT_d = ctx.din("xnT", [D, TC], BF16)
    wq_d = ctx.din("wqT", [D, D], F32)
    wk_d = ctx.din("wkT", [D, D], F32)
    wv_d = ctx.din("wvT", [D, D], F32)
    wg_d = ctx.din("wgam", [16, 1], F32)
    bg_d = ctx.din("bgam", [16, 1], F32)
    wl_d = ctx.din("wlam", [16, 1], F32)
    bl_d = ctx.din("blam", [16, 1], F32)
    g16_d = ctx.din("c_g16", [ND, 128, 16], BF16)
    g2_d = ctx.din("l0f_g2", [128, 2], F32)
    g2t_d = ctx.din("l0f_g2t", [2, 128], F32)
    qf_d = ctx.dout("q_feat", [D, TC], BF16)
    vf_d = ctx.dout("v_feat", [D, TC], BF16)
    vt_d = ctx.dout("v_tok", [TC, D], BF16)
    kt_d = ctx.dout("kh_tok", [TC, D], BF16)
    lr_d = ctx.dout("logrho", [16, TC], F32)
    consts = load_consts(ctx)

    wq = [ctx.sbuf(f"wq{k}", [128, D], BF16) for k in range(ND)]
    wk = [ctx.sbuf(f"wk{k}", [128, D], BF16) for k in range(ND)]
    wv = [ctx.sbuf(f"wv{k}", [128, D], BF16) for k in range(ND)]
    g16 = [ctx.sbuf(f"g16_{c}", [128, 16], BF16) for c in range(ND)]
    g2 = ctx.sbuf("g2", [128, 2], F32)
    g2t = ctx.sbuf("g2t", [2, 128], F32)
    wgam = ctx.sbuf("wgam_s", [16, 1], F32)
    bgam = ctx.sbuf("bgam_s", [16, 1], F32)
    wlam = ctx.sbuf("wlam_s", [16, 1], F32)
    blam = ctx.sbuf("blam_s", [16, 1], F32)
    xt = [ctx.sbuf(f"xt{i}", [128, ND * 512], BF16) for i in range(2)]
    ev = ctx.sbuf("ev", [128, 512], BF16)
    evt = ctx.sbuf("evt", [128, 128], BF16)
    ksq = ctx.sbuf("ksq", [128, 512], F32)
    nrow = ctx.sbuf("nrow", [2, 3 * 512], F32)
    ninv = ctx.sbuf("ninv", [128, 512], F32)
    khb = ctx.sbuf("khb", [128, 512], BF16)
    srow = ctx.sbuf("srow", [16, 6 * 512], F32)
    pq = ctx.psum("pq", [128, 512], F32)
    pn = ctx.psum("pn", [128, 512], F32)
    pkb = ctx.psum("pkb", [128, 512], F32)
    pT = ctx.psum("pT", [128, 1024], BF16)

    for k in range(ND):
        for w_, d_, nm in [(wq, wq_d, "wq"), (wk, wk_d, "wk"), (wv, wv_d, "wv")]:
            p.op("pool", lambda e, w_=w_, d_=d_, k=k: e.dma_start(
                    out=w_[k][:], in_=d_[k*128:(k+1)*128, :]), writes=(f"{nm}{k}",), dma=True)
        p.op("pool", lambda e, k=k: e.dma_start(out=g16[k][:], in_=g16_d[k]), writes=("g16",), dma=True)
    for sb, d_, nm in [(g2, g2_d, "g2"), (g2t, g2t_d, "g2t"), (wgam, wg_d, "wgam_s"),
                       (bgam, bg_d, "bgam"), (wlam, wl_d, "wlam"), (blam, bl_d, "blam")]:
        p.op("sp", lambda e, sb=sb, d_=d_: e.dma_start(out=sb[:], in_=d_), writes=(nm,), dma=True)

    def transpose_out(src_sb, src_buf, dram, t, m):
        for tc in range(4):
            p.op("pe", lambda e, tc=tc: e.transpose(
                    pT[:, tc*128:(tc+1)*128], src_sb[:, tc*128:(tc+1)*128], consts["ident"][:]),
                 reads=(src_buf, "ident"), writes=("pT",))
            p.op("dve", lambda e, tc=tc: e.tensor_copy(evt[:], pT[:, tc*128:(tc+1)*128]),
                 reads=("pT",), writes=("evt",))
            p.op("sp", lambda e, t=t, tc=tc, m=m: e.dma_start(
                    out=dram[t*512 + tc*128: t*512 + (tc+1)*128, m*128:(m+1)*128], in_=evt[:]),
                 reads=("evt",), dma=True)

    for t in range(TC // 512):
        i = t % 2
        load_xnT_tile(ctx, p, xnT_d, xt[i], f"xt{i}", t * 512, 512)
        for m in range(ND):
            def mmq(e, m=m, i=i):
                for k in range(ND):
                    e.matmul(pq[:], wq[k][:, m*128:(m+1)*128], xt[i][:, k*512:(k+1)*512],
                             start=(k == 0), stop=(k == ND-1))
            p.op("pe", mmq, reads=(f"xt{i}",) + tuple(f"wq{k}" for k in range(ND)), writes=("pq",))
            p.op("dve", lambda e: e.tensor_copy(ev[:], pq[:]), reads=("pq",), writes=("ev",))
            p.op("sp", lambda e, m=m, t=t: e.dma_start(
                    out=qf_d[m*128:(m+1)*128, t*512:(t+1)*512], in_=ev[:]),
                 reads=("ev",), dma=True)
        for m in range(ND):
            def mmv(e, m=m, i=i):
                for k in range(ND):
                    e.matmul(pq[:], wv[k][:, m*128:(m+1)*128], xt[i][:, k*512:(k+1)*512],
                             start=(k == 0), stop=(k == ND-1))
            p.op("pe", mmv, reads=(f"xt{i}",) + tuple(f"wv{k}" for k in range(ND)), writes=("pq",))
            p.op("dve", lambda e: e.tensor_copy(ev[:], pq[:]), reads=("pq",), writes=("ev",))
            p.op("sp", lambda e, m=m, t=t: e.dma_start(
                    out=vf_d[m*128:(m+1)*128, t*512:(t+1)*512], in_=ev[:]),
                 reads=("ev",), dma=True)
            transpose_out(ev, "ev", vt_d, t, m)
        for m in range(ND):
            def mmk(e, m=m, i=i):
                for k in range(ND):
                    e.matmul(pq[:], wk[k][:, m*128:(m+1)*128], xt[i][:, k*512:(k+1)*512],
                             start=(k == 0), stop=(k == ND-1))
            p.op("pe", mmk, reads=(f"xt{i}",) + tuple(f"wk{k}" for k in range(ND)), writes=("pq",))
            p.op("act", lambda e: e.activation(ksq[:], pq[:], AF.Square), reads=("pq",), writes=("ksq",))
            p.op("pe", lambda e: e.matmul(pn[0:2, 0:512], g2[:], ksq[:], start=True, stop=True),
                 reads=("ksq", "g2"), writes=("pn",))
            p.op("dve", lambda e: e.tensor_copy(nrow[0:2, 0:512], pn[0:2, 0:512]),
                 reads=("pn",), writes=("nrow",))
            p.op("act", lambda e: e.activation(nrow[0:2, 512:1024], nrow[0:2, 0:512], AF.Sqrt),
                 reads=("nrow",), writes=("nrow",))
            def invfn(e):
                e.tensor_scalar(nrow[0:2, 512:1024], nrow[0:2, 512:1024], 1e-12, None, ALU.max)
                e.drain()
                e.reciprocal(nrow[0:2, 1024:1536], nrow[0:2, 512:1024])
            p.op("dve", invfn, reads=("nrow",), writes=("nrow",))
            p.op("pe", lambda e: e.matmul(pn[:, 0:512], g2t[:], nrow[0:2, 1024:1536],
                                          start=True, stop=True),
                 reads=("nrow", "g2t"), writes=("pn",))
            p.op("dve", lambda e: e.tensor_copy(ninv[:], pn[:, 0:512]), reads=("pn",), writes=("ninv",))
            p.op("dve", lambda e: e.tensor_tensor(khb[:], pq[:], ninv[:], ALU.mult),
                 reads=("pq", "ninv"), writes=("khb",))
            transpose_out(khb, "khb", kt_d, t, m)
            def mmb(e, m=m):
                e.matmul(pkb[0:16, 0:512], g16[m][:], khb[:], start=(m == 0), stop=(m == ND-1))
            p.op("pe", mmb, reads=("khb", "g16"), writes=("pkb",))
        def gam_fn(e):
            e.activation(srow[:, 0:512], pkb[0:16, 0:512], AF.Sigmoid,
                         bias=bgam[:, 0:1], scale=wgam[:, 0:1])
            e.activation(srow[:, 512:1024], pkb[0:16, 0:512], AF.Sigmoid,
                         bias=blam[:, 0:1], scale=wlam[:, 0:1])
        p.op("act", gam_fn, reads=("pkb", "wgam_s", "bgam_s", "wlam_s", "blam_s"), writes=("srow",))
        def rho_fn(e):
            e.tensor_scalar(srow[:, 1024:1536], srow[:, 512:1024], -1.0, 1.0, ALU.mult, ALU.add)
            e.drain()
            e.tensor_tensor(srow[:, 1536:2048], srow[:, 0:512], srow[:, 1024:1536], ALU.mult)
            e.drain()
            e.tensor_scalar(srow[:, 1536:2048], srow[:, 1536:2048], 1e-38, None, ALU.max)
        p.op("dve", rho_fn, reads=("srow",), writes=("srow",))
        p.op("act", lambda e: e.activation(srow[:, 2048:2560], srow[:, 1536:2048], AF.Ln),
             reads=("srow",), writes=("srow",))
        p.op("sp", lambda e, t=t: e.dma_start(out=lr_d[:, t*512:(t+1)*512], in_=srow[:, 2048:2560]),
             reads=("srow",), dma=True)
    p.finish()
    return ctx


def l2_wg_fold(wgam):
    return (wgam / 64.0).reshape(16, 1).astype(np.float32)

def build_l2_scan():
    H16 = 16
    ctx = Ctx()
    p = ctx.p
    qf_d = ctx.din("q_full", [D, S], BF16)
    kh_d = ctx.din("kh_full", [S, D], BF16)
    vt_d = ctx.din("vt_full", [S, D], BF16)
    vf_d = ctx.din("vf_full", [D, S], BF16)
    lr_d = ctx.din("lr_full", [16, S], F32)
    wo_d = ctx.din("woT", [D, D], F32)
    mneg_d = ctx.din("l2_maskneg", [128, 128], F32)
    identf_d = ctx.din("c_ident_f32", [128, 128], F32)
    e16_d = ctx.din("c_e16", [16, 16 * 64], F32)
    out_d = ctx.dout("out", [S, D], F32)
    consts = load_consts(ctx)

    wo = [ctx.sbuf(f"wo{k}", [128, D], BF16) for k in range(ND)]
    mneg = ctx.sbuf("mneg", [128, 128], F32)
    identf = ctx.sbuf("identf", [128, 128], F32)
    e16 = ctx.sbuf("e16", [16, 16 * 64], F32)
    onesc = ctx.sbuf("onesc", [128, S // 16], F32)
    qfc = ctx.sbuf("qfc", [128, ND * 128], BF16)
    khc = ctx.sbuf("khc", [128, D], BF16)
    vtc = ctx.sbuf("vtc", [128, D], BF16)
    vfc = ctx.sbuf("vfc", [128, ND * 128], BF16)
    lrc = ctx.sbuf("lrc", [16, 128], F32)
    Lc = ctx.sbuf("Lc", [16, 128], F32)
    LcT = ctx.sbuf("LcT", [128, 16], F32)
    rho = ctx.sbuf("rho", [16, 128], F32)
    rhoT = ctx.sbuf("rhoT", [128, 16], F32)
    onem = ctx.sbuf("onem", [128, 16], F32)
    pc16 = ctx.sbuf("pc16", [16, 1], F32)
    lrow = ctx.sbuf("lrow", [1, 128], F32)
    crow = ctx.sbuf("crow", [1, 128], F32)
    cb64 = ctx.sbuf("cb64", [128, 128], F32)
    dmat = ctx.sbuf("dmat", [128, 128], F32)
    wmat = ctx.sbuf("wmat", [128, 128], F32)
    wzv = ctx.sbuf("wzv", [128, 1], F32)
    awb = ctx.sbuf("awb", [128, 128], BF16)
    qpb = ctx.sbuf("qpb", [64, 128], BF16)
    vwz = ctx.sbuf("vwz", [128, 64], BF16)
    Sall = ctx.sbuf("Sall", [64, H16 * 64], F32)
    Sb = ctx.sbuf("Sb", [64, 64], BF16)
    pcv = ctx.sbuf("pcv", [64, 1], F32)
    ych = [ctx.sbuf(f"ych{c}", [128, 128], BF16) for c in range(ND)]
    outt = ctx.sbuf("outt", [128, D], F32)
    ones16 = ctx.sbuf("ones16", [16, 128], F32)
    pA = ctx.psum("pA", [128, 512], F32)
    pD = ctx.psum("pD", [128, 512], F32)
    pY = ctx.psum("pY", [128, 512], F32)
    pSd = ctx.psum("pSd", [128, 512], F32)
    pO = [ctx.psum(f"pO{j}", [128, 512], F32) for j in range(2)]
    pT = ctx.psum("pT", [128, 1024], BF16)

    p.op("pool", lambda e: e.dma_start(out=mneg[:], in_=mneg_d), writes=("mneg",), dma=True)
    p.op("pool", lambda e: e.dma_start(out=identf[:], in_=identf_d), writes=("identf",), dma=True)
    p.op("pool", lambda e: e.dma_start(out=e16[:], in_=e16_d), writes=("e16",), dma=True)
    for k in range(ND):
        p.op("pool", lambda e, k=k: e.dma_start(out=wo[k][:], in_=wo_d[k*128:(k+1)*128, :]),
             writes=(f"wo{k}",), dma=True)
    p.op("dve", lambda e: e.memset(ones16[:], 1.0), writes=("ones16",))
    p.op("dve", lambda e: e.memset(Sall[:], 0.0), writes=("Sall",))

    for c in range(S // 128):
        p.op("sp", lambda e, c=c: e.dma_start(out=khc[:], in_=kh_d[c*128:(c+1)*128, :]),
             writes=("khc",), dma=True)
        p.op("sp", lambda e, c=c: e.dma_start(out=vtc[:], in_=vt_d[c*128:(c+1)*128, :]),
             writes=("vtc",), dma=True)
        def ldq(e, c=c):
            src = bass.AP(qf_d.tensor, c * 128, [[S, 128], [128 * S, ND], [1, 128]])
            return e.dma_start(out=qfc[:].rearrange("p (f x) -> p f x", f=ND), in_=src)
        p.op("sp", ldq, writes=("qfc",), dma=True)
        def ldvf(e, c=c):
            src = bass.AP(vf_d.tensor, c * 128, [[S, 128], [128 * S, ND], [1, 128]])
            return e.dma_start(out=vfc[:].rearrange("p (f x) -> p f x", f=ND), in_=src)
        p.op("sp", ldvf, writes=("vfc",), dma=True)
        p.op("sp", lambda e, c=c: e.dma_start(out=lrc[:], in_=lr_d[:, c*128:(c+1)*128]),
             writes=("lrc",), dma=True)
        p.op("dve", lambda e: e.tensor_tensor_scan(Lc[:], ones16[:], lrc[:], 0.0, ALU.mult, ALU.add),
             reads=("ones16", "lrc"), writes=("Lc",))
        p.op("act", lambda e: e.activation(rho[:], lrc[:], AF.Exp), reads=("lrc",), writes=("rho",))
        p.op("act", lambda e: e.activation(pc16[:], Lc[:, 127:128], AF.Exp), reads=("Lc",), writes=("pc16",))
        p.op("pe", lambda e: e.transpose(pA[0:128, 0:16], Lc[:], identf[0:16, 0:16]),
             reads=("Lc", "identf"), writes=("pA",))
        p.op("dve", lambda e: e.tensor_copy(LcT[:], pA[0:128, 0:16]), reads=("pA",), writes=("LcT",))
        p.op("pe", lambda e: e.transpose(pA[0:128, 16:32], rho[:], identf[0:16, 0:16]),
             reads=("rho", "identf"), writes=("pA",))
        p.op("dve", lambda e: e.tensor_copy(rhoT[:], pA[0:128, 16:32]), reads=("pA",), writes=("rhoT",))
        p.op("dve", lambda e: e.tensor_scalar(onem[:], rhoT[:], -1.0, 1.0, ALU.mult, ALU.add),
             reads=("rhoT",), writes=("onem",))
        for h in range(H16):
            hc2, ho = h // 2, (h % 2) * 64
            p.op("pe", lambda e, h=h: e.transpose(pD[0:1, 0:128], LcT[:, h:h+1], identf[:]),
                 reads=("LcT", "identf"), writes=("pD",))
            p.op("dve", lambda e: e.tensor_copy(lrow[:], pD[0:1, 0:128]), reads=("pD",), writes=("lrow",))
            p.op("act", lambda e: e.activation(crow[:], lrow[:], AF.Exp), reads=("lrow",), writes=("crow",))
            p.op("pe", lambda e: e.matmul(pD[:, 128:256], consts["onesr"][:], lrow[:],
                                          start=True, stop=True),
                 reads=("lrow", "onesr"), writes=("pD",))
            p.op("pe", lambda e: e.matmul(pD[:, 256:384], consts["onesr"][:], crow[:],
                                          start=True, stop=True),
                 reads=("crow", "onesr"), writes=("pD",))
            def dfn(e, h=h):
                e.tensor_scalar(dmat[:], pD[:, 128:256], LcT[:, h:h+1], None, ALU.subtract)
                e.drain()
                e.tensor_tensor(dmat[:], dmat[:], mneg[:], ALU.add)
            p.op("dve", dfn, reads=("pD", "LcT", "mneg"), writes=("dmat",))
            p.op("act", lambda e: e.activation(wmat[:], dmat[:], AF.Exp), reads=("dmat",), writes=("wmat",))
            def wfn(e, h=h):
                e.tensor_scalar(wmat[:], wmat[:], onem[:, h:h+1], None, ALU.mult)
                e.drain()
                e.tensor_copy(wzv[:], wmat[:, 127:128])
            p.op("dve", wfn, reads=("wmat", "onem"), writes=("wmat", "wzv"))
            p.op("dve", lambda e: e.tensor_copy(cb64[:], pD[:, 256:384]), reads=("pD",), writes=("cb64",))
            def mmA(e, hc2=hc2, ho=ho):
                e.matmul(pA[:, 0:128], vfc[ho:ho+64, hc2*128:(hc2+1)*128],
                         qfc[ho:ho+64, hc2*128:(hc2+1)*128], start=True, stop=True)
            p.op("pe", mmA, reads=("vfc", "qfc"), writes=("pA",))
            p.op("dve", lambda e: e.tensor_tensor(awb[:], pA[:, 0:128], wmat[:], ALU.mult),
                 reads=("pA", "wmat"), writes=("awb",))
            def qpfn(e, hc2=hc2, ho=ho):
                e.tensor_tensor(qpb[:], qfc[ho:ho+64, hc2*128:(hc2+1)*128], cb64[ho:ho+64, :], ALU.mult)
            p.op("dve", qpfn, reads=("qfc", "cb64"), writes=("qpb",))
            p.op("dve", lambda e, h=h: e.tensor_copy(Sb[:], Sall[:, h*64:(h+1)*64]),
                 reads=("Sall",), writes=("Sb",))
            def mmY(e, h=h):
                e.matmul(pY[0:64, 0:128], khc[:, h*64:(h+1)*64], awb[:], start=True, stop=False)
                e.matmul(pY[0:64, 0:128], Sb[:], qpb[:], start=False, stop=True)
            p.op("pe", mmY, reads=("khc", "awb", "Sb", "qpb"), writes=("pY",))
            p.op("dve", lambda e, hc2=hc2, ho=ho: e.tensor_copy(
                    ych[hc2][ho:ho+64, :], pY[0:64, 0:128]),
                 reads=("pY",), writes=(f"ych{hc2}",))
            def vwzfn(e, h=h):
                e.tensor_scalar(vwz[:], vtc[:, h*64:(h+1)*64], wzv[:, 0:1], None, ALU.mult)
            p.op("dve", vwzfn, reads=("vtc", "wzv"), writes=("vwz",))
            def mmDS(e, h=h):
                e.matmul(pSd[0:64, 0:64], vwz[:], khc[:, h*64:(h+1)*64], start=True, stop=True)
            p.op("pe", mmDS, reads=("vwz", "khc"), writes=("pSd",))
            def mmPC(e, h=h):
                e.matmul(pSd[0:64, 64:65], e16[:, h*64:(h+1)*64], pc16[:], start=True, stop=True)
            p.op("pe", mmPC, reads=("e16", "pc16"), writes=("pSd",))
            p.op("dve", lambda e: e.tensor_copy(pcv[:], pSd[0:64, 64:65]), reads=("pSd",), writes=("pcv",))
            def supd(e, h=h):
                e.scalar_tensor_tensor(Sall[:, h*64:(h+1)*64], Sall[:, h*64:(h+1)*64],
                                       pcv[:, 0:1], pSd[0:64, 0:64], ALU.mult, ALU.add)
            p.op("dve", supd, reads=("Sall", "pcv", "pSd"), writes=("Sall",))
        for j in range(2):
            def mmo(e, j=j):
                for cc in range(ND):
                    e.matmul(pO[j][:], ych[cc][:], wo[cc][:, j*512:(j+1)*512],
                             start=(cc == 0), stop=(cc == ND-1))
            p.op("pe", mmo, reads=tuple(f"ych{cc}" for cc in range(ND)) + tuple(f"wo{cc}" for cc in range(ND)),
                 writes=(f"pO{j}",))
        def evo(e):
            e.tensor_copy(outt[:, 0:512], pO[0][:])
            e.tensor_copy(outt[:, 512:1024], pO[1][:])
        p.op("dve", evo, reads=("pO0", "pO1"), writes=("outt",))
        p.op("sp", lambda e, c=c: e.dma_start(out=out_d[c*128:(c+1)*128, :], in_=outt[:]),
             reads=("outt",), dma=True)
    p.finish()
    return ctx


def build_l2_scan2():
    H16 = 8
    HD = 512
    ctx = Ctx()
    p = ctx.p
    qf_d = ctx.din("q_full", [HD, S], BF16)
    kh_d = ctx.din("kh_full", [S, HD], BF16)
    vt_d = ctx.din("vt_full", [S, HD], BF16)
    vf_d = ctx.din("vf_full", [HD, S], BF16)
    lr_d = ctx.din("lr_full", [8, S], F32)
    wo_d = ctx.din("woT", [HD, D], F32)
    mneg_d = ctx.din("l2_maskneg", [128, 128], F32)
    identf_d = ctx.din("c_ident_f32", [128, 128], F32)
    e16_d = ctx.din("c_e16", [8, 8 * 64], F32)
    out_d = ctx.dout("out", [S, D], F32)
    consts = load_consts(ctx)

    NDH = HD // 128
    wo = [ctx.sbuf(f"wo{k}", [128, D], BF16) for k in range(NDH)]
    mneg = ctx.sbuf("mneg", [128, 128], F32)
    identf = ctx.sbuf("identf", [128, 128], F32)
    e16 = ctx.sbuf("e16", [8, 8 * 64], F32)
    onesc = ctx.sbuf("onesc", [128, S // 16], F32)
    qfc = ctx.sbuf("qfc", [128, NDH * 128], BF16)
    khc = ctx.sbuf("khc", [128, HD], BF16)
    vtc = ctx.sbuf("vtc", [128, HD], BF16)
    vfc = ctx.sbuf("vfc", [128, NDH * 128], BF16)
    lrc = ctx.sbuf("lrc", [8, 128], F32)
    Lc = ctx.sbuf("Lc", [8, 128], F32)
    LcT = ctx.sbuf("LcT", [128, 8], F32)
    rho = ctx.sbuf("rho", [8, 128], F32)
    rhoT = ctx.sbuf("rhoT", [128, 8], F32)
    onem = ctx.sbuf("onem", [128, 8], F32)
    pc16 = ctx.sbuf("pc16", [8, 1], F32)
    lrow = ctx.sbuf("lrow", [1, 128], F32)
    crow = ctx.sbuf("crow", [1, 128], F32)
    cb64 = ctx.sbuf("cb64", [128, 128], F32)
    dmat = ctx.sbuf("dmat", [128, 128], F32)
    wmat = ctx.sbuf("wmat", [128, 128], F32)
    wzv = ctx.sbuf("wzv", [128, 1], F32)
    awb = ctx.sbuf("awb", [128, 128], BF16)
    qpb = ctx.sbuf("qpb", [64, 128], BF16)
    vwz = ctx.sbuf("vwz", [128, 64], BF16)
    Sall = ctx.sbuf("Sall", [64, H16 * 64], F32)
    Sb = ctx.sbuf("Sb", [64, 64], BF16)
    pcv = ctx.sbuf("pcv", [64, 1], F32)
    ych = [ctx.sbuf(f"ych{c}", [128, 128], BF16) for c in range(NDH)]
    outt = ctx.sbuf("outt", [128, D], F32)
    ones16 = ctx.sbuf("ones16", [8, 128], F32)
    pA = ctx.psum("pA", [128, 512], F32)
    pD = ctx.psum("pD", [128, 512], F32)
    pY = ctx.psum("pY", [128, 512], F32)
    pSd = ctx.psum("pSd", [128, 512], F32)
    pO = [ctx.psum(f"pO{j}", [128, 512], F32) for j in range(2)]
    pT = ctx.psum("pT", [128, 1024], BF16)

    p.op("pool", lambda e: e.dma_start(out=mneg[:], in_=mneg_d), writes=("mneg",), dma=True)
    p.op("pool", lambda e: e.dma_start(out=identf[:], in_=identf_d), writes=("identf",), dma=True)
    p.op("pool", lambda e: e.dma_start(out=e16[:], in_=e16_d), writes=("e16",), dma=True)
    for k in range(NDH):
        p.op("pool", lambda e, k=k: e.dma_start(out=wo[k][:], in_=wo_d[k*128:(k+1)*128, :]),
             writes=(f"wo{k}",), dma=True)
    p.op("dve", lambda e: e.memset(ones16[:], 1.0), writes=("ones16",))
    p.op("dve", lambda e: e.memset(Sall[:], 0.0), writes=("Sall",))

    for c in range(S // 128):
        p.op("sp", lambda e, c=c: e.dma_start(out=khc[:], in_=kh_d[c*128:(c+1)*128, :]),
             writes=("khc",), dma=True)
        p.op("sp", lambda e, c=c: e.dma_start(out=vtc[:], in_=vt_d[c*128:(c+1)*128, :]),
             writes=("vtc",), dma=True)
        def ldq(e, c=c):
            src = bass.AP(qf_d.tensor, c * 128, [[S, 128], [128 * S, NDH], [1, 128]])
            return e.dma_start(out=qfc[:].rearrange("p (f x) -> p f x", f=NDH), in_=src)
        p.op("sp", ldq, writes=("qfc",), dma=True)
        def ldvf(e, c=c):
            src = bass.AP(vf_d.tensor, c * 128, [[S, 128], [128 * S, NDH], [1, 128]])
            return e.dma_start(out=vfc[:].rearrange("p (f x) -> p f x", f=NDH), in_=src)
        p.op("sp", ldvf, writes=("vfc",), dma=True)
        p.op("sp", lambda e, c=c: e.dma_start(out=lrc[:], in_=lr_d[:, c*128:(c+1)*128]),
             writes=("lrc",), dma=True)
        p.op("dve", lambda e: e.tensor_tensor_scan(Lc[:], ones16[:], lrc[:], 0.0, ALU.mult, ALU.add),
             reads=("ones16", "lrc"), writes=("Lc",))
        p.op("act", lambda e: e.activation(rho[:], lrc[:], AF.Exp), reads=("lrc",), writes=("rho",))
        p.op("act", lambda e: e.activation(pc16[:], Lc[:, 127:128], AF.Exp), reads=("Lc",), writes=("pc16",))
        p.op("pe", lambda e: e.transpose(pA[0:128, 0:8], Lc[:], identf[0:8, 0:8]),
             reads=("Lc", "identf"), writes=("pA",))
        p.op("dve", lambda e: e.tensor_copy(LcT[:], pA[0:128, 0:8]), reads=("pA",), writes=("LcT",))
        p.op("pe", lambda e: e.transpose(pA[0:128, 16:24], rho[:], identf[0:8, 0:8]),
             reads=("rho", "identf"), writes=("pA",))
        p.op("dve", lambda e: e.tensor_copy(rhoT[:], pA[0:128, 16:24]), reads=("pA",), writes=("rhoT",))
        p.op("dve", lambda e: e.tensor_scalar(onem[:], rhoT[:], -1.0, 1.0, ALU.mult, ALU.add),
             reads=("rhoT",), writes=("onem",))
        for h in range(H16):
            hc2, ho = h // 2, (h % 2) * 64
            p.op("pe", lambda e, h=h: e.transpose(pD[0:1, 0:128], LcT[:, h:h+1], identf[:]),
                 reads=("LcT", "identf"), writes=("pD",))
            p.op("dve", lambda e: e.tensor_copy(lrow[:], pD[0:1, 0:128]), reads=("pD",), writes=("lrow",))
            p.op("act", lambda e: e.activation(crow[:], lrow[:], AF.Exp), reads=("lrow",), writes=("crow",))
            p.op("pe", lambda e: e.matmul(pD[:, 128:256], consts["onesr"][:], lrow[:],
                                          start=True, stop=True),
                 reads=("lrow", "onesr"), writes=("pD",))
            p.op("pe", lambda e: e.matmul(pD[:, 256:384], consts["onesr"][:], crow[:],
                                          start=True, stop=True),
                 reads=("crow", "onesr"), writes=("pD",))
            def dfn(e, h=h):
                e.tensor_scalar(dmat[:], pD[:, 128:256], LcT[:, h:h+1], None, ALU.subtract)
                e.drain()
                e.tensor_tensor(dmat[:], dmat[:], mneg[:], ALU.add)
            p.op("dve", dfn, reads=("pD", "LcT", "mneg"), writes=("dmat",))
            p.op("act", lambda e: e.activation(wmat[:], dmat[:], AF.Exp), reads=("dmat",), writes=("wmat",))
            def wfn(e, h=h):
                e.tensor_scalar(wmat[:], wmat[:], onem[:, h:h+1], None, ALU.mult)
                e.drain()
                e.tensor_copy(wzv[:], wmat[:, 127:128])
            p.op("dve", wfn, reads=("wmat", "onem"), writes=("wmat", "wzv"))
            p.op("dve", lambda e: e.tensor_copy(cb64[:], pD[:, 256:384]), reads=("pD",), writes=("cb64",))
            def mmA(e, hc2=hc2, ho=ho):
                e.matmul(pA[:, 0:128], vfc[ho:ho+64, hc2*128:(hc2+1)*128],
                         qfc[ho:ho+64, hc2*128:(hc2+1)*128], start=True, stop=True)
            p.op("pe", mmA, reads=("vfc", "qfc"), writes=("pA",))
            p.op("dve", lambda e: e.tensor_tensor(awb[:], pA[:, 0:128], wmat[:], ALU.mult),
                 reads=("pA", "wmat"), writes=("awb",))
            def qpfn(e, hc2=hc2, ho=ho):
                e.tensor_tensor(qpb[:], qfc[ho:ho+64, hc2*128:(hc2+1)*128], cb64[ho:ho+64, :], ALU.mult)
            p.op("dve", qpfn, reads=("qfc", "cb64"), writes=("qpb",))
            p.op("dve", lambda e, h=h: e.tensor_copy(Sb[:], Sall[:, h*64:(h+1)*64]),
                 reads=("Sall",), writes=("Sb",))
            def mmY(e, h=h):
                e.matmul(pY[0:64, 0:128], khc[:, h*64:(h+1)*64], awb[:], start=True, stop=False)
                e.matmul(pY[0:64, 0:128], Sb[:], qpb[:], start=False, stop=True)
            p.op("pe", mmY, reads=("khc", "awb", "Sb", "qpb"), writes=("pY",))
            p.op("dve", lambda e, hc2=hc2, ho=ho: e.tensor_copy(
                    ych[hc2][ho:ho+64, :], pY[0:64, 0:128]),
                 reads=("pY",), writes=(f"ych{hc2}",))
            def vwzfn(e, h=h):
                e.tensor_scalar(vwz[:], vtc[:, h*64:(h+1)*64], wzv[:, 0:1], None, ALU.mult)
            p.op("dve", vwzfn, reads=("vtc", "wzv"), writes=("vwz",))
            def mmDS(e, h=h):
                e.matmul(pSd[0:64, 0:64], vwz[:], khc[:, h*64:(h+1)*64], start=True, stop=True)
            p.op("pe", mmDS, reads=("vwz", "khc"), writes=("pSd",))
            def mmPC(e, h=h):
                e.matmul(pSd[0:64, 64:65], e16[:, h*64:(h+1)*64], pc16[:], start=True, stop=True)
            p.op("pe", mmPC, reads=("e16", "pc16"), writes=("pSd",))
            p.op("dve", lambda e: e.tensor_copy(pcv[:], pSd[0:64, 64:65]), reads=("pSd",), writes=("pcv",))
            def supd(e, h=h):
                e.scalar_tensor_tensor(Sall[:, h*64:(h+1)*64], Sall[:, h*64:(h+1)*64],
                                       pcv[:, 0:1], pSd[0:64, 0:64], ALU.mult, ALU.add)
            p.op("dve", supd, reads=("Sall", "pcv", "pSd"), writes=("Sall",))
        for j in range(2):
            def mmo(e, j=j):
                for cc in range(NDH):
                    e.matmul(pO[j][:], ych[cc][:], wo[cc][:, j*512:(j+1)*512],
                             start=(cc == 0), stop=(cc == ND-1))
            p.op("pe", mmo, reads=tuple(f"ych{cc}" for cc in range(NDH)) + tuple(f"wo{cc}" for cc in range(NDH)),
                 writes=(f"pO{j}",))
        def evo(e):
            e.tensor_copy(outt[:, 0:512], pO[0][:])
            e.tensor_copy(outt[:, 512:1024], pO[1][:])
        p.op("dve", evo, reads=("pO0", "pO1"), writes=("outt",))
        p.op("sp", lambda e, c=c: e.dma_start(out=out_d[c*128:(c+1)*128, :], in_=outt[:]),
             reads=("outt",), dma=True)
    p.finish()
    return ctx




def l2_host_consts():
    u = np.arange(128)
    mneg = np.where(u[:, None] <= np.arange(128)[None, :], 0.0, -1e30).astype(np.float32)
    e16 = np.zeros((8, 8 * 64), np.float32)
    for h in range(8):
        e16[h, h*64:(h+1)*64] = 1.0
    g16 = np.zeros((ND, 128, 16), np.float32)
    for c in range(ND):
        g16[c, 0:64, 2*c] = 1.0 / 64.0
        g16[c, 64:128, 2*c+1] = 1.0 / 64.0
    return dict(l2_maskneg=mneg, c_e16=e16, c_g16=g16.astype(BF))


def build_l3_qkv():
    ctx = Ctx()
    p = ctx.p
    xnT_d = ctx.din("xnT", [D, TC], BF16)
    wq_d = ctx.din("wqT3", [D, D], F32)
    wk_d = ctx.din("wkT3", [D, D], F32)
    wv_d = ctx.din("wvT3", [D, D], F32)
    qf_d = ctx.dout("q_feat", [D, TC], BF16)
    kf_d = ctx.dout("k_feat", [D, TC], BF16)
    vt_d = ctx.dout("v_tok", [TC, D], BF16)
    consts = load_consts(ctx)
    wq = [ctx.sbuf(f"wq{k}", [128, D], BF16) for k in range(ND)]
    wk = [ctx.sbuf(f"wk{k}", [128, D], BF16) for k in range(ND)]
    wv = [ctx.sbuf(f"wv{k}", [128, D], BF16) for k in range(ND)]
    xt = [ctx.sbuf(f"xt{i}", [128, ND * 512], BF16) for i in range(2)]
    ev = ctx.sbuf("ev", [128, 512], BF16)
    evt = ctx.sbuf("evt", [128, 128], BF16)
    ps = ctx.psum("ps", [128, 512], F32)
    pT = ctx.psum("pT", [128, 1024], BF16)
    for k in range(ND):
        for w_, d_, nm in [(wq, wq_d, "wq"), (wk, wk_d, "wk"), (wv, wv_d, "wv")]:
            p.op("pool", lambda e, w_=w_, d_=d_, k=k: e.dma_start(
                    out=w_[k][:], in_=d_[k*128:(k+1)*128, :]), writes=(f"{nm}{k}",), dma=True)
    for t in range(TC // 512):
        i = t % 2
        load_xnT_tile(ctx, p, xnT_d, xt[i], f"xt{i}", t * 512, 512)
        for dd, w_, nm, tok in [(qf_d, wq, "wq", False), (kf_d, wk, "wk", False),
                                (vt_d, wv, "wv", True)]:
            for m in range(ND):
                def mm(e, m=m, i=i, w_=w_):
                    for k in range(ND):
                        e.matmul(ps[:], w_[k][:, m*128:(m+1)*128], xt[i][:, k*512:(k+1)*512],
                                 start=(k == 0), stop=(k == ND-1))
                p.op("pe", mm, reads=(f"xt{i}",) + tuple(f"{nm}{k}" for k in range(ND)), writes=("ps",))
                p.op("dve", lambda e: e.tensor_copy(ev[:], ps[:]), reads=("ps",), writes=("ev",))
                if not tok:
                    p.op("sp", lambda e, dd=dd, m=m, t=t: e.dma_start(
                            out=dd[m*128:(m+1)*128, t*512:(t+1)*512], in_=ev[:]),
                         reads=("ev",), dma=True)
                else:
                    for tc in range(4):
                        p.op("pe", lambda e, tc=tc: e.transpose(
                                pT[:, tc*128:(tc+1)*128], ev[:, tc*128:(tc+1)*128],
                                consts["ident"][:]),
                             reads=("ev", "ident"), writes=("pT",))
                        p.op("dve", lambda e, tc=tc: e.tensor_copy(evt[:], pT[:, tc*128:(tc+1)*128]),
                             reads=("pT",), writes=("evt",))
                        p.op("sp", lambda e, t=t, tc=tc, m=m: e.dma_start(
                                out=vt_d[t*512 + tc*128: t*512 + (tc+1)*128, m*128:(m+1)*128],
                                in_=evt[:]),
                             reads=("evt",), dma=True)
    p.finish()
    return ctx


def build_l3_attn():
    NH = 8
    NQT = S // 128
    ctx = Ctx()
    p = ctx.p
    qf_d = ctx.din("q_half", [512, S], BF16)
    kf_d = ctx.din("k_half", [512, S], BF16)
    vt_d = ctx.din("v_half", [S, 512], BF16)
    msk_d = ctx.din("l3_masks", [4 * 128, 512], F32)
    identf_d = ctx.din("c_ident_f32", [128, 128], F32)
    out_d = ctx.dout("attnT", [512, S], BF16)
    consts = load_consts(ctx)

    qf = [ctx.sbuf(f"qf{c}", [128, S], BF16) for c in range(4)]
    kf = [ctx.sbuf(f"kf{c}", [128, S], BF16) for c in range(4)]
    vt = ctx.sbuf("vt", [128, 32 * 512], BF16)
    msks = ctx.sbuf("msks", [128, 4 * 512], F32)
    identf = ctx.sbuf("identf", [128, 128], F32)
    acc = ctx.sbuf("acc", [128, 64], F32)
    accb = ctx.sbuf("accb", [128, 64], BF16)
    avsb = ctx.sbuf("avsb", [64, 128], F32)
    lrun = ctx.sbuf("lrun", [128, 1], F32)
    lscr = ctx.sbuf("lscr", [128, 1], F32)
    mrow = ctx.sbuf("mrow", [128, 6], F32)
    pexp = ctx.sbuf("pexp", [128, 512], BF16)
    ptsb = ctx.sbuf("ptsb", [128, 512], BF16)
    oT = ctx.sbuf("oT", [64, 128], BF16)
    pS = ctx.psum("pS", [128, 512], F32)
    pav = ctx.psum("pav", [128, 512], F32)
    pT = ctx.psum("pT", [128, 1024], BF16)
    pR = ctx.psum("pR", [128, 512], F32)

    for c in range(4):
        p.op("pool", lambda e, c=c: e.dma_start(out=qf[c][:], in_=qf_d[c*128:(c+1)*128, :]),
             writes=(f"qf{c}",), dma=True)
        p.op("pool", lambda e, c=c: e.dma_start(out=kf[c][:], in_=kf_d[c*128:(c+1)*128, :]),
             writes=(f"kf{c}",), dma=True)
    def ldv(e):
        src = bass.AP(vt_d.tensor, 0, [[512, 128], [512 * 128, 32], [1, 512]])
        return e.dma_start(out=vt[:].rearrange("p (c x) -> p c x", c=32), in_=src)
    p.op("sp", ldv, writes=("vt",), dma=True)
    def ldm(e):
        return e.dma_start(out=msks[:].rearrange("p (m c) -> p m c", m=4),
                           in_=msk_d.rearrange("(m p) c -> p m c", p=128))
    p.op("sp", ldm, writes=("msks",), dma=True)
    p.op("sp", lambda e: e.dma_start(out=identf[:], in_=identf_d), writes=("identf",), dma=True)

    SC = 1.0 / 8.0
    for h in range(NH):
        c, o = h // 2, (h % 2) * 64
        for qt in range(NQT):
            nkb = qt // 4 + 1
            p.op("dve", lambda e: e.memset(acc[:], 0.0), writes=("acc",))
            p.op("dve", lambda e: e.memset(lrun[:], 0.0), writes=("lrun",))
            p.op("dve", lambda e: e.memset(mrow[:, 0:1], -1e30), writes=("mrow",))
            for kb in range(nkb):
                partial = (kb == qt // 4)
                def mmS(e, c=c, o=o, qt=qt, kb=kb):
                    e.matmul(pS[:], qf[c][o:o+64, qt*128:(qt+1)*128],
                             kf[c][o:o+64, kb*512:(kb+1)*512], start=True, stop=True)
                p.op("pe", mmS, reads=(f"qf{c}", f"kf{c}"), writes=("pS",))
                if partial:
                    v_ = qt % 4
                    p.op("dve", lambda e, v_=v_: e.tensor_tensor(
                            pS[:], pS[:], msks[:, v_*512:(v_+1)*512], ALU.add),
                         reads=("pS", "msks"), writes=("pS",))
                def redm(e):
                    e.reduce_max(mrow[:, 1:2], pS[:], axis=AX.X)
                    e.drain()
                    e.tensor_scalar(mrow[:, 1:2], mrow[:, 1:2], SC, None, ALU.mult)
                    e.drain()
                    e.tensor_tensor(mrow[:, 2:3], mrow[:, 0:1], mrow[:, 1:2], ALU.max)
                    e.drain()
                    e.tensor_tensor(mrow[:, 3:4], mrow[:, 0:1], mrow[:, 2:3], ALU.subtract)
                    e.drain()
                    e.tensor_scalar(mrow[:, 5:6], mrow[:, 2:3], -1.0, None, ALU.mult)
                    e.drain()
                    e.tensor_copy(mrow[:, 0:1], mrow[:, 2:3])
                p.op("dve", redm, reads=("pS", "mrow"), writes=("mrow",))
                p.op("act", lambda e: e.activation(mrow[:, 4:5], mrow[:, 3:4], AF.Exp),
                     reads=("mrow",), writes=("mrow",))
                p.op("dve", lambda e: e.tensor_scalar(acc[:], acc[:], mrow[:, 4:5], None, ALU.mult),
                     reads=("acc", "mrow"), writes=("acc",))
                p.op("act", lambda e: e.activation(pexp[:], pS[:], AF.Exp, bias=mrow[:, 5:6],
                                                   scale=SC, accum_out=mrow[:, 3:4]),
                     reads=("pS", "mrow"), writes=("pexp", "mrow"))
                p.op("dve", lambda e: e.scalar_tensor_tensor(
                        lscr[:, 0:1], lrun[:, 0:1], 1.0, mrow[:, 4:5], ALU.mult, ALU.mult),
                     reads=("lrun", "mrow"), writes=("lscr",))
                p.op("dve", lambda e: e.tensor_tensor(lrun[:, 0:1], lscr[:, 0:1], mrow[:, 3:4],
                                                      ALU.add),
                     reads=("lscr", "mrow"), writes=("lrun",))
                for kc in range(4):
                    p.op("pe", lambda e, kc=kc: e.transpose(
                            pT[:, kc*128:(kc+1)*128], pexp[:, kc*128:(kc+1)*128], consts["ident"][:]),
                         reads=("pexp", "ident"), writes=("pT",))
                    if kc % 2 == 0:
                        p.op("dve", lambda e, kc=kc: e.tensor_copy(
                                ptsb[:, kc*128:(kc+1)*128], pT[:, kc*128:(kc+1)*128]),
                             reads=("pT",), writes=("ptsb",))
                    else:
                        p.op("act", lambda e, kc=kc: e.copy(
                                ptsb[:, kc*128:(kc+1)*128], pT[:, kc*128:(kc+1)*128]),
                             reads=("pT",), writes=("ptsb",))
                def mmav(e, h=h, kb=kb):
                    for kc in range(4):
                        e.matmul(pav[0:64, 0:128],
                                 vt[:, (kb*4+kc)*512 + h*64: (kb*4+kc)*512 + (h+1)*64],
                                 ptsb[:, kc*128:(kc+1)*128], start=(kc == 0), stop=(kc == 3))
                p.op("pe", mmav, reads=("ptsb", "vt"), writes=("pav",))
                p.op("act", lambda e: e.copy(avsb[:], pav[0:64, 0:128]), reads=("pav",), writes=("avsb",))
                p.op("pe", lambda e: e.transpose(pR[0:128, 0:64], avsb[:], identf[0:64, 0:64]),
                     reads=("avsb", "identf"), writes=("pR",))
                p.op("dve", lambda e: e.tensor_tensor(acc[:], acc[:], pR[0:128, 0:64], ALU.add),
                     reads=("pR", "acc"), writes=("acc",))
            p.op("dve", lambda e: e.reciprocal(mrow[:, 1:2], lrun[:, 0:1]),
                 reads=("lrun",), writes=("mrow",))
            p.op("dve", lambda e: e.tensor_scalar(accb[:], acc[:], mrow[:, 1:2], None, ALU.mult),
                 reads=("acc", "mrow"), writes=("accb",))
            p.op("pe", lambda e: e.transpose(pT[0:64, 0:128], accb[:], consts["ident"][:]),
                 reads=("accb", "ident"), writes=("pT",))
            p.op("dve", lambda e: e.tensor_copy(oT[:], pT[0:64, 0:128]), reads=("pT",), writes=("oT",))
            p.op("sp", lambda e, h=h, qt=qt: e.dma_start(
                    out=out_d[h*64:(h+1)*64, qt*128:(qt+1)*128], in_=oT[:]),
                 reads=("oT",), dma=True)
    p.finish()
    return ctx


def build_l3_attn2():
    NH = 8
    NQT = S // 128
    ctx = Ctx()
    p = ctx.p
    qf_d = ctx.din("q_half", [512, S], BF16)
    kf_d = ctx.din("k_half", [512, S], BF16)
    vt_d = ctx.din("v_half", [S, 512], BF16)
    msk_d = ctx.din("l3_masks", [4 * 128, 512], F32)
    ident_d = ctx.din("c_ident", [128, 128], BF16)
    out_d = ctx.dout("attnT", [512, S], BF16)

    qf = [ctx.sbuf(f"qf{c}", [128, S], BF16) for c in range(4)]
    kf = [ctx.sbuf(f"kf{c}", [128, S], BF16) for c in range(4)]
    vt = ctx.sbuf("vt", [128, 32 * 512], BF16)
    msks = ctx.sbuf("msks", [128, 4 * 512], F32)
    ident = ctx.sbuf("ident", [128, 128], BF16)
    NDEP = 2
    sm = [ctx.sbuf(f"sm{i}", [128, S], F32) for i in range(NDEP)]
    pq2 = [ctx.sbuf(f"pq2{i}", [128, S], BF16) for i in range(NDEP)]
    PT = [ctx.sbuf(f"PT{i}", [128, S], BF16) for i in range(NDEP)]
    mxc = [ctx.sbuf(f"mxc{i}", [128, 1], F32) for i in range(NDEP)]
    mxn = [ctx.sbuf(f"mxn{i}", [128, 1], F32) for i in range(NDEP)]
    rsc = [ctx.sbuf(f"rsc{i}", [128, 1], F32) for i in range(NDEP)]
    rsr = [ctx.sbuf(f"rsr{i}", [128, 1], F32) for i in range(NDEP)]
    outsb = [ctx.sbuf(f"outsb{i}", [64, 128], BF16) for i in range(NDEP)]
    pS = [ctx.psum(f"pS{b}", [128, 512], F32) for b in range(2)]
    pT2 = [ctx.psum(f"pT2{i}", [128, 1024], BF16) for i in range(2)]
    pAV = [ctx.psum(f"pAV{i}", [128, 512], F32) for i in range(2)]

    for c in range(4):
        p.op("pool", lambda e, c=c: e.dma_start(out=qf[c][:], in_=qf_d[c*128:(c+1)*128, :]),
             writes=(f"qf{c}",), dma=True)
        p.op("pool", lambda e, c=c: e.dma_start(out=kf[c][:], in_=kf_d[c*128:(c+1)*128, :]),
             writes=(f"kf{c}",), dma=True)
    def ldv(e):
        srcv = bass.AP(vt_d.tensor, 0, [[512, 128], [512 * 128, 32], [1, 512]])
        return e.dma_start(out=vt[:].rearrange("p (c x) -> p c x", c=32), in_=srcv)
    p.op("sp", ldv, writes=("vt",), dma=True)
    def ldm(e):
        return e.dma_start(out=msks[:].rearrange("p (m c) -> p m c", m=4),
                           in_=msk_d.rearrange("(m p) c -> p m c", p=128))
    p.op("sp", ldm, writes=("msks",), dma=True)
    p.op("sp", lambda e: e.dma_start(out=ident[:], in_=ident_d), writes=("ident",), dma=True)

    SC = 1.0 / 8.0
    its = [(h, qt) for h in range(NH) for qt in range(NQT)]

    def emitA(idx):
        h, qt = its[idx]
        i = idx % NDEP
        c, o = h // 2, (h % 2) * 64
        nkb = qt // 4 + 1
        for kb in range(nkb):
            p.op("pe", lambda e, c=c, o=o, qt=qt, kb=kb: e.matmul(
                    pS[kb % 2][:], qf[c][o:o+64, qt*128:(qt+1)*128],
                    kf[c][o:o+64, kb*512:(kb+1)*512], start=True, stop=True),
                 reads=(f"qf{c}", f"kf{c}"), writes=(f"pS{kb % 2}",))
            if kb == nkb - 1:
                v_ = qt % 4
                p.op("dve", lambda e, kb=kb, i=i, v_=v_: e.tensor_tensor(
                        sm[i][:, kb*512:(kb+1)*512], pS[kb % 2][:],
                        msks[:, v_*512:(v_+1)*512], ALU.add),
                     reads=(f"pS{kb % 2}", "msks"), writes=(f"sm{i}",))
            else:
                eng = "dve" if kb % 2 == 0 else "act"
                fn = ((lambda e, kb=kb, i=i: e.tensor_copy(
                        sm[i][:, kb*512:(kb+1)*512], pS[kb % 2][:])) if kb % 2 == 0 else
                      (lambda e, kb=kb, i=i: e.copy(
                        sm[i][:, kb*512:(kb+1)*512], pS[kb % 2][:])))
                p.op(eng, fn, reads=(f"pS{kb % 2}",), writes=(f"sm{i}",))

    def emitB(idx):
        h, qt = its[idx]
        i = idx % NDEP
        W = (qt // 4 + 1) * 512
        def mxfn(e, i=i, W=W):
            e.reduce_max(mxc[i][:], sm[i][:, 0:W], axis=AX.X)
            e.drain()
            e.tensor_scalar(mxn[i][:], mxc[i][:], -SC, None, ALU.mult)
        p.op("dve", mxfn, reads=(f"sm{i}",), writes=(f"mxc{i}", f"mxn{i}"))
        p.op("act", lambda e, i=i, W=W: e.activation(
                pq2[i][:, 0:W], sm[i][:, 0:W], AF.Exp, bias=mxn[i][:], scale=SC,
                accum_out=rsc[i][:]),
             reads=(f"sm{i}", f"mxn{i}"), writes=(f"pq2{i}", f"rsc{i}"))
        p.op("dve", lambda e, i=i: e.reciprocal(rsr[i][:], rsc[i][:]),
             reads=(f"rsc{i}",), writes=(f"rsr{i}",))
        p.op("act", lambda e, i=i, W=W: e.activation(
                pq2[i][:, 0:W], pq2[i][:, 0:W], AF.Identity, scale=rsr[i][:]),
             reads=(f"pq2{i}", f"rsr{i}"), writes=(f"pq2{i}",))

    def emitC(idx):
        h, qt = its[idx]
        i = idx % NDEP
        ip = idx % 2
        nb = qt + 1
        for jb in range(nb):
            p.op("pe", lambda e, i=i, ip=ip, jb=jb: e.transpose(
                    pT2[ip][:, (jb % 8)*128: (jb % 8 + 1)*128],
                    pq2[i][:, jb*128:(jb+1)*128], ident[:]),
                 reads=(f"pq2{i}", "ident"), writes=(f"pT2{ip}",))
            if jb % 8 == 7 or jb == nb - 1:
                lo = (jb // 8) * 8
                cnt = jb - lo + 1
                eng = "dve" if (jb // 8) % 2 == 0 else "act"
                fn = ((lambda e, i=i, ip=ip, lo=lo, cnt=cnt: e.tensor_copy(
                        PT[i][:, lo*128:(lo+cnt)*128], pT2[ip][:, 0:cnt*128])) if eng == "dve" else
                      (lambda e, i=i, ip=ip, lo=lo, cnt=cnt: e.copy(
                        PT[i][:, lo*128:(lo+cnt)*128], pT2[ip][:, 0:cnt*128])))
                p.op(eng, fn, reads=(f"pT2{ip}",), writes=(f"PT{i}",))
        def mmav(e, ip=ip, h=h, nb=nb):
            for jb in range(nb):
                e.matmul(pAV[ip][0:64, 0:128],
                         vt[:, jb*512 + h*64: jb*512 + (h+1)*64],
                         PT[i][:, jb*128:(jb+1)*128],
                         start=(jb == 0), stop=(jb == nb-1))
        p.op("pe", mmav, reads=(f"PT{i}", "vt"), writes=(f"pAV{ip}",))
        p.op("dve", lambda e, i=i, ip=ip: e.tensor_copy(outsb[i][:], pAV[ip][0:64, 0:128]),
             reads=(f"pAV{ip}",), writes=(f"outsb{i}",))
        p.op("sp", lambda e, i=i, h=h, qt=qt: e.dma_start(
                out=out_d[h*64:(h+1)*64, qt*128:(qt+1)*128], in_=outsb[i][:]),
             reads=(f"outsb{i}",), dma=True)

    NIT = len(its)
    for idx in range(NIT + 2):
        if idx < NIT:
            emitA(idx)
        if 1 <= idx <= NIT:
            emitB(idx - 1)
        if 2 <= idx <= NIT + 1:
            emitC(idx - 2)
    p.finish()
    return ctx


def l3_host_masks():
    masks = np.zeros((4, 128, 512), np.float32)
    for v in range(4):
        i = np.arange(128)[:, None] + v * 128
        j = np.arange(512)[None, :]
        masks[v] = np.where(i >= j, 0.0, -1e30)
    return masks.reshape(4 * 128, 512)


def build_l3_proj():
    ctx = Ctx()
    p = ctx.p
    at_d = ctx.din("attnT_own", [D, TC], BF16)
    wp_d = ctx.din("wpT", [D, D], F32)
    out_d = ctx.dout("out", [TC, D], F32)
    wp = [ctx.sbuf(f"wp{k}", [128, D], BF16) for k in range(ND)]
    att = ctx.sbuf("att", [128, ND * 128], BF16)
    outt = ctx.sbuf("outt", [128, D], F32)
    po = [ctx.psum(f"po{j}", [128, 512], F32) for j in range(2)]
    for k in range(ND):
        p.op("pool", lambda e, k=k: e.dma_start(out=wp[k][:], in_=wp_d[k*128:(k+1)*128, :]),
             writes=(f"wp{k}",), dma=True)
    for tcn in range(NT):
        def lda(e, tcn=tcn):
            src = bass.AP(at_d.tensor, tcn * 128, [[TC, 128], [128 * TC, ND], [1, 128]])
            return e.dma_start(out=att[:].rearrange("p (f x) -> p f x", f=ND), in_=src)
        p.op("sp", lda, writes=("att",), dma=True)
        for j in range(2):
            def mmo(e, j=j, tcn=tcn):
                for k in range(ND):
                    e.matmul(po[j][:], att[:, k*128:(k+1)*128], wp[k][:, j*512:(j+1)*512],
                             start=(k == 0), stop=(k == ND-1))
            p.op("pe", mmo, reads=("att",) + tuple(f"wp{k}" for k in range(ND)), writes=(f"po{j}",))
        def evo(e):
            e.tensor_copy(outt[:, 0:512], po[0][:])
            e.tensor_copy(outt[:, 512:1024], po[1][:])
        p.op("dve", evo, reads=("po0", "po1"), writes=("outt",))
        p.op("sp", lambda e, tcn=tcn: e.dma_start(out=out_d[tcn*128:(tcn+1)*128, :], in_=outt[:]),
             reads=("outt",), dma=True)
    p.finish()
    return ctx


def build_final_add():
    ctx = Ctx()
    p = ctx.p
    x_d = ctx.din("x", [TC, D], F32)
    d1_d = ctx.din("d1", [TC, D], F32)
    d2_d = ctx.din("d2", [TC, D], F32)
    o_d = ctx.dout("x_new", [TC, D], F32)
    xs = ctx.sbuf("xs", [128, D], F32)
    d1s = ctx.sbuf("d1s", [128, D], F32)
    d2s = ctx.sbuf("d2s", [128, D], F32)
    ts = ctx.sbuf("ts", [128, D], F32)
    for t in range(NT):
        p.op("sp", lambda e, t=t: e.dma_start(out=xs[:], in_=x_d[t*128:(t+1)*128, :]),
             writes=("xs",), dma=True)
        p.op("sp", lambda e, t=t: e.dma_start(out=d1s[:], in_=d1_d[t*128:(t+1)*128, :]),
             writes=("d1s",), dma=True)
        p.op("sp", lambda e, t=t: e.dma_start(out=d2s[:], in_=d2_d[t*128:(t+1)*128, :]),
             writes=("d2s",), dma=True)
        def add_fn(e):
            e.tensor_tensor(ts[:], xs[:], d1s[:], ALU.add)
            e.drain()
            e.tensor_tensor(ts[:], ts[:], d2s[:], ALU.add)
        p.op("dve", add_fn, reads=("xs", "d1s", "d2s"), writes=("ts",))
        p.op("sp", lambda e, t=t: e.dma_start(out=o_d[t*128:(t+1)*128, :], in_=ts[:]),
             reads=("ts",), dma=True)
    p.finish()
    return ctx



def _shard_tok(xf):
    return [np.ascontiguousarray(xf[b, 2048*s:2048*(s+1), :]) for b in range(4) for s in range(2)]

def _C(a):
    return np.ascontiguousarray(a)

def _featlay(vec, nch):
    return np.ascontiguousarray(np.asarray(vec, np.float32).reshape(nch, 128).T)

def _l0f_w1pack(W1):
    out = np.zeros((44, D, 128), np.float32)
    for j in range(22):
        w = min(128, 2730 - j*128)
        out[j, :, :w] = W1[:, j*128:j*128+w]
        out[22+j, :, :w] = W1[:, 2730+j*128:2730+j*128+w]
    return out

def _l0f_b1pack(b1):
    out = np.zeros((128, 44), np.float32)
    for j in range(22):
        w = min(128, 2730 - j*128)
        out[:w, j] = b1[j*128:j*128+w]
        out[:w, 22+j] = b1[2730+j*128:2730+j*128+w]
    return out

def _sel(E):
    s = np.zeros((E, E * 128), BF)
    for e in range(E):
        s[e, e*128:(e+1)*128] = 1
    return s

DUMP_HOOK = None

def _dump(stage, arrs):
    if DUMP_HOOK is not None:
        DUMP_HOOK(stage, arrs)

def kernel(**inp):
    CI = const_inputs()
    identf = np.eye(128, dtype=np.float32)
    onescol = np.ones((128, 1), np.float32)
    x = np.asarray(inp["L0_attn_x"], np.float32)

    def run(ctx, ims):
        res = run_bass_kernel_spmd(ctx.finalize(), ims, core_ids=list(range(NCORES)))
        return res.results

    def gather_tok(res, key):
        return [np.concatenate([np.asarray(res[2*b][key]), np.asarray(res[2*b+1][key])], 0)
                for b in range(4)]

    def gather_feat(res, key):
        return [np.concatenate([np.asarray(res[2*b][key]), np.asarray(res[2*b+1][key])], 1)
                for b in range(4)]

    shards = _shard_tok(x)
    res = run(build_ln_update(False, True),
              [dict(x=sh, **CI) for sh in shards])
    res0 = res
    xcur = [np.asarray(r["x_new"]) for r in res]
    xnT = [np.asarray(r["xnT"]) for r in res]
    xnTlo = [np.asarray(r["xnT_lo"]) for r in res]
    xnf = [np.asarray(r["xn_f32"]) for r in res]

    res = run(build_l0a_sig(),
              [dict(xnT=xnT[c], wret=_C(inp["L0_attn_W_ret"]),
                    bret=_featlay(inp["L0_attn_b_ret"], 8), **CI) for c in range(8)])
    sig = [np.asarray(r["sig"]) for r in res]
    ema_in = []
    for b in range(4):
        full = np.concatenate([sig[2*b], sig[2*b+1]], axis=1)
        for hh in range(2):
            ema_in.append(_C(full[hh*512:(hh+1)*512]))
    res = run(build_l0a_ema(), [dict(sig_bd=s_) for s_ in ema_in])
    rets = [np.asarray(r["ret_bd"]) for r in res]
    ret_in = []
    for b in range(4):
        full = np.concatenate([rets[2*b], rets[2*b+1]], axis=0)
        for s_ in range(2):
            ret_in.append(_C(full[:, s_*2048:(s_+1)*2048]))
    lnp4 = np.concatenate([_featlay(inp[f"L0_attn_{nm}"], 8)
                           for nm in ["ln1_scale", "ln1_bias", "ln2_scale", "ln2_bias"]], axis=1)
    res = run(build_l0a_rest(),
              [dict(xnT=xnT[c], retT=ret_in[c], w1=_C(inp["L0_attn_W1"]),
                    b1=_featlay(inp["L0_attn_b1"], 32), w2=_C(inp["L0_attn_W2"]),
                    b2=_featlay(inp["L0_attn_b2"], 8), lnp4=lnp4,
                    c_ident_f32=identf, c_ones_col=onescol, **CI) for c in range(8)])
    a0 = [np.asarray(r["out"]) for r in res]
    _dump("a0", a0); _dump("xn0", xnT); _dump("sig", sig); _dump("ret", rets)

    y = [np.asarray(v) for v in xnf]
    ybf = [np.asarray(r["xn_bf"]) for r in res0]
    Wqkv = np.asarray(inp["L0_ffn_Wqkv"], np.float32)
    Wol = np.asarray(inp["L0_ffn_Wo"], np.float32)
    qsl = np.asarray(inp["L0_ffn_q_scale"], np.float32)
    ksl = np.asarray(inp["L0_ffn_k_scale"], np.float32)
    W1l = np.asarray(inp["L0_ffn_W1"], np.float32)
    b1l = np.asarray(inp["L0_ffn_b1"], np.float32)
    W2l = np.asarray(inp["L0_ffn_W2"], np.float32)
    b2l = np.asarray(inp["L0_ffn_b2"], np.float32)
    lngl = np.asarray(inp["L0_ffn_ln_g"], np.float32)
    lnbl = np.asarray(inp["L0_ffn_ln_b"], np.float32)
    hcs2 = [l0f_host_consts2(c % 2) for c in range(2)]
    zhalo = np.zeros((HB, D), BF)
    for i in range(6):
        ims = []
        for c in range(8):
            s_ = c % 2
            halo_bf = (zhalo if s_ == 0 else np.asarray(ybf[c - 1][TC - HB:, :]))
            ims.append(dict(y=y[c], ybfh=np.concatenate([halo_bf, ybf[c]], 0),
                            wqkv=_C(Wqkv[i]), wo=_C(Wol[i]),
                            qsv=np.tile(qsl[i], 2).reshape(128, 1).astype(np.float32),
                            ksv=np.tile(ksl[i], 2).reshape(128, 1).astype(np.float32),
                            **hcs2[s_], **CI))
        res = run(build_l0f_attn2(), ims)
        y = [np.asarray(r["y_new"]) for r in res]
        ybf = [np.asarray(r["y_new_bf"]) for r in res]
        ims = []
        for c in range(8):
            s_ = c % 2
            halo = (np.zeros((128, D), BF) if s_ == 0
                    else np.asarray(ybf[c - 1][TC - 128:, :]))
            ims.append(dict(y=y[c], ybfh1=np.concatenate([halo, ybf[c]], 0),
                            w1p=_l0f_w1pack(W1l[i]),
                            b1p=_l0f_b1pack(b1l[i]), w2=_C(W2l[i]),
                            b2row=b2l[i].reshape(1, -1).astype(np.float32),
                            lng=_featlay(lngl[i], 8), lnb=_featlay(lnbl[i], 8),
                            c_ones_col=onescol, **CI))
        res = run(build_l0f_ffn(), ims)
        y = [np.asarray(r["y_new"]) for r in res]
        ybf = [np.asarray(r["y_new_bf"]) for r in res]
        _dump(f"y{i}", y)

    res = run(build_ln_update(True, False),
              [dict(x=xcur[c], d1=a0[c], d2=y[c], **CI) for c in range(8)])
    xcur = [np.asarray(r["x_new"]) for r in res]
    xnT = [np.asarray(r["xnT"]) for r in res]
    xnTlo = [np.asarray(r["xnT_lo"]) for r in res]

    res = run(build_l1_qkv(),
              [dict(xnT=xnT[c], wqT=_C(inp["L1_attn_Wq"].T), wkT=_C(inp["L1_attn_Wk"].T),
                    wvT=_C(inp["L1_attn_Wv"].T)) for c in range(8)])
    qf_b = gather_feat(res, "q_feat")
    k_all = np.concatenate(gather_tok(res, "k_tok"), 0)
    v_all = np.concatenate(gather_tok(res, "v_tok"), 0)
    res = run(build_l1_ffn(),
              [dict(xnT=xnT[c], wgT=_C(np.asarray(inp["L1_ffn_w_gate"]).T),
                    wuT=_C(np.asarray(inp["L1_ffn_w_up"]).T),
                    wdT=_C(np.asarray(inp["L1_ffn_w_down"]).T)) for c in range(8)])
    f1 = [np.asarray(r["out"]) for r in res]
    _dump("f1", f1)
    hc1 = l1_host_consts()
    res = run(build_l1_attn(),
              [dict(q_feat=_C(qf_b[c // 2]), k_all=k_all, v_all=v_all,
                    m0=_C(np.asarray(inp["L1_attn_M0"]).T),
                    woT=_C(np.asarray(inp["L1_attn_Wo"]).T),
                    gate=np.asarray(inp["L1_attn_gate"], np.float32).reshape(1, 1),
                    **hc1, **CI) for c in range(8)])
    a1 = [np.asarray(res[c]["out"])[(c % 2)*TC:((c % 2)+1)*TC] for c in range(8)]
    _dump("a1", a1)

    res = run(build_ln_update(True, False),
              [dict(x=xcur[c], d1=a1[c], d2=f1[c], **CI) for c in range(8)])
    xcur = [np.asarray(r["x_new"]) for r in res]
    xnT = [np.asarray(r["xnT"]) for r in res]
    xnTlo = [np.asarray(r["xnT_lo"]) for r in res]

    hc2 = l2_host_consts()
    hcf = l0f_host_consts(0)
    res = run(build_l2_proj(),
              [dict(xnT=xnT[c], wqT=_C(inp["L2_attn_Wq"].T), wkT=_C(inp["L2_attn_Wk"].T),
                    wvT=_C(inp["L2_attn_Wv"].T),
                    wgam=np.asarray(inp["L2_attn_W_gamma"], np.float32).reshape(16, 1),
                    bgam=np.asarray(inp["L2_attn_b_gamma"], np.float32).reshape(16, 1),
                    wlam=np.asarray(inp["L2_attn_W_lambda"], np.float32).reshape(16, 1),
                    blam=np.asarray(inp["L2_attn_b_lambda"], np.float32).reshape(16, 1),
                    c_g16=hc2["c_g16"], l0f_g2=hcf["l0f_g2"], l0f_g2t=hcf["l0f_g2t"],
                    **CI) for c in range(8)])
    qf_b = gather_feat(res, "q_feat")
    vf_b = gather_feat(res, "v_feat")
    kh_b = gather_tok(res, "kh_tok")
    vt_b = gather_tok(res, "v_tok")
    lr_b = gather_feat(res, "logrho")
    res = run(build_l2_ffn(),
              [dict(xnT=xnT[c], xnT_lo=xnTlo[c], rwT=_C(inp["L2_ffn_router_w"].T),
                    rb=np.asarray(inp["L2_ffn_router_b"], np.float32).reshape(1, 8),
                    ewT=_C(np.transpose(np.asarray(inp["L2_ffn_expert_w"]), (0, 2, 1))),
                    eb=_C(np.asarray(inp["L2_ffn_expert_b"], np.float32)),
                    c_sel8=_sel(8), **CI) for c in range(8)])
    f2 = [np.asarray(r["out"]) for r in res]
    _dump("f2", f2)
    woT2 = _C(inp["L2_attn_Wo"].T)
    res = run(build_l2_scan2(),
              [dict(q_full=_C(qf_b[c // 2][(c % 2)*512:((c % 2)+1)*512]),
                    kh_full=_C(kh_b[c // 2][:, (c % 2)*512:((c % 2)+1)*512]),
                    vt_full=_C(vt_b[c // 2][:, (c % 2)*512:((c % 2)+1)*512]),
                    vf_full=_C(vf_b[c // 2][(c % 2)*512:((c % 2)+1)*512]),
                    lr_full=_C(lr_b[c // 2][(c % 2)*8:((c % 2)+1)*8]),
                    woT=_C(woT2[(c % 2)*512:((c % 2)+1)*512]),
                    l2_maskneg=hc2["l2_maskneg"], c_e16=hc2["c_e16"],
                    c_ident_f32=identf, **CI) for c in range(8)])
    a2p = [np.asarray(r["out"]) for r in res]
    res = run(build_ln_update(3, False),
              [dict(x=xcur[c],
                    d1=_C(a2p[2*(c // 2)][(c % 2)*TC:((c % 2)+1)*TC]),
                    d3=_C(a2p[2*(c // 2)+1][(c % 2)*TC:((c % 2)+1)*TC]),
                    d2=f2[c], **CI) for c in range(8)])
    xcur = [np.asarray(r["x_new"]) for r in res]
    xnT = [np.asarray(r["xnT"]) for r in res]
    xnTlo = [np.asarray(r["xnT_lo"]) for r in res]

    wqkv3 = np.asarray(inp["L3_attn_w_qkv"], np.float32)
    res = run(build_l3_qkv(),
              [dict(xnT=xnT[c], wqT3=_C(wqkv3[0:1024].T), wkT3=_C(wqkv3[1024:2048].T),
                    wvT3=_C(wqkv3[2048:3072].T), **CI) for c in range(8)])
    qf_b = gather_feat(res, "q_feat")
    kf_b = gather_feat(res, "k_feat")
    vt_b = gather_tok(res, "v_tok")
    g48 = np.zeros((4, 32), BF)
    for g in range(4):
        g48[g, g*8:(g+1)*8] = 1
    res = run(build_l3_ffn(),
              [dict(xnT=xnT[c], xnT_lo=xnTlo[c],
                    rwT=_C(np.asarray(inp["L3_ffn_router_weight"]).T),
                    rb=np.asarray(inp["L3_ffn_router_bias"], np.float32).reshape(1, 32),
                    gwT=_C(np.transpose(np.asarray(inp["L3_ffn_gate_w"]), (0, 2, 1))),
                    uwT=_C(np.transpose(np.asarray(inp["L3_ffn_up_w"]), (0, 2, 1))),
                    dwT=_C(np.transpose(np.asarray(inp["L3_ffn_down_w"]), (0, 2, 1))),
                    sgwT=_C(np.asarray(inp["L3_ffn_shared_gate_w"]).T),
                    suwT=_C(np.asarray(inp["L3_ffn_shared_up_w"]).T),
                    sdwT=_C(np.asarray(inp["L3_ffn_shared_down_w"]).T),
                    c_g48=g48, c_sel32=_sel(32), **CI) for c in range(8)])
    f3 = [np.asarray(r["out"]) for r in res]
    _dump("f3", f3)
    msks3 = l3_host_masks()
    res = run(build_l3_attn2(),
              [dict(q_half=_C(qf_b[c // 2][(c % 2)*512:((c % 2)+1)*512]),
                    k_half=_C(kf_b[c // 2][(c % 2)*512:((c % 2)+1)*512]),
                    v_half=_C(vt_b[c // 2][:, (c % 2)*512:((c % 2)+1)*512]),
                    l3_masks=msks3, c_ident=CI["c_ident"]) for c in range(8)])
    at_b = [np.concatenate([np.asarray(res[2*b]["attnT"]), np.asarray(res[2*b+1]["attnT"])], 0)
            for b in range(4)]
    res = run(build_l3_proj(),
              [dict(attnT_own=_C(at_b[c // 2][:, (c % 2)*TC:((c % 2)+1)*TC]),
                    wpT=_C(np.asarray(inp["L3_attn_w_proj"]).T)) for c in range(8)])
    a3 = [np.asarray(r["out"]) for r in res]
    _dump("a3", a3)

    res = run(build_final_add(),
              [dict(x=xcur[c], d1=a3[c], d2=f3[c]) for c in range(8)])
    xout = [np.asarray(r["x_new"]) for r in res]

    out = np.zeros((B, S, D), np.float32)
    for b in range(4):
        for s_ in range(2):
            out[b, s_*2048:(s_+1)*2048] = xout[2*b + s_]
    return out
